# version 02
# baseline (speedup 1.0000x reference)
import numpy as np
import contextlib
import concourse.bass as bass
import concourse.mybir as mybir
from concourse.bass_utils import run_bass_kernel_spmd

F32 = mybir.dt.float32
BF16 = mybir.dt.bfloat16
I32 = mybir.dt.int32
FP8 = mybir.dt.float8e4
Alu = mybir.AluOpType
Act = mybir.ActivationFunctionType
AxX = mybir.AxisListType.X


class CFG:
    def __init__(self, mini=False, debug=False):
        self.NC = 8
        self.debug = debug
        if mini:
            self.H = 1024
            self.IFF = 1024
            self.TC = 128
            self.HD = 512
        else:
            self.H = 2048
            self.IFF = 8192
            self.TC = 1024
            self.HD = 2048
        self.NH = 2
        self.IP = self.NH * self.HD
        self.NS = 16
        self.RK = 128
        self.HALO = 0
        self.TW = self.TC
        self.HT = self.H // 128
        self.FT = self.IFF // 128
        self.DT = self.IP // 128
        self.DTH = self.HD // 128
        self.NQ = 4
        self.QT = self.FT // self.NQ
        self.KGRP = min(8, self.DTH)


def chunks_of(total, step=512):
    out = []
    cpos = 0
    while cpos < total:
        out.append((cpos, min(step, total - cpos)))
        cpos += step
    return out


class Sem:
    def __init__(self, h, name):
        self.h = h
        self.n = 0
        self.name = name


class KB:
    ENGINES = ("sync", "tensor", "vector", "scalar", "gpsimd")

    def __init__(self, nc):
        self.nc = nc
        self.streams = {e: [] for e in self.ENGINES}

    def emit(self, eng, fn, inc=None, inck=1):
        if inc is not None:
            inc.n += inck

        def th(api, fn=fn, inc=inc, inck=inck):
            r = fn(api)
            if inc is not None:
                r.then_inc(inc.h, inck)
        self.streams[eng].append(th)

    def wait(self, eng, sem, val=None):
        v = sem.n if val is None else val
        if v is None or v <= 0:
            return
        assert v <= sem.n, f"wait {sem.name}>={v} but only {sem.n} planned"
        self.streams[eng].append(lambda api, s=sem, v=v: api.wait_ge(s.h, v))

    def inc(self, eng, sem, k=1):
        sem.n += k
        self.streams[eng].append(lambda api, s=sem, k=k: api.sem_inc(s.h, k))


def build_program(cfg):
    c = cfg
    nc = bass.Bass("TRN2", target_bir_lowering=False, debug=False,
                   detect_race_conditions=False)
    P = 128
    TW, TC, HALO = c.TW, c.TC, c.HALO
    HT, FT, DT, DTH = c.HT, c.FT, c.DT, c.DTH
    NS, RK, NH = c.NS, c.RK, c.NH
    KGRP = c.KGRP
    NGR = DT // KGRP

    d_x = nc.dram_tensor("x_in", [c.H, TW], F32, kind="ExternalInput").ap()
    d_out = nc.dram_tensor("out_fm", [c.H, TC], F32, kind="ExternalOutput").ap()
    d_wg = nc.dram_tensor("wg_sl", [c.IFF // c.NC, c.H], F32, kind="ExternalInput").ap()
    d_wu = nc.dram_tensor("wu_sl", [c.IFF // c.NC, c.H], F32, kind="ExternalInput").ap()
    d_wd = nc.dram_tensor("wd_sl", [c.H // c.NC, c.IFF], F32, kind="ExternalInput").ap()
    d_wi = nc.dram_tensor("wi_sl", [c.H // c.NC, 2 * c.IP], F32, kind="ExternalInput").ap()
    d_wo = nc.dram_tensor("wo_sl", [c.IP // c.NC, c.H], F32, kind="ExternalInput").ap()
    d_wx = nc.dram_tensor("wx", [NH, c.HD, 2 * NS + RK], F32, kind="ExternalInput").ap()
    d_wdt = nc.dram_tensor("wdt", [NH, RK, c.HD], F32, kind="ExternalInput").ap()
    d_cw = nc.dram_tensor("convw", [DT, 128, 4], F32, kind="ExternalInput").ap()
    d_cb = nc.dram_tensor("convb", [128, DT], F32, kind="ExternalInput").ap()
    d_dtb = nc.dram_tensor("dtb", [128, DT], F32, kind="ExternalInput").ap()
    d_Dp = nc.dram_tensor("Dp", [128, DT], F32, kind="ExternalInput").ap()
    d_alog = nc.dram_tensor("alog", [DT, 128, NS], F32, kind="ExternalInput").ap()
    d_iden = nc.dram_tensor("iden", [128, 128], BF16, kind="ExternalInput").ap()
    d_idenf = nc.dram_tensor("idenf", [128, 128], F32, kind="ExternalInput").ap()
    d_pmask = nc.dram_tensor("pmask", [128, c.NC], F32, kind="ExternalInput").ap()
    d_cmask = nc.dram_tensor("cmask", [128, c.NC], F32, kind="ExternalInput").ap()

    TB = 128 * 128
    RPC = c.IFF // c.NC // 128
    HPC = c.H // c.NC // 128
    OPC = c.IP // c.NC // 128
    n_gate_blk = RPC * HT
    n_down_blk = HPC * FT
    n_win_blk = HPC * (2 * c.IP // 128)
    n_wout_blk = OPC * HT
    base_gate = 0
    base_up = base_gate + n_gate_blk * TB
    base_down = base_up + n_gate_blk * TB
    base_win = base_down + n_down_blk * TB
    base_wout = base_win + n_win_blk * TB
    PACK = base_wout + n_wout_blk * TB + 8 * TB
    DBGK = dict(kind="ExternalOutput") if c.debug else {}
    d_pack_h = nc.dram_tensor("wpack", [PACK], BF16)
    d_wag_h = nc.dram_tensor("wag", [c.NC * PACK], BF16)
    d_pack = d_pack_h.ap()
    wag = d_wag_h.ap()

    d_scr_xc_h = nc.dram_tensor("scr_xc", [DT, 128, TC], F32, **DBGK)
    d_cc3in_h = nc.dram_tensor("cc3in", [128, DT * 3], F32)
    d_cc3out_h = nc.dram_tensor("cc3out", [c.NC, 128, DT * 3], F32)
    d_scr_u_h = nc.dram_tensor("scr_u", [DT, 128, TC], F32, **DBGK)
    d_scr_dt_h = nc.dram_tensor("scr_dt", [DT, 128, TC], F32, **DBGK)
    d_scr_sg_h = nc.dram_tensor("scr_sg", [DT, 128, TC], F32, **DBGK)
    d_scr_bc_h = nc.dram_tensor("scr_bc", [NH * 2 * NS, TC], BF16, **DBGK)
    d_scr_ln_h = nc.dram_tensor("scr_ln", [2, TW], F32)
    d_x1_h = nc.dram_tensor("scr_x1", [c.H, TW], F32, **DBGK)
    d_macc_h = nc.dram_tensor("scr_macc", [NGR, c.H, TC], F32)
    d_cc2in_h = nc.dram_tensor("cc2in", [2, 128, DT * NS], F32)
    d_cc2out_h = nc.dram_tensor("cc2out", [c.NC, 2, 128, DT * NS], F32)
    d_scr_xc = d_scr_xc_h.ap(); d_cc3in = d_cc3in_h.ap(); d_cc3out = d_cc3out_h.ap()
    d_scr_u = d_scr_u_h.ap(); d_scr_dt = d_scr_dt_h.ap(); d_scr_sg = d_scr_sg_h.ap()
    d_scr_bc = d_scr_bc_h.ap(); d_scr_ln = d_scr_ln_h.ap()
    d_x1 = d_x1_h.ap(); d_macc = d_macc_h.ap()
    d_cc2in = d_cc2in_h.ap(); d_cc2out = d_cc2out_h.ap()
    d_dbg3 = d_dbgcc = None
    if c.debug:
        d_dbg3 = nc.dram_tensor("dbg_xc", [DT, 128, TC], F32, kind="ExternalOutput").ap()
        d_dbgcc = nc.dram_tensor("dbg_cc", [3, 128, DT * NS], F32, kind="ExternalOutput").ap()
        d_dbgxq = nc.dram_tensor("dbg_xq", [128, HT * TW], BF16, kind="ExternalOutput").ap()
        d_dbgiq = nc.dram_tensor("dbg_iq", [128, c.QT * TW], BF16, kind="ExternalOutput").ap()
        d_dbgxn2 = nc.dram_tensor("dbg_xn2", [128, HT * TW], BF16, kind="ExternalOutput").ap()
        d_dbgwin = nc.dram_tensor("dbg_win", [8, 128, 128], BF16, kind="ExternalOutput").ap()

    st = contextlib.ExitStack()

    class Alloc:
        def __init__(self, off=0):
            self.off = off

        def take(self, n):
            o = self.off
            self.off += n
            return o

    al = Alloc()
    O_R0 = al.take(3456)
    O_R1 = al.take(max(HT * TW, NS * TC, 5 * 2048 + 2 * 1024 + 256))
    O_R2 = al.take(max(HT * TW // 2, 4 * TC + (KGRP * TC) // 2 + 64))
    O_R3 = al.take(max(c.QT * TW // 2, 8 * TC + 64))
    O_R4 = al.take(4096)
    O_R5 = al.take(2 * TW + TW // 2 + TW // 4 + 320)
    O_R6 = al.take(max(2 * TW, 2 * TC))
    O_R7 = al.take(NH * TC + TC + (NH * DTH * (RK + 2 * NS)) // 2 + 128)
    ATOT = al.off
    arena = st.enter_context(nc.sbuf_tensor("arena", [P, ATOT], F32))
    arbf = arena[:].bitcast(BF16)
    ar8 = arena[:].bitcast(FP8)
    ari = arena[:].bitcast(I32)

    def fview(off, n):
        return arena[:, off:off + n]

    def bview(off_f, n_bf):
        return arbf[:, 2 * off_f:2 * off_f + n_bf]

    def iview(off_f, n):
        return ari[:, off_f:off_f + n]

    a0 = Alloc(O_R0)
    posA = fview(a0.take(DT * NS), DT * NS)
    negA = fview(a0.take(DT * NS), DT * NS)
    cw_sb = fview(a0.take(DT * 4), DT * 4)
    cb_sb = fview(a0.take(DT), DT)
    dtb_sb = fview(a0.take(DT), DT)
    Dp_sb = fview(a0.take(DT), DT)
    pmask_sb = fview(a0.take(c.NC), c.NC)
    cmask_sb = fview(a0.take(c.NC), c.NC)
    xc3_sb = fview(a0.take(DT * 3), DT * 3)
    b3_sb = fview(a0.take(DT * 3), DT * 3)
    ext6 = fview(a0.take(8), 8)
    identb = bview(a0.take(64), 128)
    identf = fview(a0.take(128), 128)
    onesb = bview(a0.take(1), 1)
    nrow = fview(a0.take(2), 2)
    zcol = fview(a0.take(1), 1)
    Scol = fview(a0.take(DT * NS), DT * NS)
    Decb = fview(a0.take(DT * NS), DT * NS)
    sInb = fview(a0.take(DT * NS), DT * NS)
    dec16 = fview(a0.take(NS), NS)
    assert a0.off <= O_R0 + 3456, a0.off - O_R0

    xres = fview(O_R1, HT * TW)
    brep = bview(O_R1, NS * TC)
    crep = bview(O_R1 + (NS * TC) // 2, NS * TC)


    xqbf = bview(O_R2, HT * TW)
    dAb = fview(O_R2, 2 * TC)
    gBb = fview(O_R2 + 2 * TC, 2 * TC)
    Fb, Wb = dAb, gBb
    ygbuf = bview(O_R2 + 4 * TC, KGRP * TC)

    iqbf = bview(O_R3, c.QT * TW)
    SP_U = [O_R3 + 0 * TC, O_R3 + 1 * TC]
    SP_DT = [O_R3 + 2 * TC, O_R3 + 3 * TC]
    SP_SG = [O_R3 + 4 * TC, O_R3 + 5 * TC]
    stb = fview(O_R3 + 6 * TC, TC)
    cdtb = stb
    gb = fview(O_R3 + 7 * TC, TC)
    ubft = [bview(O_R3 + 6 * TC, TC), bview(O_R3 + 6 * TC + TC // 2, TC)]

    wstream = bview(O_R4, 8192)

    a5 = Alloc(O_R5)
    scrF = fview(a5.take(2 * TW), 2 * TW)
    scrH = bview(a5.take(TW // 2), TW)
    _o8 = a5.take(TW // 4)
    q8t = ar8[:, 4 * _o8:4 * _o8 + TW]
    amt = fview(a5.take(64), 64)
    _oi = a5.take(192)
    set_ = iview(_oi, 64)
    invt = iview(_oi + 64, 64)
    sct = iview(_oi + 128, 64)
    assert TW // 32 <= 64

    lnbc = fview(O_R6, 2 * TW)
    csb = fview(O_R6, 2 * TC)
    agb = csb

    a7 = Alloc(O_R7)
    ssm_dt = fview(O_R6, NH * TC)
    ssm_bc = arena[0:32, O_R5:O_R5 + NH * TC]
    yb = fview(O_R4 + 1024, TC)
    argb = yb
    dumpb = fview(O_R4 + 2048, TC)
    _od = a7.take(NH * TC // 2)
    dtin_bf = arbf[:, 2 * _od:2 * _od + NH * TC]
    _obc = a7.take(NH * TC // 2)
    bcrow_bf = arbf[0:32, 2 * _obc:2 * _obc + NH * TC]
    evst = fview(a7.take(TC), TC)
    wx_cols = NH * DTH * (RK + 2 * NS)
    _owx = a7.take((wx_cols + 1) // 2)
    wxbf = arbf[:, 2 * _owx:2 * _owx + wx_cols]

    def semh(name):
        return Sem(st.enter_context(nc.semaphore(name)), name)
    dL = semh("dL")
    dW = semh("dW")
    dU = semh("dU")
    dR = semh("dR")
    dO = semh("dO")
    sT = semh("sT")
    sV = semh("sV")
    sS = semh("sS")
    sG = semh("sG")
    sCC = semh("sCC")
    dWG = semh("dWG")

    kb = KB(nc)
    E = kb.emit
    W = kb.wait

    wsmark = {0: 0, 4096: 0}

    def wslab(buf, dst_off, nblk, src_elem_off, wait_mark=True):
        if wait_mark:
            W("sync", sT, wsmark[buf])
        dview = wag[src_elem_off:src_elem_off + nblk * TB].rearrange(
            "(b r q) -> r b q", r=128, q=128)
        sview = wstream[:, buf + dst_off:buf + dst_off + nblk * 128].rearrange(
            "p (b q) -> p b q", q=128)
        E("sync", lambda api, d=dview, s=sview: api.dma_start(out=s, in_=d), inc=dW, inck=16)

    def dma(eng, out, in_, sem):
        E(eng, lambda api, o=out, i=in_: api.dma_start(out=o, in_=i), inc=sem, inck=16)

    def r32(ap, nblk):
        return ap.rearrange("p (b k) -> p b k", k=32)

    def bc32(intap, nblk):
        return intap[:, 0:nblk].rearrange("p (b o) -> p b o", o=1).bitcast(F32).broadcast_to([P, nblk, 32])

    dma("sync", cw_sb.rearrange("p (t k) -> p t k", k=4), d_cw.rearrange("t p k -> p t k"), dL)
    dma("sync", cb_sb, d_cb, dL)
    dma("sync", dtb_sb, d_dtb, dL)
    dma("sync", Dp_sb, d_Dp, dL)
    dma("sync", posA.rearrange("p (t n) -> p t n", n=NS), d_alog.rearrange("t p n -> p t n"), dL)
    dma("sync", identb, d_iden, dL)
    dma("sync", identf, d_idenf, dL)
    dma("sync", pmask_sb, d_pmask, dL)
    dma("sync", cmask_sb, d_cmask, dL)
    base_load_n = dL.n
    W("scalar", dL, base_load_n)
    E("scalar", lambda api: api.activation(out=posA, in_=posA, func=Act.Exp), inc=sS)
    W("vector", sS)
    E("vector", lambda api: api.tensor_scalar(out=negA, in0=posA, scalar1=-1.0,
                                              scalar2=None, op0=Alu.mult), inc=sV)
    E("vector", lambda api: api.memset(zcol, 0.0), inc=sV)
    E("vector", lambda api: api.memset(onesb, 1.0), inc=sV)
    E("vector", lambda api: api.memset(xqbf[:, 0:1024], 0.0), inc=sV)
    E("vector", lambda api: api.memset(amt[:, 0:64], 0.0), inc=sV)
    W("sync", sV)
    dma("sync", d_pack[PACK - 8 * TB:PACK].rearrange("(p q) -> p q", p=128),
        xqbf[:, 0:1024], dO)

    WPW = 2048
    ap1 = Alloc(O_R1)
    wp_in = fview(ap1.take(WPW), WPW)
    wp_z = fview(ap1.take(WPW), WPW)
    wp_sgn = fview(ap1.take(WPW), WPW)
    wp_acc = fview(ap1.take(WPW), WPW)
    wp_tmp = fview(ap1.take(WPW), WPW)
    _oq = ap1.take(WPW // 2)
    wp_q = arbf[:, 2 * _oq:2 * _oq + WPW]
    _oe = ap1.take(WPW // 2)
    wp_ev = arbf[:, 2 * _oe:2 * _oe + WPW]
    wp_am = fview(ap1.take(WPW // 32), WPW // 32)
    _wo = ap1.take(3 * (WPW // 32))
    wp_se = iview(_wo, WPW // 32)
    wp_inv = iview(_wo + WPW // 32, WPW // 32)
    wp_sc = iview(_wo + 2 * (WPW // 32), WPW // 32)
    assert ap1.off <= O_R1 + max(HT * TW, NS * TC, 5 * 2048 + 2 * 1024 + 256)
    psTr_cm = nc.psum_tensor("psTr", [P, 2048], BF16)
    psTr = psTr_cm.__enter__()

    def quantize_weights(ncols):
        nblk = ncols // 32
        W("vector", sT)
        E("vector", lambda api, ncols=ncols, nblk=nblk: api.tensor_reduce(
            out=wp_am[:, 0:nblk], in_=r32(wp_in[:, 0:ncols], nblk),
            axis=AxX, op=Alu.max, apply_absolute_value=True))
        E("vector", lambda api, nblk=nblk: api.tensor_scalar(out=wp_se[:, 0:nblk], in0=wp_am[:, 0:nblk].bitcast(I32),
                                                             scalar1=23, scalar2=None, op0=Alu.logical_shift_right))
        E("vector", lambda api, nblk=nblk: api.tensor_scalar(out=wp_se[:, 0:nblk], in0=wp_se[:, 0:nblk],
                                                             scalar1=-2, scalar2=0, op0=Alu.add, op1=Alu.max))
        E("vector", lambda api, nblk=nblk: api.tensor_scalar(out=wp_se[:, 0:nblk], in0=wp_se[:, 0:nblk],
                                                             scalar1=254, scalar2=None, op0=Alu.min))
        E("vector", lambda api, nblk=nblk: api.tensor_scalar(out=wp_inv[:, 0:nblk], in0=wp_se[:, 0:nblk],
                                                             scalar1=-1, scalar2=254, op0=Alu.mult, op1=Alu.add))
        E("vector", lambda api, nblk=nblk: api.tensor_scalar(out=wp_inv[:, 0:nblk], in0=wp_inv[:, 0:nblk],
                                                             scalar1=23, scalar2=None, op0=Alu.logical_shift_left))
        E("vector", lambda api, nblk=nblk: api.tensor_scalar(out=wp_sc[:, 0:nblk], in0=wp_se[:, 0:nblk],
                                                             scalar1=23, scalar2=None, op0=Alu.logical_shift_left))
        E("vector", lambda api, ncols=ncols, nblk=nblk: api.tensor_tensor(
            out=r32(wp_tmp[:, 0:ncols], nblk), in0=r32(wp_in[:, 0:ncols], nblk),
            in1=bc32(wp_inv, nblk), op=Alu.mult), inc=sV)
        W("scalar", sV)
        E("scalar", lambda api, ncols=ncols: api.activation(out=wp_z[:, 0:ncols], in_=wp_tmp[:, 0:ncols],
                                                            func=Act.Abs), inc=sS)
        E("vector", lambda api, ncols=ncols: api.tensor_scalar(out=wp_sgn[:, 0:ncols], in0=wp_tmp[:, 0:ncols],
                                                               scalar1=0.0, scalar2=2.0, op0=Alu.is_ge, op1=Alu.mult))
        E("vector", lambda api, ncols=ncols: api.tensor_scalar(out=wp_sgn[:, 0:ncols], in0=wp_sgn[:, 0:ncols],
                                                               scalar1=-1.0, scalar2=None, op0=Alu.add))
        W("vector", sS)
        th = [(0.25, 0.5), (0.75, 0.5), (1.25, 0.5), (1.75, 0.5), (2.5, 1.0), (3.5, 1.0), (5.0, 2.0)]
        E("vector", lambda api, ncols=ncols: api.tensor_scalar(out=wp_acc[:, 0:ncols], in0=wp_z[:, 0:ncols],
                                                               scalar1=th[0][0], scalar2=th[0][1],
                                                               op0=Alu.is_ge, op1=Alu.mult))
        for cc_, ww_ in th[1:]:
            E("vector", lambda api, cc_=cc_, ww_=ww_, ncols=ncols: api.tensor_scalar(
                out=wp_tmp[:, 0:ncols], in0=wp_z[:, 0:ncols],
                scalar1=cc_, scalar2=ww_, op0=Alu.is_ge, op1=Alu.mult))
            E("vector", lambda api, ncols=ncols: api.tensor_tensor(out=wp_acc[:, 0:ncols], in0=wp_acc[:, 0:ncols],
                                                                   in1=wp_tmp[:, 0:ncols], op=Alu.add))
        E("vector", lambda api, ncols=ncols: api.tensor_tensor(out=wp_acc[:, 0:ncols], in0=wp_acc[:, 0:ncols],
                                                               in1=wp_sgn[:, 0:ncols], op=Alu.mult))
        E("vector", lambda api, ncols=ncols, nblk=nblk: api.tensor_tensor(
            out=r32(wp_q[:, 0:ncols], nblk), in0=r32(wp_acc[:, 0:ncols], nblk),
            in1=bc32(wp_sc, nblk), op=Alu.mult), inc=sV)

    W("tensor", dL, base_load_n)

    def prep_matrix(src, nrows, ncols, quant, blk_base, blk_order, transpose=True):
        nch = nrows // 128
        ncc = (ncols + WPW - 1) // WPW
        for ch in range(nch):
            for cc in range(ncc):
                c0 = cc * WPW
                cn = min(WPW, ncols - c0)
                nct = cn // 128
                W("sync", sV)
                W("sync", sS)
                dma("sync", wp_in[:, 0:cn], src[ch * 128:(ch + 1) * 128, c0:c0 + cn], dW)
                W("vector", dW)
                if quant:
                    quantize_weights(cn)
                else:
                    W("vector", sT)
                    kb.inc("vector", sV)
                    W("scalar", sV)
                    E("scalar", lambda api, cn=cn: api.copy(out=wp_q[:, 0:cn], in_=wp_in[:, 0:cn]), inc=sS)
                    W("vector", sS)
                    kb.inc("vector", sV)
                if transpose:
                    W("tensor", sV)
                    for j0 in range(0, nct, 16):
                        jn = min(16, nct - j0)
                        W("tensor", sS)
                        for j in range(j0, j0 + jn):
                            E("tensor", lambda api, j=j, j0=j0: api.transpose(
                                psTr[:, (j - j0) * 128:(j - j0 + 1) * 128],
                                wp_q[:, j * 128:(j + 1) * 128], identb), inc=sT)
                        W("scalar", sT)
                        E("scalar", lambda api, jn=jn, j0=j0: api.copy(
                            out=wp_ev[:, j0 * 128:(j0 + jn) * 128],
                            in_=psTr[:, 0:jn * 128]), inc=sS)
                    W("sync", sS)
                    evsrc = wp_ev
                else:
                    W("sync", sS)
                    W("sync", sV)
                    evsrc = wp_q
                idxs = [blk_order(ch, (c0 // 128) + j) for j in range(nct)]
                stride = (idxs[1] - idxs[0]) if nct > 1 else 1
                for j in range(nct - 1):
                    assert idxs[j + 1] - idxs[j] == stride
                base_off = blk_base + idxs[0] * TB
                if stride > 1:
                    dview = d_pack[base_off:base_off + nct * stride * TB]
                    dview = dview.rearrange("(b s r q) -> r b s q", s=stride, r=128, q=128)[:, :, 0, :]
                else:
                    dview = d_pack[base_off:base_off + nct * TB].rearrange("(b r q) -> r b q", r=128, q=128)
                sview = evsrc[:, 0:nct * 128].rearrange("p (b q) -> p b q", q=128)
                dma("sync", dview, sview, dO)

    prep_matrix(d_wg, c.IFF // c.NC, c.H, True, base_gate, lambda ch, j: ch * HT + j)
    prep_matrix(d_wu, c.IFF // c.NC, c.H, True, base_up, lambda ch, j: ch * HT + j)
    prep_matrix(d_wd, c.H // c.NC, c.IFF, True, base_down, lambda ch, j: ch * FT + j)
    prep_matrix(d_wi, c.H // c.NC, 2 * c.IP, False, base_win, lambda ch, j: j * HPC + ch, transpose=False)
    prep_matrix(d_wo, c.IP // c.NC, c.H, False, base_wout, lambda ch, j: j * OPC + ch, transpose=False)

    RW = RK + 2 * NS
    CHT = max(1, 2048 // RW)
    for h_ in range(NH):
        for t0_ in range(0, DTH, CHT):
            tn_ = min(CHT, DTH - t0_)
            W("sync", sS)
            W("sync", sV)
            dma("sync", wp_in[:, 0:tn_ * RW].rearrange("p (t r) -> p t r", r=RW),
                d_wx.rearrange("h (t p) r -> p h t r", p=P)[:, h_, t0_:t0_ + tn_], dL)
            W("scalar", dL)
            woff = (h_ * DTH + t0_) * RW
            E("scalar", lambda api, woff=woff, tn_=tn_: api.copy(
                out=wxbf[:, woff:woff + tn_ * RW], in_=wp_in[:, 0:tn_ * RW]), inc=sS)

    W("gpsimd", dO)
    E("gpsimd", lambda api: api.collective_compute(
        "AllGather", Alu.bypass, replica_groups=[list(range(c.NC))],
        ins=[d_pack_h.ap()], outs=[d_wag_h.ap()]), inc=sCC)

    prep_T, prep_V, prep_S = sT.n, sV.n, sS.n
    for eng in KB.ENGINES:
        W(eng, sT, prep_T)
        W(eng, sV, prep_V)
        W(eng, sS, prep_S)
    psTr_cm.__exit__(None, None, None)

    psA = st.enter_context(nc.psum_tensor("psA", [P, 1536], F32))
    psB = st.enter_context(nc.psum_tensor("psB", [P, 1536], F32))
    psC = st.enter_context(nc.psum_tensor("psC", [P, 1024], F32))
    psmark = {"A": (sV, 0), "B": (sV, 0), "C": (sV, 0)}

    def pswait_t(name):
        sem, val = psmark[name]
        W("tensor", sem, val)

    dma("sync", xres.rearrange("p (t w) -> p t w", w=TW),
        d_x.rearrange("(t p) w -> p t w", p=P), dL)
    xload_mark = dL.n
    W("vector", dL, xload_mark)
    W("scalar", dL, xload_mark)
    W("tensor", dL, xload_mark)

    def ln_and(width, col0, out_consumer):
        inv_n = 1.0 / c.H
        pswait_t("A")
        pswait_t("B")
        W("scalar", sV)
        cast_mark = {}
        xcast = [bview(O_R3, TW), bview(O_R3 + TW // 2, TW)]
        qcast = [bview(O_R3 + TW, TW), bview(O_R3 + 3 * (TW // 2), TW)]
        for t in range(HT):
            xa = xres[:, t * TW + col0: t * TW + col0 + width]
            pb = t % 2
            if t >= 2:
                W("scalar", sT, cast_mark[t - 2])
            E("scalar", lambda api, xa=xa, pb=pb, width=width: api.copy(
                out=xcast[pb][:, 0:width], in_=xa), inc=sS)
            E("scalar", lambda api, xa=xa, pb=pb, width=width: api.activation(
                out=qcast[pb][:, 0:width], in_=xa, func=Act.Square), inc=sS)
            W("tensor", sS)
            for (cc0, cn) in chunks_of(width):
                E("tensor", lambda api, cc0=cc0, cn=cn, t=t, pb=pb: api.matmul(
                    psA[0:1, cc0:cc0 + cn], onesb, xcast[pb][:, cc0:cc0 + cn],
                    start=(t == 0), stop=(t == HT - 1)), inc=sT)
                E("tensor", lambda api, cc0=cc0, cn=cn, t=t, pb=pb: api.matmul(
                    psB[0:1, cc0:cc0 + cn], onesb, qcast[pb][:, cc0:cc0 + cn],
                    start=(t == 0), stop=(t == HT - 1)), inc=sT)
            cast_mark[t] = sT.n
        W("vector", sT)
        m_ = scrF[0:1, 0:width]
        v_ = scrF[0:1, TW:TW + width]
        t_ = lnbc[0:1, 0:width]
        r_ = lnbc[0:1, TW:TW + width]
        E("vector", lambda api, m_=m_, width=width: api.tensor_scalar(out=m_, in0=psA[0:1, 0:width],
                                                                     scalar1=inv_n, scalar2=None, op0=Alu.mult))
        E("vector", lambda api, v_=v_, width=width: api.tensor_scalar(out=v_, in0=psB[0:1, 0:width],
                                                                      scalar1=inv_n, scalar2=None, op0=Alu.mult))
        E("vector", lambda api, t_=t_, m_=m_: api.tensor_tensor(out=t_, in0=m_, in1=m_, op=Alu.mult))
        E("vector", lambda api, v_=v_, t_=t_: api.tensor_tensor(out=v_, in0=v_, in1=t_, op=Alu.subtract))
        E("vector", lambda api, v_=v_: api.tensor_scalar(out=v_, in0=v_,
                                                         scalar1=1e-5, scalar2=None, op0=Alu.add), inc=sV)
        psmark["A"] = (sV, sV.n)
        psmark["B"] = (sV, sV.n)
        W("scalar", sV)
        E("scalar", lambda api, t_=t_, v_=v_: api.activation(out=t_, in_=v_, func=Act.Ln))
        E("scalar", lambda api, r_=r_, t_=t_: api.activation(out=r_, in_=t_, func=Act.Exp, scale=-0.5), inc=sS)
        W("vector", sS)
        E("vector", lambda api, t_=t_, m_=m_, r_=r_: api.tensor_tensor(out=t_, in0=m_, in1=r_, op=Alu.mult))
        E("vector", lambda api, m_=m_, t_=t_: api.tensor_scalar(out=m_, in0=t_,
                                                                scalar1=-1.0, scalar2=None, op0=Alu.mult), inc=sV)
        W("sync", sV)
        dma("sync", d_scr_ln[0:1, 0:width], r_, dU)
        dma("sync", d_scr_ln[1:2, 0:width], m_, dU)
        W("sync", dU)
        dma("sync", lnbc[:, 0:width], d_scr_ln[0:1, 0:width].broadcast_to([P, width]), dR)
        dma("sync", lnbc[:, TW:TW + width], d_scr_ln[1:2, 0:width].broadcast_to([P, width]), dR)
        W("vector", dR)
        for t in range(HT):
            xa = xres[:, t * TW + col0: t * TW + col0 + width]
            E("vector", lambda api, xa=xa, width=width: api.tensor_tensor(
                out=scrF[:, 0:width], in0=xa, in1=lnbc[:, 0:width], op=Alu.mult))
            E("vector", lambda api, width=width: api.tensor_tensor(
                out=scrF[:, TW:TW + width], in0=scrF[:, 0:width], in1=lnbc[:, TW:TW + width],
                op=Alu.add), inc=sV)
            out_consumer(t, scrF[:, TW:TW + width], width)

    def quant_act_tile(xn_ap, width, out_bf_ap):
        assert width == TC
        nblk = width // 32
        NCH = width // 128
        Y = scrF[:, 0:width]
        Y6 = scrF[:, TW:TW + width]
        pswait_t("A")
        W("tensor", sV)
        for j in range(NCH):
            E("tensor", lambda api, j=j, xn_ap=xn_ap: api.transpose(
                psA[:, j * 128:(j + 1) * 128],
                xn_ap[:, j * 128:(j + 1) * 128], identf), inc=sT)
        W("scalar", sT)
        E("scalar", lambda api, Y=Y, width=width: api.copy(out=Y, in_=psA[:, 0:width]), inc=sS)
        psmark["A"] = (sS, sS.n)
        W("vector", sS)
        E("vector", lambda api, Y=Y, nblk=nblk: api.tensor_reduce(out=amt[:, 0:nblk], in_=r32(Y, nblk),
                                                                  axis=AxX, op=Alu.max,
                                                                  apply_absolute_value=True), inc=sV)
        W("vector", sV)
        E("vector", lambda api: api.tensor_scalar(out=set_[:, 0:64], in0=amt[:, 0:64].bitcast(I32),
                                                  scalar1=23, scalar2=None, op0=Alu.logical_shift_right))
        E("vector", lambda api: api.tensor_scalar(out=set_[:, 0:64], in0=set_[:, 0:64],
                                                  scalar1=-2, scalar2=0, op0=Alu.add, op1=Alu.max))
        E("vector", lambda api: api.tensor_scalar(out=invt[:, 0:64], in0=set_[:, 0:64],
                                                  scalar1=-1, scalar2=248, op0=Alu.mult, op1=Alu.add))
        E("vector", lambda api: api.tensor_scalar(out=invt[:, 0:64], in0=invt[:, 0:64],
                                                  scalar1=23, scalar2=None, op0=Alu.logical_shift_left))
        E("vector", lambda api: api.tensor_scalar(out=sct[:, 0:64], in0=set_[:, 0:64],
                                                  scalar1=6, scalar2=None, op0=Alu.add))
        E("vector", lambda api: api.tensor_scalar(out=sct[:, 0:64], in0=sct[:, 0:64],
                                                  scalar1=23, scalar2=None, op0=Alu.logical_shift_left), inc=sV)
        W("vector", sV)
        E("vector", lambda api, Y=Y, Y6=Y6, nblk=nblk: api.tensor_tensor(
            out=r32(Y6, nblk), in0=r32(Y, nblk), in1=bc32(invt, nblk), op=Alu.mult))
        M = 7.5 * 2.0 ** -6
        E("vector", lambda api, Y6=Y6: api.tensor_scalar(out=Y6, in0=Y6, scalar1=M, scalar2=-M,
                                                         op0=Alu.min, op1=Alu.max), inc=sV)
        W("scalar", sV)
        E("scalar", lambda api, Y6=Y6, width=width: api.copy(out=q8t[:, 0:width], in_=Y6), inc=sS)
        W("vector", sS)
        E("vector", lambda api, Y=Y, width=width, nblk=nblk: api.tensor_tensor(
            out=r32(Y, nblk), in0=r32(q8t[:, 0:width], nblk),
            in1=bc32(sct, nblk), op=Alu.mult), inc=sV)
        pswait_t("A")
        W("tensor", sV)
        for j in range(NCH):
            E("tensor", lambda api, j=j, Y=Y: api.transpose(
                psA[:, j * 128:(j + 1) * 128],
                Y[:, j * 128:(j + 1) * 128], identf), inc=sT)
        W("scalar", sT)
        E("scalar", lambda api, width=width, out_bf_ap=out_bf_ap: api.copy(
            out=out_bf_ap, in_=psA[:, 0:width]), inc=sS)
        psmark["A"] = (sS, sS.n)
        W("vector", sS)
        kb.inc("vector", sV)

    def ffn(col0, width):
        nch = chunks_of(width)

        def consume(t, xn_ap, w_):
            quant_act_tile(xn_ap, w_, xqbf[:, t * TW:t * TW + w_])
        ln_and(width, col0, consume)
        xq_done = sV.n
        if c.debug and col0 == 0:
            W("sync", sV, xq_done)
            dma("sync", d_dbgxq, xqbf[:, 0:HT * TW], dU)
        for q in range(c.NQ):
            for oj in range(c.QT):
                og = q * c.QT + oj
                rank = og // RPC
                olocal = og % RPC
                gbase = rank * PACK + base_gate + (olocal * HT) * TB
                ubase = rank * PACK + base_up + (olocal * HT) * TB
                buf = (og % 2) * 4096
                wslab(buf, 0, HT, gbase)
                wslab(buf, HT * 128, HT, ubase, wait_mark=False)
                W("tensor", dW)
                W("tensor", sV, xq_done)
                pswait_t("A")
                pswait_t("B")
                for k in range(HT):
                    for (cc0, cn) in nch:
                        E("tensor", lambda api, k=k, cc0=cc0, cn=cn, buf=buf: api.matmul(
                            psA[:, cc0:cc0 + cn], wstream[:, buf + k * 128:buf + (k + 1) * 128],
                            xqbf[:, k * TW + cc0:k * TW + cc0 + cn],
                            start=(k == 0), stop=(k == HT - 1)), inc=sT)
                    for (cc0, cn) in nch:
                        E("tensor", lambda api, k=k, cc0=cc0, cn=cn, buf=buf: api.matmul(
                            psB[:, cc0:cc0 + cn], wstream[:, buf + (HT + k) * 128:buf + (HT + k + 1) * 128],
                            xqbf[:, k * TW + cc0:k * TW + cc0 + cn],
                            start=(k == 0), stop=(k == HT - 1)), inc=sT)
                wsmark[buf] = sT.n
                W("scalar", sT)
                E("scalar", lambda api, width=width: api.activation(
                    out=scrF[:, 0:width], in_=psA[:, 0:width], func=Act.Silu), inc=sS)
                W("vector", sS)
                E("vector", lambda api, width=width: api.tensor_tensor(
                    out=scrF[:, TW:TW + width], in0=scrF[:, 0:width], in1=psB[:, 0:width],
                    op=Alu.mult), inc=sV)
                psmark["A"] = (sV, sV.n)
                psmark["B"] = (sV, sV.n)
                quant_act_tile(scrF[:, TW:TW + width], width, iqbf[:, oj * TW:oj * TW + width])
            iq_done = sV.n
            if c.debug and col0 == 0 and q == 0:
                W("sync", sV, iq_done)
                dma("sync", d_dbgiq, iqbf[:, 0:c.QT * TW], dU)
            for m in range(HT):
                rank = m // HPC
                mlocal = m % HPC
                dbase = rank * PACK + base_down + (mlocal * FT + q * c.QT) * TB
                buf = (m % 2) * 4096
                wslab(buf, 0, c.QT, dbase)
                W("tensor", dW)
                W("tensor", sV, iq_done)
                ppname = "A" if (m % 2 == 0) else "B"
                pp = psA if (m % 2 == 0) else psB
                pswait_t(ppname)
                for k in range(c.QT):
                    for (cc0, cn) in nch:
                        E("tensor", lambda api, k=k, cc0=cc0, cn=cn, buf=buf, pp=pp: api.matmul(
                            pp[:, cc0:cc0 + cn], wstream[:, buf + k * 128:buf + (k + 1) * 128],
                            iqbf[:, k * TW + cc0:k * TW + cc0 + cn],
                            start=(k == 0), stop=(k == c.QT - 1)), inc=sT)
                wsmark[buf] = sT.n
                W("vector", sT)
                xa = xres[:, m * TW + col0: m * TW + col0 + width]
                E("vector", lambda api, pp=pp, xa=xa, width=width: api.scalar_tensor_tensor(
                    out=xa, in0=pp[:, 0:width], scalar=0.5, in1=xa, op0=Alu.mult, op1=Alu.add), inc=sV)
                psmark[ppname] = (sV, sV.n)

    W("tensor", sCC)
    W("sync", sCC)
    ffn(0, TW)

    def consume_ln2(t, xn_ap, w_):
        W("scalar", sV)
        E("scalar", lambda api, t=t, xn_ap=xn_ap, w_=w_: api.copy(
            out=xqbf[:, t * TW:t * TW + w_], in_=xn_ap), inc=sS)
        W("vector", sS)
    ln_and(TW, 0, consume_ln2)
    xn2_done = sS.n
    if c.debug:
        W("sync", sS)
        dma("sync", d_dbgxn2, xqbf[:, 0:HT * TW], dU)
        for rr in range(8):
            off = rr * PACK + base_win
            dma("sync", d_dbgwin[rr],
                wag[off:off + TB].rearrange("(r q) -> r q", r=128), dU)
    W("sync", sV)
    W("sync", sS)
    dma("sync", d_x1.rearrange("(t p) w -> p t w", p=P),
        xres.rearrange("p (t w) -> p t w", w=TW), dU)
    x1_spilled = dU.n

    E("vector", lambda api: api.memset(ssm_dt, 0.0))
    E("vector", lambda api: api.memset(ssm_bc, 0.0), inc=sV)

    nchW = chunks_of(TW)
    nchC = chunks_of(TC)
    ip_xp_mark = {}
    ip_spill_mark = {}
    for ot in range(2 * DT):
        buf = (ot % 2) * 4096
        for r in range(c.NC):
            wb_ = r * PACK + base_win + (ot * HPC) * TB
            wslab(buf, r * HPC * 128, HPC, wb_, wait_mark=(r == 0))
        W("tensor", dW)
        W("tensor", sS, xn2_done)
        ppname = "A" if (ot % 2 == 0) else "B"
        pp = psA if (ot % 2 == 0) else psB
        pswait_t(ppname)
        for k in range(HT):
            for (cc0, cn) in nchW:
                E("tensor", lambda api, k=k, cc0=cc0, cn=cn, buf=buf, pp=pp: api.matmul(
                    pp[:, cc0:cc0 + cn], wstream[:, buf + k * 128:buf + (k + 1) * 128],
                    xqbf[:, k * TW + cc0:k * TW + cc0 + cn],
                    start=(k == 0), stop=(k == HT - 1)), inc=sT)
        wsmark[buf] = sT.n
        if ot < DT:
            dtile = ot
            W("scalar", sT)
            if ot >= 2:
                W("scalar", dU, ip_spill_mark.get(ot - 2, 0))
            xcb = SP_U[ot % 2]
            E("scalar", lambda api, pp=pp, xcb=xcb: api.copy(out=fview(xcb, TC), in_=pp[:, 0:TC]), inc=sS)
            psmark[ppname] = (sS, sS.n)
            E("scalar", lambda api, xcb=xcb, dtile=dtile: api.copy(
                out=xc3_sb[:, dtile * 3:dtile * 3 + 3], in_=fview(xcb, TC)[:, TC - 3:TC]), inc=sS)
            W("sync", sS)
            dma("sync", d_scr_xc[dtile], fview(xcb, TC), dU)
            ip_spill_mark[ot] = dU.n
            if c.debug:
                dma("sync", d_dbg3[dtile], fview(xcb, TC), dU)
        else:
            dtile = ot - DT
            W("scalar", sT)
            if ot >= 2:
                W("scalar", dU, ip_spill_mark.get(ot - 2, 0))
            sgb = SP_SG[ot % 2]
            E("scalar", lambda api, pp=pp, sgb=sgb: api.activation(
                out=fview(sgb, TC), in_=pp[:, HALO:HALO + TC], func=Act.Silu), inc=sS)
            psmark[ppname] = (sS, sS.n)
            W("sync", sS)
            dma("sync", d_scr_sg[dtile], fview(sgb, TC), dU)
            ip_spill_mark[ot] = dU.n

    W("sync", sS)
    dma("sync", d_cc3in, xc3_sb, dU)
    W("gpsimd", dU)
    E("gpsimd", lambda api: api.collective_compute(
        "AllGather", Alu.bypass, replica_groups=[list(range(c.NC))],
        ins=[d_cc3in_h.ap().rearrange("p q -> (p q)")],
        outs=[d_cc3out_h.ap().rearrange("n p q -> (n p q)")]), inc=sCC)
    cc3_done = sCC.n
    W("vector", sCC, cc3_done)
    E("vector", lambda api: api.memset(b3_sb, 0.0), inc=sV)
    for j in range(c.NC):
        W("sync", sV)
        dma("sync", gb[:, 0:DT * 3], d_cc3out[j], dR)
        W("vector", dR)
        E("vector", lambda api, j=j: api.tensor_scalar(
            out=gb[:, 0:DT * 3], in0=gb[:, 0:DT * 3],
            scalar1=cmask_sb[:, j:j + 1], scalar2=None, op0=Alu.mult))
        E("vector", lambda api: api.tensor_tensor(out=b3_sb, in0=b3_sb, in1=gb[:, 0:DT * 3],
                                                  op=Alu.add), inc=sV)

    cv_spill = {}
    cv_v = {}
    for dtile in range(DT):
        head = dtile // DTH
        xcb = SP_U[dtile % 2]
        ub = SP_DT[dtile % 2]
        if dtile >= 2:
            W("sync", sV, cv_v.get(dtile - 2, 0))
            W("sync", dU, cv_spill.get(dtile - 2, 0))
        dma("sync", fview(xcb, TC), d_scr_xc[dtile], dR)
        W("vector", dR)
        E("vector", lambda api, xcb=xcb, dtile=dtile: api.tensor_scalar(
            out=gb[:, 3:TC], in0=fview(xcb, TC)[:, 0:TC - 3],
            scalar1=cw_sb[:, dtile * 4:dtile * 4 + 1], scalar2=None, op0=Alu.mult))
        for kk in range(1, 4):
            E("vector", lambda api, xcb=xcb, dtile=dtile, kk=kk: api.scalar_tensor_tensor(
                out=gb[:, 3:TC], in0=fview(xcb, TC)[:, kk:kk + TC - 3],
                scalar=cw_sb[:, dtile * 4 + kk:dtile * 4 + kk + 1], in1=gb[:, 3:TC],
                op0=Alu.mult, op1=Alu.add))
        E("vector", lambda api, dtile=dtile: api.tensor_copy(out=ext6[:, 0:3],
                                                             in_=b3_sb[:, dtile * 3:dtile * 3 + 3]))
        E("vector", lambda api, xcb=xcb: api.tensor_copy(out=ext6[:, 3:6],
                                                         in_=fview(xcb, TC)[:, 0:3]), inc=sV)
        W("vector", sV)
        E("vector", lambda api, dtile=dtile: api.tensor_scalar(
            out=gb[:, 0:3], in0=ext6[:, 0:3],
            scalar1=cw_sb[:, dtile * 4:dtile * 4 + 1], scalar2=None, op0=Alu.mult))
        for kk in range(1, 4):
            E("vector", lambda api, dtile=dtile, kk=kk: api.scalar_tensor_tensor(
                out=gb[:, 0:3], in0=ext6[:, kk:kk + 3],
                scalar=cw_sb[:, dtile * 4 + kk:dtile * 4 + kk + 1], in1=gb[:, 0:3],
                op0=Alu.mult, op1=Alu.add), inc=(sV if kk == 3 else None))
        cv_v[dtile] = sV.n
        W("scalar", sV)
        E("scalar", lambda api, ub=ub, dtile=dtile: api.activation(
            out=fview(ub, TC), in_=gb, func=Act.Silu,
            bias=cb_sb[:, dtile:dtile + 1]), inc=sS)
        E("scalar", lambda api, ub=ub, uu=ubft[dtile % 2]: api.copy(out=uu, in_=fview(ub, TC)), inc=sS)
        W("sync", sS)
        dma("sync", d_scr_u[dtile], fview(ub, TC), dU)
        cv_spill[dtile] = dU.n
        W("tensor", sS)
        if dtile >= 2:
            W("tensor", sS)
        wxoff = (head * DTH + (dtile % DTH)) * (RK + 2 * NS)
        pswait_t("C")
        for (cc0, cn) in nchC:
            E("tensor", lambda api, cc0=cc0, cn=cn, wxoff=wxoff, uu=ubft[dtile % 2]: api.matmul(
                psC[:, cc0:cc0 + cn], wxbf[:, wxoff:wxoff + RK],
                uu[:, cc0:cc0 + cn], start=True, stop=True), inc=sT)
        xp_t1 = sT.n
        W("vector", sT, xp_t1)
        E("vector", lambda api, head=head: api.tensor_tensor(
            out=ssm_dt[:, head * TC:(head + 1) * TC], in0=ssm_dt[:, head * TC:(head + 1) * TC],
            in1=psC[:, 0:TC], op=Alu.add), inc=sV)
        psmark["C"] = (sV, sV.n)
        pswait_t("C")
        for (cc0, cn) in nchC:
            E("tensor", lambda api, cc0=cc0, cn=cn, wxoff=wxoff, uu=ubft[dtile % 2]: api.matmul(
                psC[0:32, cc0:cc0 + cn], wxbf[:, wxoff + RK:wxoff + RK + 2 * NS],
                uu[:, cc0:cc0 + cn], start=True, stop=True), inc=sT)
        W("vector", sT)
        E("vector", lambda api, head=head: api.tensor_tensor(
            out=ssm_bc[:, head * TC:(head + 1) * TC], in0=ssm_bc[:, head * TC:(head + 1) * TC],
            in1=psC[0:32, 0:TC], op=Alu.add), inc=sV)
        psmark["C"] = (sV, sV.n)
        cv_v[dtile] = sV.n

    W("scalar", sV)
    E("scalar", lambda api: api.copy(out=dtin_bf, in_=ssm_dt), inc=sS)
    E("scalar", lambda api: api.copy(out=bcrow_bf, in_=ssm_bc), inc=sS)
    dtin_mark = sS.n
    W("sync", sS)
    dma("sync", d_scr_bc.rearrange("(h r) w -> r h w", h=NH),
        bcrow_bf.rearrange("p (h w) -> p h w", h=NH), dU)
    bc_spill_mark = dU.n

    ph1_v_mark = {}
    ph1_spill_mark = {}
    for head in range(NH):
        W("sync", dU, bc_spill_mark)
        W("sync", sT)
        W("sync", sV)
        if head == 0:
            W("sync", dU, x1_spilled)
        for n in range(NS):
            dma("sync", brep[:, n * TC:(n + 1) * TC],
                d_scr_bc[head * 2 * NS + n:head * 2 * NS + n + 1, :].broadcast_to([P, TC]), dR)
        brep_done = dR.n
        W("gpsimd", sT, wsmark[0])
        dma("gpsimd", wstream[:, 0:DTH * 128].rearrange("p (t q) -> p t q", q=128),
            d_wdt.rearrange("h p (t q) -> p h t q", q=128)[:, head], dWG)
        wdt_here = dWG.n
        W("tensor", dWG, wdt_here)
        for dl in range(DTH):
            dtile = head * DTH + dl
            W("tensor", sS, dtin_mark)
            pswait_t("C")
            for (cc0, cn) in nchC:
                E("tensor", lambda api, cc0=cc0, cn=cn, head=head, dl=dl: api.matmul(
                    psC[:, cc0:cc0 + cn], wstream[:, dl * 128:(dl + 1) * 128],
                    dtin_bf[:, head * TC + cc0:head * TC + cc0 + cn], start=True, stop=True), inc=sT)
            dtp_mark = sT.n
            W("scalar", sT, dtp_mark)
            dtb0 = SP_DT[dtile % 2]
            if dtile >= 2:
                W("scalar", sV, ph1_v_mark[dtile - 2])
                W("vector", dU, ph1_spill_mark.get(dtile - 2, 0))
            E("scalar", lambda api, dtile=dtile: api.activation(
                out=scrF[:, 0:TC], in_=psC[:, 0:TC], func=Act.Identity,
                bias=dtb_sb[:, dtile:dtile + 1]))
            E("scalar", lambda api: api.activation(out=scrF[:, TW:TW + TC], in_=scrF[:, 0:TC], func=Act.Abs))
            E("scalar", lambda api: api.activation(out=scrF[:, TW:TW + TC], in_=scrF[:, TW:TW + TC],
                                                   func=Act.Exp, scale=-1.0))
            E("scalar", lambda api: api.activation(out=scrF[:, TW:TW + TC], in_=scrF[:, TW:TW + TC],
                                                   func=Act.Ln, bias=1.0), inc=sS)
            W("vector", sS)
            E("vector", lambda api: api.tensor_scalar(out=evst, in0=scrF[:, 0:TC],
                                                      scalar1=0.0, scalar2=None, op0=Alu.max))
            E("vector", lambda api, dtb0=dtb0: api.tensor_tensor(
                out=fview(dtb0, TC), in0=scrF[:, TW:TW + TC], in1=evst, op=Alu.add), inc=sV)
            psmark["C"] = (sV, sV.n)
            dt_ready = sV.n
            W("sync", sV, dt_ready)
            dma("sync", d_scr_dt[dtile], fview(dtb0, TC), dU)
            ph1_spill_mark[dtile] = dU.n
            ub = SP_U[dtile % 2]
            if dtile >= 2:
                W("sync", sV, ph1_v_mark[dtile - 2])
            dma("sync", fview(ub, TC), d_scr_u[dtile], dR)
            u_here = dR.n
            E("vector", lambda api, dtb0=dtb0: api.tensor_tensor_scan(
                out=cdtb, data0=fview(dtb0, TC), data1=zcol.broadcast_to([P, TC]),
                initial=0.0, op0=Alu.add, op1=Alu.add), inc=sV)
            W("vector", sV)
            E("vector", lambda api: api.tensor_scalar(out=nrow[:, 0:1], in0=cdtb[:, TC - 1:TC],
                                                      scalar1=-1.0, scalar2=None, op0=Alu.mult), inc=sV)
            W("vector", sV)
            E("vector", lambda api: api.tensor_scalar(out=argb, in0=cdtb,
                                                      scalar1=nrow[:, 0:1], scalar2=None, op0=Alu.add))
            W("vector", dR, u_here)
            E("vector", lambda api, dtb0=dtb0, ub=ub: api.tensor_tensor(
                out=gb, in0=fview(dtb0, TC), in1=fview(ub, TC), op=Alu.mult))
            E("vector", lambda api, dtile=dtile: api.tensor_scalar(
                out=dec16, in0=negA[:, dtile * NS:(dtile + 1) * NS],
                scalar1=cdtb[:, TC - 1:TC], scalar2=None, op0=Alu.mult), inc=sV)
            gv = sV.n
            W("scalar", sV, gv)
            E("scalar", lambda api, dtile=dtile: api.activation(
                out=Decb[:, dtile * NS:(dtile + 1) * NS], in_=dec16, func=Act.Exp), inc=sS)
            ph1_g = {}
            ph1_vn = {}
            for n in range(NS):
                fb0 = (n % 2) * TC
                if n >= 2:
                    W("scalar", sG, ph1_g[n - 2])
                E("scalar", lambda api, dtile=dtile, n=n, fb0=fb0: api.activation(
                    out=Fb[:, fb0:fb0 + TC], in_=argb, func=Act.Exp,
                    scale=posA[:, dtile * NS + n:dtile * NS + n + 1]), inc=sS)
                W("gpsimd", sS)
                if n >= 2:
                    W("gpsimd", sV, ph1_vn[n - 2])
                E("gpsimd", lambda api, fb0=fb0: api.tensor_tensor(
                    out=Wb[:, fb0:fb0 + TC], in0=Fb[:, fb0:fb0 + TC], in1=gb, op=Alu.mult), inc=sG)
                ph1_g[n] = sG.n
                W("vector", sG, ph1_g[n])
                if dtile % DTH == 0 and n == 0:
                    W("vector", dR, brep_done)
                E("vector", lambda api, fb0=fb0, n=n, dtile=dtile: api.scalar_tensor_tensor(
                    out=dumpb, in0=Wb[:, fb0:fb0 + TC], scalar=1.0,
                    in1=brep[:, n * TC:(n + 1) * TC],
                    op0=Alu.mult, op1=Alu.mult,
                    accum_out=Scol[:, dtile * NS + n:dtile * NS + n + 1]), inc=sV)
                ph1_vn[n] = sV.n
            ph1_v_mark[dtile] = sV.n
        wsmark[0] = sT.n

    W("sync", sV)
    W("sync", sS)
    dma("sync", d_cc2in[0], Scol, dU)
    dma("sync", d_cc2in[1], Decb, dU)
    if c.debug:
        dma("sync", d_dbgcc[0], Scol, dU)
        dma("sync", d_dbgcc[1], Decb, dU)
    W("gpsimd", dU)
    E("gpsimd", lambda api: api.collective_compute(
        "AllGather", Alu.bypass, replica_groups=[list(range(c.NC))],
        ins=[d_cc2in_h.ap().rearrange("a p q -> (a p q)")],
        outs=[d_cc2out_h.ap().rearrange("n a p q -> (n a p q)")]), inc=sCC)
    W("sync", sCC)
    E("vector", lambda api: api.memset(sInb, 0.0), inc=sV)
    DN = DT * NS
    for j in range(c.NC - 1):
        W("sync", sV)
        dma("sync", agb[:, 0:DN], d_cc2out[j, 0], dR)
        dma("sync", agb[:, DN:2 * DN], d_cc2out[j, 1], dR)
        W("vector", dR)
        E("vector", lambda api: api.tensor_scalar(
            out=agb[:, DN:2 * DN], in0=agb[:, DN:2 * DN], scalar1=1.0, scalar2=None, op0=Alu.subtract))
        E("vector", lambda api, j=j: api.tensor_scalar(
            out=agb[:, DN:2 * DN], in0=agb[:, DN:2 * DN],
            scalar1=pmask_sb[:, j:j + 1], scalar2=1.0, op0=Alu.mult, op1=Alu.add))
        E("vector", lambda api: api.tensor_tensor(out=sInb, in0=sInb, in1=agb[:, DN:2 * DN], op=Alu.mult))
        E("vector", lambda api, j=j: api.tensor_scalar(
            out=agb[:, 0:DN], in0=agb[:, 0:DN], scalar1=pmask_sb[:, j:j + 1], scalar2=None, op0=Alu.mult))
        E("vector", lambda api: api.tensor_tensor(out=sInb, in0=sInb, in1=agb[:, 0:DN], op=Alu.add), inc=sV)

    if c.debug:
        W("sync", sV)
        dma("sync", d_dbgcc[2], sInb, dU)
    ph2_v_mark = {}
    ph2_ygfree = [0]
    for head in range(NH):
        W("sync", sV)
        W("sync", sT)
        for n in range(NS):
            dma("sync", brep[:, n * TC:(n + 1) * TC],
                d_scr_bc[head * 2 * NS + n:head * 2 * NS + n + 1, :].broadcast_to([P, TC]), dR)
            dma("sync", crep[:, n * TC:(n + 1) * TC],
                d_scr_bc[head * 2 * NS + NS + n:head * 2 * NS + NS + n + 1, :].broadcast_to([P, TC]), dR)
        rep_done = dR.n
        for dl in range(DTH):
            dtile = head * DTH + dl
            kg = dtile % KGRP
            b2 = dtile % 2
            dtb0 = SP_DT[b2]
            ub = SP_U[b2]
            sgb = SP_SG[b2]
            if dtile >= 2:
                W("sync", sV, ph2_v_mark[dtile - 2])
            dma("sync", fview(dtb0, TC), d_scr_dt[dtile], dR)
            dma("sync", fview(ub, TC), d_scr_u[dtile], dR)
            dma("sync", fview(sgb, TC), d_scr_sg[dtile], dR)
            W("vector", dR)
            if dtile == 0:
                W("vector", dR, rep_done)
            E("vector", lambda api, dtb0=dtb0, ub=ub: api.tensor_tensor(
                out=gb, in0=fview(dtb0, TC), in1=fview(ub, TC), op=Alu.mult))
            E("vector", lambda api: api.memset(yb, 0.0), inc=sV)
            gmark = sV.n
            W("scalar", sV, gmark)
            ph2_sn = {}
            ph2_vn = {}
            ph2_gn = {}
            for n in range(NS):
                fb0 = (n % 2) * TC
                if n >= 2:
                    W("scalar", sV, ph2_vn[n - 2])
                E("scalar", lambda api, dtile=dtile, n=n, fb0=fb0, dtb0=dtb0: api.activation(
                    out=dAb[:, fb0:fb0 + TC], in_=fview(dtb0, TC), func=Act.Exp,
                    scale=negA[:, dtile * NS + n:dtile * NS + n + 1]), inc=sS)
                ph2_sn[n] = sS.n
                W("vector", sS, ph2_sn[n])
                E("vector", lambda api, fb0=fb0, n=n: api.tensor_tensor(
                    out=gBb[:, fb0:fb0 + TC], in0=gb,
                    in1=brep[:, n * TC:(n + 1) * TC], op=Alu.mult))
                E("vector", lambda api, fb0=fb0, n=n, dtile=dtile: api.tensor_tensor_scan(
                    out=stb, data0=dAb[:, fb0:fb0 + TC], data1=gBb[:, fb0:fb0 + TC],
                    initial=sInb[:, dtile * NS + n:dtile * NS + n + 1],
                    op0=Alu.mult, op1=Alu.add))
                if n >= 2:
                    W("vector", sG, ph2_gn[n - 2])
                E("vector", lambda api, fb0=fb0, n=n: api.tensor_tensor(
                    out=csb[:, fb0:fb0 + TC], in0=stb,
                    in1=crep[:, n * TC:(n + 1) * TC], op=Alu.mult), inc=sV)
                ph2_vn[n] = sV.n
                W("gpsimd", sV, ph2_vn[n])
                E("gpsimd", lambda api, fb0=fb0: api.tensor_tensor(
                    out=yb, in0=yb, in1=csb[:, fb0:fb0 + TC], op=Alu.add), inc=sG)
                ph2_gn[n] = sG.n
            W("vector", sG, ph2_gn[NS - 1])
            E("vector", lambda api, ub=ub, dtile=dtile: api.scalar_tensor_tensor(
                out=yb, in0=fview(ub, TC), scalar=Dp_sb[:, dtile:dtile + 1],
                in1=yb, op0=Alu.mult, op1=Alu.add))
            W("vector", sT, ph2_ygfree[0])
            E("vector", lambda api, sgb=sgb, kg=kg: api.tensor_tensor(
                out=ygbuf[:, kg * TC:(kg + 1) * TC], in0=yb, in1=fview(sgb, TC),
                op=Alu.mult), inc=sV)
            ph2_v_mark[dtile] = sV.n
            if kg == KGRP - 1:
                g = dtile // KGRP
                yg_done = sV.n
                ev_mark = [0]
                for m in range(HT):
                    buf = (m % 2) * 4096
                    nkr = max(1, KGRP // OPC)
                    for half in range(nkr):
                        r = g * nkr + half
                        wb_ = r * PACK + base_wout + (m * OPC) * TB
                        wslab(buf, half * OPC * 128, OPC, wb_, wait_mark=(half == 0))
                    W("tensor", dW)
                    W("tensor", sV, yg_done)
                    ppname = "A" if (m % 2 == 0) else "B"
                    pp = psA if (m % 2 == 0) else psB
                    pswait_t(ppname)
                    for k in range(KGRP):
                        for (cc0, cn) in nchC:
                            E("tensor", lambda api, k=k, cc0=cc0, cn=cn, buf=buf, pp=pp: api.matmul(
                                pp[:, cc0:cc0 + cn], wstream[:, buf + k * 128:buf + (k + 1) * 128],
                                ygbuf[:, k * TC + cc0:k * TC + cc0 + cn],
                                start=(k == 0), stop=(k == KGRP - 1)), inc=sT)
                    wsmark[buf] = sT.n
                    W("scalar", sT)
                    W("scalar", dU, ev_mark[0])
                    E("scalar", lambda api, pp=pp: api.copy(out=evst, in_=pp[:, 0:TC]), inc=sS)
                    psmark[ppname] = (sS, sS.n)
                    W("sync", sS)
                    dma("sync", d_macc[g, m * 128:(m + 1) * 128, :], evst, dU)
                    ev_mark[0] = dU.n
                ph2_ygfree[0] = sT.n

    W("sync", sV)
    W("sync", sS)
    for t in range(HT):
        dma("sync", xres[:, t * TW:(t + 1) * TW],
            d_x1.rearrange("(t p) w -> p t w", p=P)[:, t], dR)
    x1_back = dR.n
    mslot = [fview(O_R3, TC), fview(O_R3 + TC, TC)]
    for t in range(HT):
        for g in range(NGR):
            W("sync", sV)
            dma("sync", mslot[g % 2], d_macc[g, t * 128:(t + 1) * 128, :], dR)
            W("vector", dR)
            if t == 0 and g == 0:
                W("vector", dR, x1_back)
            xa = xres[:, t * TW + HALO:(t + 1) * TW]
            E("vector", lambda api, xa=xa, g=g: api.tensor_tensor(
                out=xa, in0=xa, in1=mslot[g % 2], op=Alu.add), inc=sV)

    ffn(HALO, TC)

    W("sync", sV)
    dma("sync", d_out.rearrange("(t p) w -> p t w", p=P),
        xres.rearrange("p (t w) -> p t w", w=TW)[:, :, HALO:TW], dO)
    W("sync", dO)

    with nc.Block() as blk:
        @blk.sync
        def _(api):
            for thk in kb.streams["sync"]:
                thk(api)

        @blk.tensor
        def _(api):
            for thk in kb.streams["tensor"]:
                thk(api)

        @blk.vector
        def _(api):
            for thk in kb.streams["vector"]:
                thk(api)

        @blk.scalar
        def _(api):
            for thk in kb.streams["scalar"]:
                thk(api)

        @blk.gpsimd
        def _(api):
            for thk in kb.streams["gpsimd"]:
                thk(api)
    st.close()
    return nc


def prepare_inputs(inputs, cfg):
    c = cfg

    def get(k):
        return np.asarray(inputs[k], np.float32)
    x = get("L0_attn_hidden_states")[0]
    x_fm = np.ascontiguousarray(x.T)
    w_in = get("L0_attn_in_proj_w")
    w_in_perm = np.concatenate([w_in[0::2], w_in[1::2]], 0)
    W_inT = np.ascontiguousarray(w_in_perm.T)
    WoutT = np.ascontiguousarray(get("L0_attn_out_proj_w").T)
    WxT = np.ascontiguousarray(get("L0_attn_x_proj_weight").transpose(0, 2, 1))
    WdtT = np.ascontiguousarray(get("L0_attn_dt_proj_weight").transpose(0, 2, 1))
    cw = np.ascontiguousarray(get("L0_attn_conv_w").reshape(c.DT, 128, 4))
    cb = np.ascontiguousarray(get("L0_attn_conv_b").reshape(c.DT, 128).T)
    dtb = np.ascontiguousarray(get("L0_attn_dt_proj_bias").reshape(c.DT, 128).T)
    Dp = np.ascontiguousarray(get("L0_attn_D").reshape(c.DT, 128).T)
    alog = np.ascontiguousarray(get("L0_attn_A_log").reshape(c.DT, 128, c.NS))
    wgate = get("L0_ffn_w_gate")
    wup = get("L0_ffn_w_up")
    wdown = get("L0_ffn_w_down")
    import ml_dtypes
    iden = np.eye(128, dtype=ml_dtypes.bfloat16)
    in_maps = []
    rg = c.IFF // c.NC
    rh = c.H // c.NC
    ro = c.IP // c.NC
    for cid in range(c.NC):
        t0 = cid * c.TC
        x_sh = np.ascontiguousarray(x_fm[:, t0:t0 + c.TC])
        pm = np.zeros((128, c.NC), np.float32)
        pm[:, :cid] = 1.0
        cm = np.zeros((128, c.NC), np.float32)
        if cid > 0:
            cm[:, cid - 1] = 1.0
        in_maps.append(dict(
            x_in=x_sh,
            wg_sl=np.ascontiguousarray(wgate[cid * rg:(cid + 1) * rg]),
            wu_sl=np.ascontiguousarray(wup[cid * rg:(cid + 1) * rg]),
            wd_sl=np.ascontiguousarray(wdown[cid * rh:(cid + 1) * rh]),
            wi_sl=np.ascontiguousarray(W_inT[cid * rh:(cid + 1) * rh]),
            wo_sl=np.ascontiguousarray(WoutT[cid * ro:(cid + 1) * ro]),
            wx=WxT, wdt=WdtT, convw=cw, convb=cb, dtb=dtb, Dp=Dp, alog=alog,
            iden=iden, idenf=np.eye(128, dtype=np.float32), pmask=pm, cmask=cm,
        ))
    return in_maps


_CFG = CFG(mini=False)


def kernel(**inputs):
    cfg = _CFG
    in_maps = prepare_inputs(inputs, cfg)
    nc = build_program(cfg)
    res = run_bass_kernel_spmd(nc, in_maps, core_ids=list(range(cfg.NC)))
    outs = [res.results[i]["out_fm"] for i in range(cfg.NC)]
    out_fm = np.concatenate(outs, axis=1)
    return np.ascontiguousarray(out_fm.T)[None]


# version 04
# speedup vs baseline: 1.0222x; 1.0222x over previous
import numpy as np
import contextlib
import concourse.bass as bass
import concourse.mybir as mybir
from concourse.bass_utils import run_bass_kernel_spmd

F32 = mybir.dt.float32
BF16 = mybir.dt.bfloat16
I32 = mybir.dt.int32
FP8 = mybir.dt.float8e4
Alu = mybir.AluOpType
Act = mybir.ActivationFunctionType
AxX = mybir.AxisListType.X


class CFG:
    def __init__(self, mini=False, debug=False):
        self.NC = 8
        self.debug = debug
        if mini:
            self.H = 1024
            self.IFF = 1024
            self.TC = 128
            self.HD = 512
        else:
            self.H = 2048
            self.IFF = 8192
            self.TC = 1024
            self.HD = 2048
        self.NH = 2
        self.IP = self.NH * self.HD
        self.NS = 16
        self.RK = 128
        self.HALO = 0
        self.TW = self.TC
        self.HT = self.H // 128
        self.FT = self.IFF // 128
        self.DT = self.IP // 128
        self.DTH = self.HD // 128
        self.NQ = 4
        self.QT = self.FT // self.NQ
        self.KGRP = min(8, self.DTH)


def chunks_of(total, step=512):
    out = []
    cpos = 0
    while cpos < total:
        out.append((cpos, min(step, total - cpos)))
        cpos += step
    return out


class Sem:
    def __init__(self, h, name):
        self.h = h
        self.n = 0
        self.name = name


class KB:
    ENGINES = ("sync", "tensor", "vector", "scalar", "gpsimd")

    def __init__(self, nc):
        self.nc = nc
        self.streams = {e: [] for e in self.ENGINES}

    def emit(self, eng, fn, inc=None, inck=1):
        if inc is not None:
            inc.n += inck

        def th(api, fn=fn, inc=inc, inck=inck):
            r = fn(api)
            if inc is not None:
                r.then_inc(inc.h, inck)
        self.streams[eng].append(th)

    def wait(self, eng, sem, val=None):
        v = sem.n if val is None else val
        if v is None or v <= 0:
            return
        assert v <= sem.n, f"wait {sem.name}>={v} but only {sem.n} planned"
        self.streams[eng].append(lambda api, s=sem, v=v: api.wait_ge(s.h, v))

    def inc(self, eng, sem, k=1):
        sem.n += k
        self.streams[eng].append(lambda api, s=sem, k=k: api.sem_inc(s.h, k))


def build_program(cfg):
    c = cfg
    nc = bass.Bass("TRN2", target_bir_lowering=False, debug=False,
                   detect_race_conditions=False)
    P = 128
    TW, TC, HALO = c.TW, c.TC, c.HALO
    HT, FT, DT, DTH = c.HT, c.FT, c.DT, c.DTH
    NS, RK, NH = c.NS, c.RK, c.NH
    KGRP = c.KGRP
    NGR = DT // KGRP

    d_x = nc.dram_tensor("x_in", [c.H, TW], F32, kind="ExternalInput").ap()
    d_out = nc.dram_tensor("out_fm", [c.H, TC], F32, kind="ExternalOutput").ap()
    d_wg = nc.dram_tensor("wg_sl", [c.IFF // c.NC, c.H], F32, kind="ExternalInput").ap()
    d_wu = nc.dram_tensor("wu_sl", [c.IFF // c.NC, c.H], F32, kind="ExternalInput").ap()
    d_wd = nc.dram_tensor("wd_sl", [c.H // c.NC, c.IFF], F32, kind="ExternalInput").ap()
    d_wi = nc.dram_tensor("wi_sl", [c.H // c.NC, 2 * c.IP], F32, kind="ExternalInput").ap()
    d_wo = nc.dram_tensor("wo_sl", [c.IP // c.NC, c.H], F32, kind="ExternalInput").ap()
    d_wx = nc.dram_tensor("wx", [NH, c.HD, 2 * NS + RK], F32, kind="ExternalInput").ap()
    d_wdt = nc.dram_tensor("wdt", [NH, RK, c.HD], F32, kind="ExternalInput").ap()
    d_cw = nc.dram_tensor("convw", [DT, 128, 4], F32, kind="ExternalInput").ap()
    d_cb = nc.dram_tensor("convb", [128, DT], F32, kind="ExternalInput").ap()
    d_dtb = nc.dram_tensor("dtb", [128, DT], F32, kind="ExternalInput").ap()
    d_Dp = nc.dram_tensor("Dp", [128, DT], F32, kind="ExternalInput").ap()
    d_alog = nc.dram_tensor("alog", [DT, 128, NS], F32, kind="ExternalInput").ap()
    d_iden = nc.dram_tensor("iden", [128, 128], BF16, kind="ExternalInput").ap()
    d_idenf = nc.dram_tensor("idenf", [128, 128], F32, kind="ExternalInput").ap()
    d_pmask = nc.dram_tensor("pmask", [128, c.NC], F32, kind="ExternalInput").ap()
    d_cmask = nc.dram_tensor("cmask", [128, c.NC], F32, kind="ExternalInput").ap()

    TB = 128 * 128
    RPC = c.IFF // c.NC // 128
    HPC = c.H // c.NC // 128
    OPC = c.IP // c.NC // 128
    n_gate_blk = RPC * HT
    n_down_blk = HPC * FT
    n_win_blk = HPC * (2 * c.IP // 128)
    n_wout_blk = OPC * HT
    base_gate = 0
    base_up = base_gate + n_gate_blk * TB
    base_down = base_up + n_gate_blk * TB
    base_win = base_down + n_down_blk * TB
    base_wout = base_win + n_win_blk * TB
    PACK = base_wout + n_wout_blk * TB + 8 * TB
    DBGK = dict(kind="ExternalOutput") if c.debug else {}
    d_pack_h = nc.dram_tensor("wpack", [PACK], BF16)
    d_wag_h = nc.dram_tensor("wag", [c.NC * PACK], BF16)
    d_pack = d_pack_h.ap()
    wag = d_wag_h.ap()

    d_scr_xc_h = nc.dram_tensor("scr_xc", [DT, 128, TC], F32, **DBGK)
    d_cc3in_h = nc.dram_tensor("cc3in", [128, DT * 3], F32)
    d_cc3out_h = nc.dram_tensor("cc3out", [c.NC, 128, DT * 3], F32)
    d_scr_u_h = nc.dram_tensor("scr_u", [DT, 128, TC], F32, **DBGK)
    d_scr_dt_h = nc.dram_tensor("scr_dt", [DT, 128, TC], F32, **DBGK)
    d_scr_sg_h = nc.dram_tensor("scr_sg", [DT, 128, TC], F32, **DBGK)
    d_scr_bc_h = nc.dram_tensor("scr_bc", [NH * 2 * NS, TC], BF16, **DBGK)
    d_scr_ln_h = nc.dram_tensor("scr_ln", [2, TW], F32)
    d_x1_h = nc.dram_tensor("scr_x1", [c.H, TW], F32, **DBGK)
    d_macc_h = nc.dram_tensor("scr_macc", [NGR, c.H, TC], F32)
    d_cc2in_h = nc.dram_tensor("cc2in", [2, 128, DT * NS], F32)
    d_cc2out_h = nc.dram_tensor("cc2out", [c.NC, 2, 128, DT * NS], F32)
    d_scr_xc = d_scr_xc_h.ap(); d_cc3in = d_cc3in_h.ap(); d_cc3out = d_cc3out_h.ap()
    d_scr_u = d_scr_u_h.ap(); d_scr_dt = d_scr_dt_h.ap(); d_scr_sg = d_scr_sg_h.ap()
    d_scr_bc = d_scr_bc_h.ap(); d_scr_ln = d_scr_ln_h.ap()
    d_x1 = d_x1_h.ap(); d_macc = d_macc_h.ap()
    d_cc2in = d_cc2in_h.ap(); d_cc2out = d_cc2out_h.ap()
    d_dbg3 = d_dbgcc = None
    if c.debug:
        d_dbg3 = nc.dram_tensor("dbg_xc", [DT, 128, TC], F32, kind="ExternalOutput").ap()
        d_dbgcc = nc.dram_tensor("dbg_cc", [3, 128, DT * NS], F32, kind="ExternalOutput").ap()
        d_dbgxq = nc.dram_tensor("dbg_xq", [128, HT * TW], BF16, kind="ExternalOutput").ap()
        d_dbgiq = nc.dram_tensor("dbg_iq", [128, c.QT * TW], BF16, kind="ExternalOutput").ap()
        d_dbgxn2 = nc.dram_tensor("dbg_xn2", [128, HT * TW], BF16, kind="ExternalOutput").ap()
        d_dbgwin = nc.dram_tensor("dbg_win", [8, 128, 128], BF16, kind="ExternalOutput").ap()

    st = contextlib.ExitStack()

    class Alloc:
        def __init__(self, off=0):
            self.off = off

        def take(self, n):
            o = self.off
            self.off += n
            return o

    al = Alloc()
    O_R0 = al.take(3456)
    O_R1 = al.take(max(HT * TW, NS * TC, 5 * 2048 + 2 * 1024 + 256))
    O_R2 = al.take(max(HT * TW // 2, 4 * TC + (KGRP * TC) // 2 + 64))
    O_R3 = al.take(max(c.QT * TW // 2, 8 * TC + 64))
    O_R4 = al.take(4096)
    O_R5 = al.take(2 * TW + TW // 2 + TW // 4 + 320)
    O_R6 = al.take(max(2 * TW, 2 * TC))
    O_R7 = al.take(NH * TC + TC + (NH * DTH * (RK + 2 * NS)) // 2 + 128)
    ATOT = al.off
    arena = st.enter_context(nc.sbuf_tensor("arena", [P, ATOT], F32))
    arbf = arena[:].bitcast(BF16)
    ar8 = arena[:].bitcast(FP8)
    ari = arena[:].bitcast(I32)

    def fview(off, n):
        return arena[:, off:off + n]

    def bview(off_f, n_bf):
        return arbf[:, 2 * off_f:2 * off_f + n_bf]

    def iview(off_f, n):
        return ari[:, off_f:off_f + n]

    a0 = Alloc(O_R0)
    posA = fview(a0.take(DT * NS), DT * NS)
    negA = fview(a0.take(DT * NS), DT * NS)
    cw_sb = fview(a0.take(DT * 4), DT * 4)
    cb_sb = fview(a0.take(DT), DT)
    dtb_sb = fview(a0.take(DT), DT)
    Dp_sb = fview(a0.take(DT), DT)
    pmask_sb = fview(a0.take(c.NC), c.NC)
    cmask_sb = fview(a0.take(c.NC), c.NC)
    xc3_sb = fview(a0.take(DT * 3), DT * 3)
    b3_sb = fview(a0.take(DT * 3), DT * 3)
    ext6 = fview(a0.take(8), 8)
    identb = bview(a0.take(64), 128)
    identf = fview(a0.take(128), 128)
    onesb = bview(a0.take(1), 1)
    nrow = fview(a0.take(2), 2)
    zcol = fview(a0.take(1), 1)
    Scol = fview(a0.take(DT * NS), DT * NS)
    Decb = fview(a0.take(DT * NS), DT * NS)
    sInb = fview(a0.take(DT * NS), DT * NS)
    dec16 = fview(a0.take(NS), NS)
    assert a0.off <= O_R0 + 3456, a0.off - O_R0

    xres = fview(O_R1, HT * TW)
    brep = bview(O_R1, NS * TC)
    crep = bview(O_R1 + (NS * TC) // 2, NS * TC)


    xqbf = bview(O_R2, HT * TW)
    dAb = fview(O_R2, 2 * TC)
    gBb = fview(O_R2 + 2 * TC, 2 * TC)
    Fb, Wb = dAb, gBb
    ygbuf = bview(O_R2 + 4 * TC, KGRP * TC)

    iqbf = bview(O_R3, c.QT * TW)
    SP_U = [O_R3 + 0 * TC, O_R3 + 1 * TC]
    SP_DT = [O_R3 + 2 * TC, O_R3 + 3 * TC]
    SP_SG = [O_R3 + 4 * TC, O_R3 + 5 * TC]
    stb = fview(O_R3 + 6 * TC, TC)
    cdtb = stb
    gb = fview(O_R3 + 7 * TC, TC)
    ubft = [bview(O_R3 + 6 * TC, TC), bview(O_R3 + 6 * TC + TC // 2, TC)]

    wstream = bview(O_R4, 8192)

    a5 = Alloc(O_R5)
    scrF = fview(a5.take(2 * TW), 2 * TW)
    scrH = bview(a5.take(TW // 2), TW)
    _o8 = a5.take(TW // 4)
    q8t = ar8[:, 4 * _o8:4 * _o8 + TW]
    amt = fview(a5.take(64), 64)
    _oi = a5.take(192)
    set_ = iview(_oi, 64)
    invt = iview(_oi + 64, 64)
    sct = iview(_oi + 128, 64)
    assert TW // 32 <= 64

    lnbc = fview(O_R6, 2 * TW)
    csb = fview(O_R6, 2 * TC)
    agb = csb

    a7 = Alloc(O_R7)
    ssm_dt = fview(O_R6, NH * TC)
    ssm_bc = arena[0:32, O_R5:O_R5 + NH * TC]
    yb = fview(O_R4 + 1024, TC)
    argb = yb
    dumpb = fview(O_R4 + 2048, TC)
    _od = a7.take(NH * TC // 2)
    dtin_bf = arbf[:, 2 * _od:2 * _od + NH * TC]
    ibuf = fview(_od, TC)
    _obc = a7.take(NH * TC // 2)
    bcrow_bf = arbf[0:32, 2 * _obc:2 * _obc + NH * TC]
    evst = fview(a7.take(TC), TC)
    wx_cols = NH * DTH * (RK + 2 * NS)
    _owx = a7.take((wx_cols + 1) // 2)
    wxbf = arbf[:, 2 * _owx:2 * _owx + wx_cols]

    def semh(name):
        return Sem(st.enter_context(nc.semaphore(name)), name)
    dL = semh("dL")
    dW = semh("dW")
    dU = semh("dU")
    dR = semh("dR")
    dO = semh("dO")
    sT = semh("sT")
    sV = semh("sV")
    sS = semh("sS")
    sG = semh("sG")
    sCC = semh("sCC")
    dWG = semh("dWG")

    kb = KB(nc)
    E = kb.emit
    W = kb.wait

    wsmark = {0: 0, 4096: 0}

    def wslab(buf, dst_off, nblk, src_elem_off, wait_mark=True):
        if wait_mark:
            W("sync", sT, wsmark[buf])
        dview = wag[src_elem_off:src_elem_off + nblk * TB].rearrange(
            "(b r q) -> r b q", r=128, q=128)
        sview = wstream[:, buf + dst_off:buf + dst_off + nblk * 128].rearrange(
            "p (b q) -> p b q", q=128)
        E("sync", lambda api, d=dview, s=sview: api.dma_start(out=s, in_=d), inc=dW, inck=16)

    def dma(eng, out, in_, sem):
        E(eng, lambda api, o=out, i=in_: api.dma_start(out=o, in_=i), inc=sem, inck=16)

    def r32(ap, nblk):
        return ap.rearrange("p (b k) -> p b k", k=32)

    def bc32(intap, nblk):
        return intap[:, 0:nblk].rearrange("p (b o) -> p b o", o=1).bitcast(F32).broadcast_to([P, nblk, 32])

    dma("sync", cw_sb.rearrange("p (t k) -> p t k", k=4), d_cw.rearrange("t p k -> p t k"), dL)
    dma("sync", cb_sb, d_cb, dL)
    dma("sync", dtb_sb, d_dtb, dL)
    dma("sync", Dp_sb, d_Dp, dL)
    dma("sync", posA.rearrange("p (t n) -> p t n", n=NS), d_alog.rearrange("t p n -> p t n"), dL)
    dma("sync", identb, d_iden, dL)
    dma("sync", identf, d_idenf, dL)
    dma("sync", pmask_sb, d_pmask, dL)
    dma("sync", cmask_sb, d_cmask, dL)
    base_load_n = dL.n
    W("scalar", dL, base_load_n)
    E("scalar", lambda api: api.activation(out=posA, in_=posA, func=Act.Exp), inc=sS)
    W("vector", sS)
    E("vector", lambda api: api.tensor_scalar(out=negA, in0=posA, scalar1=-1.0,
                                              scalar2=None, op0=Alu.mult), inc=sV)
    E("vector", lambda api: api.memset(zcol, 0.0), inc=sV)
    E("vector", lambda api: api.memset(onesb, 1.0), inc=sV)
    E("vector", lambda api: api.memset(xqbf[:, 0:1024], 0.0), inc=sV)
    E("vector", lambda api: api.memset(amt[:, 0:64], 0.0), inc=sV)
    W("sync", sV)
    dma("sync", d_pack[PACK - 8 * TB:PACK].rearrange("(p q) -> p q", p=128),
        xqbf[:, 0:1024], dO)

    WPW = 2048
    ap1 = Alloc(O_R1)
    wp_in = fview(ap1.take(WPW), WPW)
    wp_z = fview(ap1.take(WPW), WPW)
    wp_sgn = fview(ap1.take(WPW), WPW)
    wp_acc = fview(ap1.take(WPW), WPW)
    wp_tmp = fview(ap1.take(WPW), WPW)
    _oq = ap1.take(WPW // 2)
    wp_q = arbf[:, 2 * _oq:2 * _oq + WPW]
    _oe = ap1.take(WPW // 2)
    wp_ev = arbf[:, 2 * _oe:2 * _oe + WPW]
    wp_am = fview(ap1.take(WPW // 32), WPW // 32)
    _wo = ap1.take(3 * (WPW // 32))
    wp_se = iview(_wo, WPW // 32)
    wp_inv = iview(_wo + WPW // 32, WPW // 32)
    wp_sc = iview(_wo + 2 * (WPW // 32), WPW // 32)
    assert ap1.off <= O_R1 + max(HT * TW, NS * TC, 5 * 2048 + 2 * 1024 + 256)
    psTr_cm = nc.psum_tensor("psTr", [P, 2048], BF16)
    psTr = psTr_cm.__enter__()

    def quantize_weights(ncols):
        nblk = ncols // 32
        W("vector", sT)
        E("vector", lambda api, ncols=ncols, nblk=nblk: api.tensor_reduce(
            out=wp_am[:, 0:nblk], in_=r32(wp_in[:, 0:ncols], nblk),
            axis=AxX, op=Alu.max, apply_absolute_value=True))
        E("vector", lambda api, nblk=nblk: api.tensor_scalar(out=wp_se[:, 0:nblk], in0=wp_am[:, 0:nblk].bitcast(I32),
                                                             scalar1=23, scalar2=None, op0=Alu.logical_shift_right))
        E("vector", lambda api, nblk=nblk: api.tensor_scalar(out=wp_se[:, 0:nblk], in0=wp_se[:, 0:nblk],
                                                             scalar1=-2, scalar2=0, op0=Alu.add, op1=Alu.max))
        E("vector", lambda api, nblk=nblk: api.tensor_scalar(out=wp_se[:, 0:nblk], in0=wp_se[:, 0:nblk],
                                                             scalar1=254, scalar2=None, op0=Alu.min))
        E("vector", lambda api, nblk=nblk: api.tensor_scalar(out=wp_inv[:, 0:nblk], in0=wp_se[:, 0:nblk],
                                                             scalar1=-1, scalar2=254, op0=Alu.mult, op1=Alu.add))
        E("vector", lambda api, nblk=nblk: api.tensor_scalar(out=wp_inv[:, 0:nblk], in0=wp_inv[:, 0:nblk],
                                                             scalar1=23, scalar2=None, op0=Alu.logical_shift_left))
        E("vector", lambda api, nblk=nblk: api.tensor_scalar(out=wp_sc[:, 0:nblk], in0=wp_se[:, 0:nblk],
                                                             scalar1=23, scalar2=None, op0=Alu.logical_shift_left))
        E("vector", lambda api, ncols=ncols, nblk=nblk: api.tensor_tensor(
            out=r32(wp_tmp[:, 0:ncols], nblk), in0=r32(wp_in[:, 0:ncols], nblk),
            in1=bc32(wp_inv, nblk), op=Alu.mult), inc=sV)
        W("scalar", sV)
        E("scalar", lambda api, ncols=ncols: api.activation(out=wp_z[:, 0:ncols], in_=wp_tmp[:, 0:ncols],
                                                            func=Act.Abs), inc=sS)
        E("vector", lambda api, ncols=ncols: api.tensor_scalar(out=wp_sgn[:, 0:ncols], in0=wp_tmp[:, 0:ncols],
                                                               scalar1=0.0, scalar2=2.0, op0=Alu.is_ge, op1=Alu.mult))
        E("vector", lambda api, ncols=ncols: api.tensor_scalar(out=wp_sgn[:, 0:ncols], in0=wp_sgn[:, 0:ncols],
                                                               scalar1=-1.0, scalar2=None, op0=Alu.add))
        W("vector", sS)
        th = [(0.25, 0.5), (0.75, 0.5), (1.25, 0.5), (1.75, 0.5), (2.5, 1.0), (3.5, 1.0), (5.0, 2.0)]
        E("vector", lambda api, ncols=ncols: api.tensor_scalar(out=wp_acc[:, 0:ncols], in0=wp_z[:, 0:ncols],
                                                               scalar1=th[0][0], scalar2=th[0][1],
                                                               op0=Alu.is_ge, op1=Alu.mult))
        for cc_, ww_ in th[1:]:
            E("vector", lambda api, cc_=cc_, ww_=ww_, ncols=ncols: api.tensor_scalar(
                out=wp_tmp[:, 0:ncols], in0=wp_z[:, 0:ncols],
                scalar1=cc_, scalar2=ww_, op0=Alu.is_ge, op1=Alu.mult))
            E("vector", lambda api, ncols=ncols: api.tensor_tensor(out=wp_acc[:, 0:ncols], in0=wp_acc[:, 0:ncols],
                                                                   in1=wp_tmp[:, 0:ncols], op=Alu.add))
        E("vector", lambda api, ncols=ncols: api.tensor_tensor(out=wp_acc[:, 0:ncols], in0=wp_acc[:, 0:ncols],
                                                               in1=wp_sgn[:, 0:ncols], op=Alu.mult))
        E("vector", lambda api, ncols=ncols, nblk=nblk: api.tensor_tensor(
            out=r32(wp_q[:, 0:ncols], nblk), in0=r32(wp_acc[:, 0:ncols], nblk),
            in1=bc32(wp_sc, nblk), op=Alu.mult), inc=sV)

    W("tensor", dL, base_load_n)

    def prep_matrix(src, nrows, ncols, quant, blk_base, blk_order, transpose=True):
        nch = nrows // 128
        ncc = (ncols + WPW - 1) // WPW
        for ch in range(nch):
            for cc in range(ncc):
                c0 = cc * WPW
                cn = min(WPW, ncols - c0)
                nct = cn // 128
                W("sync", sV)
                W("sync", sS)
                dma("sync", wp_in[:, 0:cn], src[ch * 128:(ch + 1) * 128, c0:c0 + cn], dW)
                W("vector", dW)
                if quant:
                    quantize_weights(cn)
                else:
                    W("vector", sT)
                    kb.inc("vector", sV)
                    W("scalar", sV)
                    E("scalar", lambda api, cn=cn: api.copy(out=wp_q[:, 0:cn], in_=wp_in[:, 0:cn]), inc=sS)
                    W("vector", sS)
                    kb.inc("vector", sV)
                if transpose:
                    W("tensor", sV)
                    for j0 in range(0, nct, 16):
                        jn = min(16, nct - j0)
                        W("tensor", sS)
                        for j in range(j0, j0 + jn):
                            E("tensor", lambda api, j=j, j0=j0: api.transpose(
                                psTr[:, (j - j0) * 128:(j - j0 + 1) * 128],
                                wp_q[:, j * 128:(j + 1) * 128], identb), inc=sT)
                        W("scalar", sT)
                        E("scalar", lambda api, jn=jn, j0=j0: api.copy(
                            out=wp_ev[:, j0 * 128:(j0 + jn) * 128],
                            in_=psTr[:, 0:jn * 128]), inc=sS)
                    W("sync", sS)
                    evsrc = wp_ev
                else:
                    W("sync", sS)
                    W("sync", sV)
                    evsrc = wp_q
                idxs = [blk_order(ch, (c0 // 128) + j) for j in range(nct)]
                stride = (idxs[1] - idxs[0]) if nct > 1 else 1
                for j in range(nct - 1):
                    assert idxs[j + 1] - idxs[j] == stride
                base_off = blk_base + idxs[0] * TB
                if stride > 1:
                    dview = d_pack[base_off:base_off + nct * stride * TB]
                    dview = dview.rearrange("(b s r q) -> r b s q", s=stride, r=128, q=128)[:, :, 0, :]
                else:
                    dview = d_pack[base_off:base_off + nct * TB].rearrange("(b r q) -> r b q", r=128, q=128)
                sview = evsrc[:, 0:nct * 128].rearrange("p (b q) -> p b q", q=128)
                dma("sync", dview, sview, dO)

    prep_matrix(d_wg, c.IFF // c.NC, c.H, True, base_gate, lambda ch, j: ch * HT + j)
    prep_matrix(d_wu, c.IFF // c.NC, c.H, True, base_up, lambda ch, j: ch * HT + j)
    prep_matrix(d_wd, c.H // c.NC, c.IFF, True, base_down, lambda ch, j: ch * FT + j)
    prep_matrix(d_wi, c.H // c.NC, 2 * c.IP, False, base_win, lambda ch, j: j * HPC + ch, transpose=False)
    prep_matrix(d_wo, c.IP // c.NC, c.H, False, base_wout, lambda ch, j: j * OPC + ch, transpose=False)

    RW = RK + 2 * NS
    CHT = max(1, 2048 // RW)
    for h_ in range(NH):
        for t0_ in range(0, DTH, CHT):
            tn_ = min(CHT, DTH - t0_)
            W("sync", sS)
            W("sync", sV)
            dma("sync", wp_in[:, 0:tn_ * RW].rearrange("p (t r) -> p t r", r=RW),
                d_wx.rearrange("h (t p) r -> p h t r", p=P)[:, h_, t0_:t0_ + tn_], dL)
            W("scalar", dL)
            woff = (h_ * DTH + t0_) * RW
            E("scalar", lambda api, woff=woff, tn_=tn_: api.copy(
                out=wxbf[:, woff:woff + tn_ * RW], in_=wp_in[:, 0:tn_ * RW]), inc=sS)

    W("gpsimd", dO)
    E("gpsimd", lambda api: api.collective_compute(
        "AllGather", Alu.bypass, replica_groups=[list(range(c.NC))],
        ins=[d_pack_h.ap()], outs=[d_wag_h.ap()]), inc=sCC)

    prep_T, prep_V, prep_S = sT.n, sV.n, sS.n
    for eng in KB.ENGINES:
        W(eng, sT, prep_T)
        W(eng, sV, prep_V)
        W(eng, sS, prep_S)
    psTr_cm.__exit__(None, None, None)

    psA = st.enter_context(nc.psum_tensor("psA", [P, 1536], F32))
    psB = st.enter_context(nc.psum_tensor("psB", [P, 1536], F32))
    psC = st.enter_context(nc.psum_tensor("psC", [P, 1024], F32))
    psmark = {"A": (sV, 0), "B": (sV, 0), "C": (sV, 0)}

    def pswait_t(name):
        sem, val = psmark[name]
        W("tensor", sem, val)

    dma("sync", xres.rearrange("p (t w) -> p t w", w=TW),
        d_x.rearrange("(t p) w -> p t w", p=P), dL)
    xload_mark = dL.n
    W("vector", dL, xload_mark)
    W("scalar", dL, xload_mark)
    W("tensor", dL, xload_mark)

    def ln_and(width, col0, out_consumer):
        inv_n = 1.0 / c.H
        pswait_t("A")
        pswait_t("B")
        W("scalar", sV)
        cast_mark = {}
        xcast = [bview(O_R3, TW), bview(O_R3 + TW // 2, TW)]
        qcast = [bview(O_R3 + TW, TW), bview(O_R3 + 3 * (TW // 2), TW)]
        for t in range(HT):
            xa = xres[:, t * TW + col0: t * TW + col0 + width]
            pb = t % 2
            if t >= 2:
                W("scalar", sT, cast_mark[t - 2])
            E("scalar", lambda api, xa=xa, pb=pb, width=width: api.copy(
                out=xcast[pb][:, 0:width], in_=xa), inc=sS)
            E("scalar", lambda api, xa=xa, pb=pb, width=width: api.activation(
                out=qcast[pb][:, 0:width], in_=xa, func=Act.Square), inc=sS)
            W("tensor", sS)
            for (cc0, cn) in chunks_of(width):
                E("tensor", lambda api, cc0=cc0, cn=cn, t=t, pb=pb: api.matmul(
                    psA[0:1, cc0:cc0 + cn], onesb, xcast[pb][:, cc0:cc0 + cn],
                    start=(t == 0), stop=(t == HT - 1)), inc=sT)
                E("tensor", lambda api, cc0=cc0, cn=cn, t=t, pb=pb: api.matmul(
                    psB[0:1, cc0:cc0 + cn], onesb, qcast[pb][:, cc0:cc0 + cn],
                    start=(t == 0), stop=(t == HT - 1)), inc=sT)
            cast_mark[t] = sT.n
        W("vector", sT)
        m_ = scrF[0:1, 0:width]
        v_ = scrF[0:1, TW:TW + width]
        t_ = lnbc[0:1, 0:width]
        r_ = lnbc[0:1, TW:TW + width]
        E("vector", lambda api, m_=m_, width=width: api.tensor_scalar(out=m_, in0=psA[0:1, 0:width],
                                                                     scalar1=inv_n, scalar2=None, op0=Alu.mult))
        E("vector", lambda api, v_=v_, width=width: api.tensor_scalar(out=v_, in0=psB[0:1, 0:width],
                                                                      scalar1=inv_n, scalar2=None, op0=Alu.mult))
        E("vector", lambda api, t_=t_, m_=m_: api.tensor_tensor(out=t_, in0=m_, in1=m_, op=Alu.mult))
        E("vector", lambda api, v_=v_, t_=t_: api.tensor_tensor(out=v_, in0=v_, in1=t_, op=Alu.subtract))
        E("vector", lambda api, v_=v_: api.tensor_scalar(out=v_, in0=v_,
                                                         scalar1=1e-5, scalar2=None, op0=Alu.add), inc=sV)
        psmark["A"] = (sV, sV.n)
        psmark["B"] = (sV, sV.n)
        W("scalar", sV)
        E("scalar", lambda api, t_=t_, v_=v_: api.activation(out=t_, in_=v_, func=Act.Ln))
        E("scalar", lambda api, r_=r_, t_=t_: api.activation(out=r_, in_=t_, func=Act.Exp, scale=-0.5), inc=sS)
        W("vector", sS)
        E("vector", lambda api, t_=t_, m_=m_, r_=r_: api.tensor_tensor(out=t_, in0=m_, in1=r_, op=Alu.mult))
        E("vector", lambda api, m_=m_, t_=t_: api.tensor_scalar(out=m_, in0=t_,
                                                                scalar1=-1.0, scalar2=None, op0=Alu.mult), inc=sV)
        W("sync", sV)
        dma("sync", d_scr_ln[0:1, 0:width], r_, dU)
        dma("sync", d_scr_ln[1:2, 0:width], m_, dU)
        W("sync", dU)
        dma("sync", lnbc[:, 0:width], d_scr_ln[0:1, 0:width].broadcast_to([P, width]), dR)
        dma("sync", lnbc[:, TW:TW + width], d_scr_ln[1:2, 0:width].broadcast_to([P, width]), dR)
        W("vector", dR)
        for t in range(HT):
            xa = xres[:, t * TW + col0: t * TW + col0 + width]
            E("vector", lambda api, xa=xa, width=width: api.tensor_tensor(
                out=scrF[:, 0:width], in0=xa, in1=lnbc[:, 0:width], op=Alu.mult))
            E("vector", lambda api, width=width: api.tensor_tensor(
                out=scrF[:, TW:TW + width], in0=scrF[:, 0:width], in1=lnbc[:, TW:TW + width],
                op=Alu.add), inc=sV)
            out_consumer(t, scrF[:, TW:TW + width], width)

    def quant_act_tile(xn_ap, width, out_bf_ap):
        assert width == TC
        nblk = width // 32
        NCH = width // 128
        Y = scrF[:, 0:width]
        Y6 = scrF[:, TW:TW + width]
        pswait_t("C")
        W("tensor", sV)
        for j in range(NCH):
            E("tensor", lambda api, j=j, xn_ap=xn_ap: api.transpose(
                psC[:, j * 128:(j + 1) * 128],
                xn_ap[:, j * 128:(j + 1) * 128], identf), inc=sT)
        quant_act_tile.qin_mark = sT.n
        W("scalar", sT)
        E("scalar", lambda api, Y=Y, width=width: api.copy(out=Y, in_=psC[:, 0:width]), inc=sS)
        psmark["C"] = (sS, sS.n)
        W("vector", sS)
        E("vector", lambda api, Y=Y, nblk=nblk: api.tensor_reduce(out=amt[:, 0:nblk], in_=r32(Y, nblk),
                                                                  axis=AxX, op=Alu.max,
                                                                  apply_absolute_value=True), inc=sV)
        W("vector", sV)
        E("vector", lambda api: api.tensor_scalar(out=set_[:, 0:64], in0=amt[:, 0:64].bitcast(I32),
                                                  scalar1=23, scalar2=None, op0=Alu.logical_shift_right))
        E("vector", lambda api: api.tensor_scalar(out=set_[:, 0:64], in0=set_[:, 0:64],
                                                  scalar1=-2, scalar2=0, op0=Alu.add, op1=Alu.max))
        E("vector", lambda api: api.tensor_scalar(out=invt[:, 0:64], in0=set_[:, 0:64],
                                                  scalar1=-1, scalar2=248, op0=Alu.mult, op1=Alu.add))
        E("vector", lambda api: api.tensor_scalar(out=invt[:, 0:64], in0=invt[:, 0:64],
                                                  scalar1=23, scalar2=None, op0=Alu.logical_shift_left))
        E("vector", lambda api: api.tensor_scalar(out=sct[:, 0:64], in0=set_[:, 0:64],
                                                  scalar1=6, scalar2=None, op0=Alu.add))
        E("vector", lambda api: api.tensor_scalar(out=sct[:, 0:64], in0=sct[:, 0:64],
                                                  scalar1=23, scalar2=None, op0=Alu.logical_shift_left), inc=sV)
        W("vector", sV)
        E("vector", lambda api, Y=Y, Y6=Y6, nblk=nblk: api.tensor_tensor(
            out=r32(Y6, nblk), in0=r32(Y, nblk), in1=bc32(invt, nblk), op=Alu.mult))
        M = 7.5 * 2.0 ** -6
        E("vector", lambda api, Y6=Y6: api.tensor_scalar(out=Y6, in0=Y6, scalar1=M, scalar2=-M,
                                                         op0=Alu.min, op1=Alu.max), inc=sV)
        W("scalar", sV)
        E("scalar", lambda api, Y6=Y6, width=width: api.copy(out=q8t[:, 0:width], in_=Y6), inc=sS)
        W("vector", sS)
        E("vector", lambda api, Y=Y, width=width, nblk=nblk: api.tensor_tensor(
            out=r32(Y, nblk), in0=r32(q8t[:, 0:width], nblk),
            in1=bc32(sct, nblk), op=Alu.mult), inc=sV)
        pswait_t("C")
        W("tensor", sV)
        for j in range(NCH):
            E("tensor", lambda api, j=j, Y=Y: api.transpose(
                psC[:, j * 128:(j + 1) * 128],
                Y[:, j * 128:(j + 1) * 128], identf), inc=sT)
        W("scalar", sT)
        E("scalar", lambda api, width=width, out_bf_ap=out_bf_ap: api.copy(
            out=out_bf_ap, in_=psC[:, 0:width]), inc=sS)
        psmark["C"] = (sS, sS.n)
        W("vector", sS)
        kb.inc("vector", sV)

    def ffn(col0, width):
        nch = chunks_of(width)

        def consume(t, xn_ap, w_):
            quant_act_tile(xn_ap, w_, xqbf[:, t * TW:t * TW + w_])
        ln_and(width, col0, consume)
        xq_done = sV.n
        if c.debug and col0 == 0:
            W("sync", sV, xq_done)
            dma("sync", d_dbgxq, xqbf[:, 0:HT * TW], dU)
        for q in range(c.NQ):
            for oj in range(c.QT):
                og = q * c.QT + oj
                rank = og // RPC
                olocal = og % RPC
                gbase = rank * PACK + base_gate + (olocal * HT) * TB
                ubase = rank * PACK + base_up + (olocal * HT) * TB
                buf = (og % 2) * 4096
                wslab(buf, 0, HT, gbase)
                wslab(buf, HT * 128, HT, ubase, wait_mark=False)
                W("tensor", dW)
                W("tensor", sV, xq_done)
                pswait_t("A")
                pswait_t("B")
                for k in range(HT):
                    for (cc0, cn) in nch:
                        E("tensor", lambda api, k=k, cc0=cc0, cn=cn, buf=buf: api.matmul(
                            psA[:, cc0:cc0 + cn], wstream[:, buf + k * 128:buf + (k + 1) * 128],
                            xqbf[:, k * TW + cc0:k * TW + cc0 + cn],
                            start=(k == 0), stop=(k == HT - 1)), inc=sT)
                    for (cc0, cn) in nch:
                        E("tensor", lambda api, k=k, cc0=cc0, cn=cn, buf=buf: api.matmul(
                            psB[:, cc0:cc0 + cn], wstream[:, buf + (HT + k) * 128:buf + (HT + k + 1) * 128],
                            xqbf[:, k * TW + cc0:k * TW + cc0 + cn],
                            start=(k == 0), stop=(k == HT - 1)), inc=sT)
                wsmark[buf] = sT.n
                W("scalar", sT)
                E("scalar", lambda api, width=width: api.activation(
                    out=evst[:, 0:width], in_=psA[:, 0:width], func=Act.Silu), inc=sS)
                W("vector", sS)
                W("vector", sT, getattr(quant_act_tile, "qin_mark", 0))
                E("vector", lambda api, width=width: api.tensor_tensor(
                    out=ibuf[:, 0:width], in0=evst[:, 0:width], in1=psB[:, 0:width],
                    op=Alu.mult), inc=sV)
                psmark["A"] = (sV, sV.n)
                psmark["B"] = (sV, sV.n)
                quant_act_tile(ibuf[:, 0:width], width, iqbf[:, oj * TW:oj * TW + width])
            iq_done = sV.n
            if c.debug and col0 == 0 and q == 0:
                W("sync", sV, iq_done)
                dma("sync", d_dbgiq, iqbf[:, 0:c.QT * TW], dU)
            for m in range(HT):
                rank = m // HPC
                mlocal = m % HPC
                dbase = rank * PACK + base_down + (mlocal * FT + q * c.QT) * TB
                buf = (m % 2) * 4096
                wslab(buf, 0, c.QT, dbase)
                W("tensor", dW)
                W("tensor", sV, iq_done)
                ppname = "A" if (m % 2 == 0) else "B"
                pp = psA if (m % 2 == 0) else psB
                pswait_t(ppname)
                for k in range(c.QT):
                    for (cc0, cn) in nch:
                        E("tensor", lambda api, k=k, cc0=cc0, cn=cn, buf=buf, pp=pp: api.matmul(
                            pp[:, cc0:cc0 + cn], wstream[:, buf + k * 128:buf + (k + 1) * 128],
                            iqbf[:, k * TW + cc0:k * TW + cc0 + cn],
                            start=(k == 0), stop=(k == c.QT - 1)), inc=sT)
                wsmark[buf] = sT.n
                W("vector", sT)
                xa = xres[:, m * TW + col0: m * TW + col0 + width]
                E("vector", lambda api, pp=pp, xa=xa, width=width: api.scalar_tensor_tensor(
                    out=xa, in0=pp[:, 0:width], scalar=0.5, in1=xa, op0=Alu.mult, op1=Alu.add), inc=sV)
                psmark[ppname] = (sV, sV.n)

    W("tensor", sCC)
    W("sync", sCC)
    ffn(0, TW)

    def consume_ln2(t, xn_ap, w_):
        W("scalar", sV)
        E("scalar", lambda api, t=t, xn_ap=xn_ap, w_=w_: api.copy(
            out=xqbf[:, t * TW:t * TW + w_], in_=xn_ap), inc=sS)
        W("vector", sS)
    ln_and(TW, 0, consume_ln2)
    xn2_done = sS.n
    if c.debug:
        W("sync", sS)
        dma("sync", d_dbgxn2, xqbf[:, 0:HT * TW], dU)
        for rr in range(8):
            off = rr * PACK + base_win
            dma("sync", d_dbgwin[rr],
                wag[off:off + TB].rearrange("(r q) -> r q", r=128), dU)
    W("sync", sV)
    W("sync", sS)
    dma("sync", d_x1.rearrange("(t p) w -> p t w", p=P),
        xres.rearrange("p (t w) -> p t w", w=TW), dU)
    x1_spilled = dU.n

    E("vector", lambda api: api.memset(ssm_dt, 0.0))
    E("vector", lambda api: api.memset(ssm_bc, 0.0), inc=sV)

    nchW = chunks_of(TW)
    nchC = chunks_of(TC)
    ip_xp_mark = {}
    ip_spill_mark = {}
    for ot in range(2 * DT):
        buf = (ot % 2) * 4096
        for r in range(c.NC):
            wb_ = r * PACK + base_win + (ot * HPC) * TB
            wslab(buf, r * HPC * 128, HPC, wb_, wait_mark=(r == 0))
        W("tensor", dW)
        W("tensor", sS, xn2_done)
        ppname = "A" if (ot % 2 == 0) else "B"
        pp = psA if (ot % 2 == 0) else psB
        pswait_t(ppname)
        for k in range(HT):
            for (cc0, cn) in nchW:
                E("tensor", lambda api, k=k, cc0=cc0, cn=cn, buf=buf, pp=pp: api.matmul(
                    pp[:, cc0:cc0 + cn], wstream[:, buf + k * 128:buf + (k + 1) * 128],
                    xqbf[:, k * TW + cc0:k * TW + cc0 + cn],
                    start=(k == 0), stop=(k == HT - 1)), inc=sT)
        wsmark[buf] = sT.n
        if ot < DT:
            dtile = ot
            W("scalar", sT)
            if ot >= 2:
                W("scalar", dU, ip_spill_mark.get(ot - 2, 0))
            xcb = SP_U[ot % 2]
            E("scalar", lambda api, pp=pp, xcb=xcb: api.copy(out=fview(xcb, TC), in_=pp[:, 0:TC]), inc=sS)
            psmark[ppname] = (sS, sS.n)
            E("scalar", lambda api, xcb=xcb, dtile=dtile: api.copy(
                out=xc3_sb[:, dtile * 3:dtile * 3 + 3], in_=fview(xcb, TC)[:, TC - 3:TC]), inc=sS)
            W("sync", sS)
            dma("sync", d_scr_xc[dtile], fview(xcb, TC), dU)
            ip_spill_mark[ot] = dU.n
            if c.debug:
                dma("sync", d_dbg3[dtile], fview(xcb, TC), dU)
        else:
            dtile = ot - DT
            W("scalar", sT)
            if ot >= 2:
                W("scalar", dU, ip_spill_mark.get(ot - 2, 0))
            sgb = SP_SG[ot % 2]
            E("scalar", lambda api, pp=pp, sgb=sgb: api.activation(
                out=fview(sgb, TC), in_=pp[:, HALO:HALO + TC], func=Act.Silu), inc=sS)
            psmark[ppname] = (sS, sS.n)
            W("sync", sS)
            dma("sync", d_scr_sg[dtile], fview(sgb, TC), dU)
            ip_spill_mark[ot] = dU.n

    W("sync", sS)
    dma("sync", d_cc3in, xc3_sb, dU)
    W("gpsimd", dU)
    E("gpsimd", lambda api: api.collective_compute(
        "AllGather", Alu.bypass, replica_groups=[list(range(c.NC))],
        ins=[d_cc3in_h.ap().rearrange("p q -> (p q)")],
        outs=[d_cc3out_h.ap().rearrange("n p q -> (n p q)")]), inc=sCC)
    cc3_done = sCC.n
    W("vector", sCC, cc3_done)
    E("vector", lambda api: api.memset(b3_sb, 0.0), inc=sV)
    for j in range(c.NC):
        W("sync", sV)
        dma("sync", gb[:, 0:DT * 3], d_cc3out[j], dR)
        W("vector", dR)
        E("vector", lambda api, j=j: api.tensor_scalar(
            out=gb[:, 0:DT * 3], in0=gb[:, 0:DT * 3],
            scalar1=cmask_sb[:, j:j + 1], scalar2=None, op0=Alu.mult))
        E("vector", lambda api: api.tensor_tensor(out=b3_sb, in0=b3_sb, in1=gb[:, 0:DT * 3],
                                                  op=Alu.add), inc=sV)

    cv_spill = {}
    cv_v = {}
    for dtile in range(DT):
        head = dtile // DTH
        xcb = SP_U[dtile % 2]
        ub = SP_DT[dtile % 2]
        if dtile >= 2:
            W("sync", sV, cv_v.get(dtile - 2, 0))
            W("sync", dU, cv_spill.get(dtile - 2, 0))
        dma("sync", fview(xcb, TC), d_scr_xc[dtile], dR)
        W("vector", dR)
        E("vector", lambda api, xcb=xcb, dtile=dtile: api.tensor_scalar(
            out=gb[:, 3:TC], in0=fview(xcb, TC)[:, 0:TC - 3],
            scalar1=cw_sb[:, dtile * 4:dtile * 4 + 1], scalar2=None, op0=Alu.mult))
        for kk in range(1, 4):
            E("vector", lambda api, xcb=xcb, dtile=dtile, kk=kk: api.scalar_tensor_tensor(
                out=gb[:, 3:TC], in0=fview(xcb, TC)[:, kk:kk + TC - 3],
                scalar=cw_sb[:, dtile * 4 + kk:dtile * 4 + kk + 1], in1=gb[:, 3:TC],
                op0=Alu.mult, op1=Alu.add))
        E("vector", lambda api, dtile=dtile: api.tensor_copy(out=ext6[:, 0:3],
                                                             in_=b3_sb[:, dtile * 3:dtile * 3 + 3]))
        E("vector", lambda api, xcb=xcb: api.tensor_copy(out=ext6[:, 3:6],
                                                         in_=fview(xcb, TC)[:, 0:3]), inc=sV)
        W("vector", sV)
        E("vector", lambda api, dtile=dtile: api.tensor_scalar(
            out=gb[:, 0:3], in0=ext6[:, 0:3],
            scalar1=cw_sb[:, dtile * 4:dtile * 4 + 1], scalar2=None, op0=Alu.mult))
        for kk in range(1, 4):
            E("vector", lambda api, dtile=dtile, kk=kk: api.scalar_tensor_tensor(
                out=gb[:, 0:3], in0=ext6[:, kk:kk + 3],
                scalar=cw_sb[:, dtile * 4 + kk:dtile * 4 + kk + 1], in1=gb[:, 0:3],
                op0=Alu.mult, op1=Alu.add), inc=(sV if kk == 3 else None))
        cv_v[dtile] = sV.n
        W("scalar", sV)
        E("scalar", lambda api, ub=ub, dtile=dtile: api.activation(
            out=fview(ub, TC), in_=gb, func=Act.Silu,
            bias=cb_sb[:, dtile:dtile + 1]), inc=sS)
        E("scalar", lambda api, ub=ub, uu=ubft[dtile % 2]: api.copy(out=uu, in_=fview(ub, TC)), inc=sS)
        W("sync", sS)
        dma("sync", d_scr_u[dtile], fview(ub, TC), dU)
        cv_spill[dtile] = dU.n
        W("tensor", sS)
        if dtile >= 2:
            W("tensor", sS)
        wxoff = (head * DTH + (dtile % DTH)) * (RK + 2 * NS)
        pswait_t("C")
        for (cc0, cn) in nchC:
            E("tensor", lambda api, cc0=cc0, cn=cn, wxoff=wxoff, uu=ubft[dtile % 2]: api.matmul(
                psC[:, cc0:cc0 + cn], wxbf[:, wxoff:wxoff + RK],
                uu[:, cc0:cc0 + cn], start=True, stop=True), inc=sT)
        xp_t1 = sT.n
        W("vector", sT, xp_t1)
        E("vector", lambda api, head=head: api.tensor_tensor(
            out=ssm_dt[:, head * TC:(head + 1) * TC], in0=ssm_dt[:, head * TC:(head + 1) * TC],
            in1=psC[:, 0:TC], op=Alu.add), inc=sV)
        psmark["C"] = (sV, sV.n)
        pswait_t("C")
        for (cc0, cn) in nchC:
            E("tensor", lambda api, cc0=cc0, cn=cn, wxoff=wxoff, uu=ubft[dtile % 2]: api.matmul(
                psC[0:32, cc0:cc0 + cn], wxbf[:, wxoff + RK:wxoff + RK + 2 * NS],
                uu[:, cc0:cc0 + cn], start=True, stop=True), inc=sT)
        W("vector", sT)
        E("vector", lambda api, head=head: api.tensor_tensor(
            out=ssm_bc[:, head * TC:(head + 1) * TC], in0=ssm_bc[:, head * TC:(head + 1) * TC],
            in1=psC[0:32, 0:TC], op=Alu.add), inc=sV)
        psmark["C"] = (sV, sV.n)
        cv_v[dtile] = sV.n

    W("scalar", sV)
    E("scalar", lambda api: api.copy(out=dtin_bf, in_=ssm_dt), inc=sS)
    E("scalar", lambda api: api.copy(out=bcrow_bf, in_=ssm_bc), inc=sS)
    dtin_mark = sS.n
    W("sync", sS)
    dma("sync", d_scr_bc.rearrange("(h r) w -> r h w", h=NH),
        bcrow_bf.rearrange("p (h w) -> p h w", h=NH), dU)
    bc_spill_mark = dU.n

    ph1_v_mark = {}
    ph1_spill_mark = {}
    for head in range(NH):
        W("sync", dU, bc_spill_mark)
        W("sync", sT)
        W("sync", sV)
        if head == 0:
            W("sync", dU, x1_spilled)
        for n in range(NS):
            dma("sync", brep[:, n * TC:(n + 1) * TC],
                d_scr_bc[head * 2 * NS + n:head * 2 * NS + n + 1, :].broadcast_to([P, TC]), dR)
        brep_done = dR.n
        W("gpsimd", sT, wsmark[0])
        dma("gpsimd", wstream[:, 0:DTH * 128].rearrange("p (t q) -> p t q", q=128),
            d_wdt.rearrange("h p (t q) -> p h t q", q=128)[:, head], dWG)
        wdt_here = dWG.n
        W("tensor", dWG, wdt_here)
        for dl in range(DTH):
            dtile = head * DTH + dl
            W("tensor", sS, dtin_mark)
            pswait_t("C")
            for (cc0, cn) in nchC:
                E("tensor", lambda api, cc0=cc0, cn=cn, head=head, dl=dl: api.matmul(
                    psC[:, cc0:cc0 + cn], wstream[:, dl * 128:(dl + 1) * 128],
                    dtin_bf[:, head * TC + cc0:head * TC + cc0 + cn], start=True, stop=True), inc=sT)
            dtp_mark = sT.n
            W("scalar", sT, dtp_mark)
            dtb0 = SP_DT[dtile % 2]
            if dtile >= 2:
                W("scalar", sV, ph1_v_mark[dtile - 2])
                W("vector", dU, ph1_spill_mark.get(dtile - 2, 0))
            E("scalar", lambda api, dtile=dtile: api.activation(
                out=scrF[:, 0:TC], in_=psC[:, 0:TC], func=Act.Identity,
                bias=dtb_sb[:, dtile:dtile + 1]))
            E("scalar", lambda api: api.activation(out=scrF[:, TW:TW + TC], in_=scrF[:, 0:TC], func=Act.Abs))
            E("scalar", lambda api: api.activation(out=scrF[:, TW:TW + TC], in_=scrF[:, TW:TW + TC],
                                                   func=Act.Exp, scale=-1.0))
            E("scalar", lambda api: api.activation(out=scrF[:, TW:TW + TC], in_=scrF[:, TW:TW + TC],
                                                   func=Act.Ln, bias=1.0), inc=sS)
            W("vector", sS)
            E("vector", lambda api: api.tensor_scalar(out=evst, in0=scrF[:, 0:TC],
                                                      scalar1=0.0, scalar2=None, op0=Alu.max))
            E("vector", lambda api, dtb0=dtb0: api.tensor_tensor(
                out=fview(dtb0, TC), in0=scrF[:, TW:TW + TC], in1=evst, op=Alu.add), inc=sV)
            psmark["C"] = (sV, sV.n)
            dt_ready = sV.n
            W("sync", sV, dt_ready)
            dma("sync", d_scr_dt[dtile], fview(dtb0, TC), dU)
            ph1_spill_mark[dtile] = dU.n
            ub = SP_U[dtile % 2]
            if dtile >= 2:
                W("sync", sV, ph1_v_mark[dtile - 2])
            dma("sync", fview(ub, TC), d_scr_u[dtile], dR)
            u_here = dR.n
            E("vector", lambda api, dtb0=dtb0: api.tensor_tensor_scan(
                out=cdtb, data0=fview(dtb0, TC), data1=zcol.broadcast_to([P, TC]),
                initial=0.0, op0=Alu.add, op1=Alu.add), inc=sV)
            W("vector", sV)
            E("vector", lambda api: api.tensor_scalar(out=nrow[:, 0:1], in0=cdtb[:, TC - 1:TC],
                                                      scalar1=-1.0, scalar2=None, op0=Alu.mult), inc=sV)
            W("vector", sV)
            E("vector", lambda api: api.tensor_scalar(out=argb, in0=cdtb,
                                                      scalar1=nrow[:, 0:1], scalar2=None, op0=Alu.add))
            W("vector", dR, u_here)
            E("vector", lambda api, dtb0=dtb0, ub=ub: api.tensor_tensor(
                out=gb, in0=fview(dtb0, TC), in1=fview(ub, TC), op=Alu.mult))
            E("vector", lambda api, dtile=dtile: api.tensor_scalar(
                out=dec16, in0=negA[:, dtile * NS:(dtile + 1) * NS],
                scalar1=cdtb[:, TC - 1:TC], scalar2=None, op0=Alu.mult), inc=sV)
            gv = sV.n
            W("scalar", sV, gv)
            E("scalar", lambda api, dtile=dtile: api.activation(
                out=Decb[:, dtile * NS:(dtile + 1) * NS], in_=dec16, func=Act.Exp), inc=sS)
            ph1_g = {}
            ph1_vn = {}
            for n in range(NS):
                fb0 = (n % 2) * TC
                if n >= 2:
                    W("scalar", sG, ph1_g[n - 2])
                E("scalar", lambda api, dtile=dtile, n=n, fb0=fb0: api.activation(
                    out=Fb[:, fb0:fb0 + TC], in_=argb, func=Act.Exp,
                    scale=posA[:, dtile * NS + n:dtile * NS + n + 1]), inc=sS)
                W("gpsimd", sS)
                if n >= 2:
                    W("gpsimd", sV, ph1_vn[n - 2])
                E("gpsimd", lambda api, fb0=fb0: api.tensor_tensor(
                    out=Wb[:, fb0:fb0 + TC], in0=Fb[:, fb0:fb0 + TC], in1=gb, op=Alu.mult), inc=sG)
                ph1_g[n] = sG.n
                W("vector", sG, ph1_g[n])
                if dtile % DTH == 0 and n == 0:
                    W("vector", dR, brep_done)
                E("vector", lambda api, fb0=fb0, n=n, dtile=dtile: api.scalar_tensor_tensor(
                    out=dumpb, in0=Wb[:, fb0:fb0 + TC], scalar=1.0,
                    in1=brep[:, n * TC:(n + 1) * TC],
                    op0=Alu.mult, op1=Alu.mult,
                    accum_out=Scol[:, dtile * NS + n:dtile * NS + n + 1]), inc=sV)
                ph1_vn[n] = sV.n
            ph1_v_mark[dtile] = sV.n
        wsmark[0] = sT.n

    W("sync", sV)
    W("sync", sS)
    dma("sync", d_cc2in[0], Scol, dU)
    dma("sync", d_cc2in[1], Decb, dU)
    if c.debug:
        dma("sync", d_dbgcc[0], Scol, dU)
        dma("sync", d_dbgcc[1], Decb, dU)
    W("gpsimd", dU)
    E("gpsimd", lambda api: api.collective_compute(
        "AllGather", Alu.bypass, replica_groups=[list(range(c.NC))],
        ins=[d_cc2in_h.ap().rearrange("a p q -> (a p q)")],
        outs=[d_cc2out_h.ap().rearrange("n a p q -> (n a p q)")]), inc=sCC)
    W("sync", sCC)
    E("vector", lambda api: api.memset(sInb, 0.0), inc=sV)
    DN = DT * NS
    for j in range(c.NC - 1):
        W("sync", sV)
        dma("sync", agb[:, 0:DN], d_cc2out[j, 0], dR)
        dma("sync", agb[:, DN:2 * DN], d_cc2out[j, 1], dR)
        W("vector", dR)
        E("vector", lambda api: api.tensor_scalar(
            out=agb[:, DN:2 * DN], in0=agb[:, DN:2 * DN], scalar1=1.0, scalar2=None, op0=Alu.subtract))
        E("vector", lambda api, j=j: api.tensor_scalar(
            out=agb[:, DN:2 * DN], in0=agb[:, DN:2 * DN],
            scalar1=pmask_sb[:, j:j + 1], scalar2=1.0, op0=Alu.mult, op1=Alu.add))
        E("vector", lambda api: api.tensor_tensor(out=sInb, in0=sInb, in1=agb[:, DN:2 * DN], op=Alu.mult))
        E("vector", lambda api, j=j: api.tensor_scalar(
            out=agb[:, 0:DN], in0=agb[:, 0:DN], scalar1=pmask_sb[:, j:j + 1], scalar2=None, op0=Alu.mult))
        E("vector", lambda api: api.tensor_tensor(out=sInb, in0=sInb, in1=agb[:, 0:DN], op=Alu.add), inc=sV)

    if c.debug:
        W("sync", sV)
        dma("sync", d_dbgcc[2], sInb, dU)
    ph2_v_mark = {}
    ph2_ygfree = [0]
    for head in range(NH):
        W("sync", sV)
        W("sync", sT)
        for n in range(NS):
            dma("sync", brep[:, n * TC:(n + 1) * TC],
                d_scr_bc[head * 2 * NS + n:head * 2 * NS + n + 1, :].broadcast_to([P, TC]), dR)
            dma("sync", crep[:, n * TC:(n + 1) * TC],
                d_scr_bc[head * 2 * NS + NS + n:head * 2 * NS + NS + n + 1, :].broadcast_to([P, TC]), dR)
        rep_done = dR.n
        for dl in range(DTH):
            dtile = head * DTH + dl
            kg = dtile % KGRP
            b2 = dtile % 2
            dtb0 = SP_DT[b2]
            ub = SP_U[b2]
            sgb = SP_SG[b2]
            if dtile >= 2:
                W("sync", sV, ph2_v_mark[dtile - 2])
            dma("sync", fview(dtb0, TC), d_scr_dt[dtile], dR)
            dma("sync", fview(ub, TC), d_scr_u[dtile], dR)
            dma("sync", fview(sgb, TC), d_scr_sg[dtile], dR)
            W("vector", dR)
            if dtile == 0:
                W("vector", dR, rep_done)
            E("vector", lambda api, dtb0=dtb0, ub=ub: api.tensor_tensor(
                out=gb, in0=fview(dtb0, TC), in1=fview(ub, TC), op=Alu.mult))
            E("vector", lambda api: api.memset(yb, 0.0), inc=sV)
            gmark = sV.n
            W("scalar", sV, gmark)
            ph2_sn = {}
            ph2_vn = {}
            ph2_gn = {}
            for n in range(NS):
                fb0 = (n % 2) * TC
                if n >= 2:
                    W("scalar", sV, ph2_vn[n - 2])
                E("scalar", lambda api, dtile=dtile, n=n, fb0=fb0, dtb0=dtb0: api.activation(
                    out=dAb[:, fb0:fb0 + TC], in_=fview(dtb0, TC), func=Act.Exp,
                    scale=negA[:, dtile * NS + n:dtile * NS + n + 1]), inc=sS)
                ph2_sn[n] = sS.n
                W("vector", sS, ph2_sn[n])
                E("vector", lambda api, fb0=fb0, n=n: api.tensor_tensor(
                    out=gBb[:, fb0:fb0 + TC], in0=gb,
                    in1=brep[:, n * TC:(n + 1) * TC], op=Alu.mult))
                E("vector", lambda api, fb0=fb0, n=n, dtile=dtile: api.tensor_tensor_scan(
                    out=stb, data0=dAb[:, fb0:fb0 + TC], data1=gBb[:, fb0:fb0 + TC],
                    initial=sInb[:, dtile * NS + n:dtile * NS + n + 1],
                    op0=Alu.mult, op1=Alu.add))
                if n >= 2:
                    W("vector", sG, ph2_gn[n - 2])
                E("vector", lambda api, fb0=fb0, n=n: api.tensor_tensor(
                    out=csb[:, fb0:fb0 + TC], in0=stb,
                    in1=crep[:, n * TC:(n + 1) * TC], op=Alu.mult), inc=sV)
                ph2_vn[n] = sV.n
                W("gpsimd", sV, ph2_vn[n])
                E("gpsimd", lambda api, fb0=fb0: api.tensor_tensor(
                    out=yb, in0=yb, in1=csb[:, fb0:fb0 + TC], op=Alu.add), inc=sG)
                ph2_gn[n] = sG.n
            W("vector", sG, ph2_gn[NS - 1])
            E("vector", lambda api, ub=ub, dtile=dtile: api.scalar_tensor_tensor(
                out=yb, in0=fview(ub, TC), scalar=Dp_sb[:, dtile:dtile + 1],
                in1=yb, op0=Alu.mult, op1=Alu.add))
            W("vector", sT, ph2_ygfree[0])
            E("vector", lambda api, sgb=sgb, kg=kg: api.tensor_tensor(
                out=ygbuf[:, kg * TC:(kg + 1) * TC], in0=yb, in1=fview(sgb, TC),
                op=Alu.mult), inc=sV)
            ph2_v_mark[dtile] = sV.n
            if kg == KGRP - 1:
                g = dtile // KGRP
                yg_done = sV.n
                ev_mark = [0]
                for m in range(HT):
                    buf = (m % 2) * 4096
                    nkr = max(1, KGRP // OPC)
                    for half in range(nkr):
                        r = g * nkr + half
                        wb_ = r * PACK + base_wout + (m * OPC) * TB
                        wslab(buf, half * OPC * 128, OPC, wb_, wait_mark=(half == 0))
                    W("tensor", dW)
                    W("tensor", sV, yg_done)
                    ppname = "A" if (m % 2 == 0) else "B"
                    pp = psA if (m % 2 == 0) else psB
                    pswait_t(ppname)
                    for k in range(KGRP):
                        for (cc0, cn) in nchC:
                            E("tensor", lambda api, k=k, cc0=cc0, cn=cn, buf=buf, pp=pp: api.matmul(
                                pp[:, cc0:cc0 + cn], wstream[:, buf + k * 128:buf + (k + 1) * 128],
                                ygbuf[:, k * TC + cc0:k * TC + cc0 + cn],
                                start=(k == 0), stop=(k == KGRP - 1)), inc=sT)
                    wsmark[buf] = sT.n
                    W("scalar", sT)
                    W("scalar", dU, ev_mark[0])
                    E("scalar", lambda api, pp=pp: api.copy(out=evst, in_=pp[:, 0:TC]), inc=sS)
                    psmark[ppname] = (sS, sS.n)
                    W("sync", sS)
                    dma("sync", d_macc[g, m * 128:(m + 1) * 128, :], evst, dU)
                    ev_mark[0] = dU.n
                ph2_ygfree[0] = sT.n

    W("sync", sV)
    W("sync", sS)
    for t in range(HT):
        dma("sync", xres[:, t * TW:(t + 1) * TW],
            d_x1.rearrange("(t p) w -> p t w", p=P)[:, t], dR)
    x1_back = dR.n
    mslot = [fview(O_R3, TC), fview(O_R3 + TC, TC)]
    for t in range(HT):
        for g in range(NGR):
            W("sync", sV)
            dma("sync", mslot[g % 2], d_macc[g, t * 128:(t + 1) * 128, :], dR)
            W("vector", dR)
            if t == 0 and g == 0:
                W("vector", dR, x1_back)
            xa = xres[:, t * TW + HALO:(t + 1) * TW]
            E("vector", lambda api, xa=xa, g=g: api.tensor_tensor(
                out=xa, in0=xa, in1=mslot[g % 2], op=Alu.add), inc=sV)

    ffn(HALO, TC)

    W("sync", sV)
    dma("sync", d_out.rearrange("(t p) w -> p t w", p=P),
        xres.rearrange("p (t w) -> p t w", w=TW)[:, :, HALO:TW], dO)
    W("sync", dO)

    with nc.Block() as blk:
        @blk.sync
        def _(api):
            for thk in kb.streams["sync"]:
                thk(api)

        @blk.tensor
        def _(api):
            for thk in kb.streams["tensor"]:
                thk(api)

        @blk.vector
        def _(api):
            for thk in kb.streams["vector"]:
                thk(api)

        @blk.scalar
        def _(api):
            for thk in kb.streams["scalar"]:
                thk(api)

        @blk.gpsimd
        def _(api):
            for thk in kb.streams["gpsimd"]:
                thk(api)
    st.close()
    return nc


def prepare_inputs(inputs, cfg):
    c = cfg

    def get(k):
        return np.asarray(inputs[k], np.float32)
    x = get("L0_attn_hidden_states")[0]
    x_fm = np.ascontiguousarray(x.T)
    w_in = get("L0_attn_in_proj_w")
    w_in_perm = np.concatenate([w_in[0::2], w_in[1::2]], 0)
    W_inT = np.ascontiguousarray(w_in_perm.T)
    WoutT = np.ascontiguousarray(get("L0_attn_out_proj_w").T)
    WxT = np.ascontiguousarray(get("L0_attn_x_proj_weight").transpose(0, 2, 1))
    WdtT = np.ascontiguousarray(get("L0_attn_dt_proj_weight").transpose(0, 2, 1))
    cw = np.ascontiguousarray(get("L0_attn_conv_w").reshape(c.DT, 128, 4))
    cb = np.ascontiguousarray(get("L0_attn_conv_b").reshape(c.DT, 128).T)
    dtb = np.ascontiguousarray(get("L0_attn_dt_proj_bias").reshape(c.DT, 128).T)
    Dp = np.ascontiguousarray(get("L0_attn_D").reshape(c.DT, 128).T)
    alog = np.ascontiguousarray(get("L0_attn_A_log").reshape(c.DT, 128, c.NS))
    wgate = get("L0_ffn_w_gate")
    wup = get("L0_ffn_w_up")
    wdown = get("L0_ffn_w_down")
    import ml_dtypes
    iden = np.eye(128, dtype=ml_dtypes.bfloat16)
    in_maps = []
    rg = c.IFF // c.NC
    rh = c.H // c.NC
    ro = c.IP // c.NC
    for cid in range(c.NC):
        t0 = cid * c.TC
        x_sh = np.ascontiguousarray(x_fm[:, t0:t0 + c.TC])
        pm = np.zeros((128, c.NC), np.float32)
        pm[:, :cid] = 1.0
        cm = np.zeros((128, c.NC), np.float32)
        if cid > 0:
            cm[:, cid - 1] = 1.0
        in_maps.append(dict(
            x_in=x_sh,
            wg_sl=np.ascontiguousarray(wgate[cid * rg:(cid + 1) * rg]),
            wu_sl=np.ascontiguousarray(wup[cid * rg:(cid + 1) * rg]),
            wd_sl=np.ascontiguousarray(wdown[cid * rh:(cid + 1) * rh]),
            wi_sl=np.ascontiguousarray(W_inT[cid * rh:(cid + 1) * rh]),
            wo_sl=np.ascontiguousarray(WoutT[cid * ro:(cid + 1) * ro]),
            wx=WxT, wdt=WdtT, convw=cw, convb=cb, dtb=dtb, Dp=Dp, alog=alog,
            iden=iden, idenf=np.eye(128, dtype=np.float32), pmask=pm, cmask=cm,
        ))
    return in_maps


_CFG = CFG(mini=False)


def kernel(**inputs):
    cfg = _CFG
    in_maps = prepare_inputs(inputs, cfg)
    nc = build_program(cfg)
    res = run_bass_kernel_spmd(nc, in_maps, core_ids=list(range(cfg.NC)))
    outs = [res.results[i]["out_fm"] for i in range(cfg.NC)]
    out_fm = np.concatenate(outs, axis=1)
    return np.ascontiguousarray(out_fm.T)[None]


# version 06
# speedup vs baseline: 1.0703x; 1.0471x over previous
import numpy as np
import contextlib
import concourse.bass as bass
import concourse.mybir as mybir
from concourse.bass_utils import run_bass_kernel_spmd

F32 = mybir.dt.float32
BF16 = mybir.dt.bfloat16
I32 = mybir.dt.int32
FP8 = mybir.dt.float8e4
Alu = mybir.AluOpType
Act = mybir.ActivationFunctionType
AxX = mybir.AxisListType.X


class CFG:
    def __init__(self, mini=False, debug=False):
        self.NC = 8
        self.debug = debug
        if mini:
            self.H = 1024
            self.IFF = 1024
            self.TC = 128
            self.HD = 512
        else:
            self.H = 2048
            self.IFF = 8192
            self.TC = 1024
            self.HD = 2048
        self.NH = 2
        self.IP = self.NH * self.HD
        self.NS = 16
        self.RK = 128
        self.HALO = 0
        self.TW = self.TC
        self.HT = self.H // 128
        self.FT = self.IFF // 128
        self.DT = self.IP // 128
        self.DTH = self.HD // 128
        self.NQ = 4
        self.QT = self.FT // self.NQ
        self.KGRP = min(8, self.DTH)


def chunks_of(total, step=512):
    out = []
    cpos = 0
    while cpos < total:
        out.append((cpos, min(step, total - cpos)))
        cpos += step
    return out


class Sem:
    def __init__(self, h, name):
        self.h = h
        self.n = 0
        self.name = name


class KB:
    ENGINES = ("sync", "tensor", "vector", "scalar", "gpsimd")

    def __init__(self, nc):
        self.nc = nc
        self.streams = {e: [] for e in self.ENGINES}

    def emit(self, eng, fn, inc=None, inck=1):
        if inc is not None:
            inc.n += inck

        def th(api, fn=fn, inc=inc, inck=inck):
            r = fn(api)
            if inc is not None:
                r.then_inc(inc.h, inck)
        self.streams[eng].append(th)

    def wait(self, eng, sem, val=None):
        v = sem.n if val is None else val
        if v is None or v <= 0:
            return
        assert v <= sem.n, f"wait {sem.name}>={v} but only {sem.n} planned"
        self.streams[eng].append(lambda api, s=sem, v=v: api.wait_ge(s.h, v))

    def inc(self, eng, sem, k=1):
        sem.n += k
        self.streams[eng].append(lambda api, s=sem, k=k: api.sem_inc(s.h, k))


def build_program(cfg):
    c = cfg
    nc = bass.Bass("TRN2", target_bir_lowering=False, debug=False,
                   detect_race_conditions=False)
    P = 128
    TW, TC, HALO = c.TW, c.TC, c.HALO
    HT, FT, DT, DTH = c.HT, c.FT, c.DT, c.DTH
    NS, RK, NH = c.NS, c.RK, c.NH
    KGRP = c.KGRP
    NGR = DT // KGRP

    d_x = nc.dram_tensor("x_in", [c.H, TW], F32, kind="ExternalInput").ap()
    d_out = nc.dram_tensor("out_fm", [c.H, TC], F32, kind="ExternalOutput").ap()
    d_wg = nc.dram_tensor("wg_sl", [c.IFF // c.NC, c.H], F32, kind="ExternalInput").ap()
    d_wu = nc.dram_tensor("wu_sl", [c.IFF // c.NC, c.H], F32, kind="ExternalInput").ap()
    d_wd = nc.dram_tensor("wd_sl", [c.H // c.NC, c.IFF], F32, kind="ExternalInput").ap()
    d_wi = nc.dram_tensor("wi_sl", [c.H // c.NC, 2 * c.IP], F32, kind="ExternalInput").ap()
    d_wo = nc.dram_tensor("wo_sl", [c.IP // c.NC, c.H], F32, kind="ExternalInput").ap()
    d_wx = nc.dram_tensor("wx", [NH, c.HD, 2 * NS + RK], F32, kind="ExternalInput").ap()
    d_wdt = nc.dram_tensor("wdt", [NH, RK, c.HD], F32, kind="ExternalInput").ap()
    d_cw = nc.dram_tensor("convw", [DT, 128, 4], F32, kind="ExternalInput").ap()
    d_cb = nc.dram_tensor("convb", [128, DT], F32, kind="ExternalInput").ap()
    d_dtb = nc.dram_tensor("dtb", [128, DT], F32, kind="ExternalInput").ap()
    d_Dp = nc.dram_tensor("Dp", [128, DT], F32, kind="ExternalInput").ap()
    d_alog = nc.dram_tensor("alog", [DT, 128, NS], F32, kind="ExternalInput").ap()
    d_iden = nc.dram_tensor("iden", [128, 128], BF16, kind="ExternalInput").ap()
    d_idenf = nc.dram_tensor("idenf", [128, 128], F32, kind="ExternalInput").ap()
    d_pmask = nc.dram_tensor("pmask", [128, c.NC], F32, kind="ExternalInput").ap()
    d_cmask = nc.dram_tensor("cmask", [128, c.NC], F32, kind="ExternalInput").ap()

    TB = 128 * 128
    RPC = c.IFF // c.NC // 128
    HPC = c.H // c.NC // 128
    OPC = c.IP // c.NC // 128
    n_gate_blk = RPC * HT
    n_down_blk = HPC * FT
    n_win_blk = HPC * (2 * c.IP // 128)
    n_wout_blk = OPC * HT
    base_gate = 0
    base_up = base_gate + n_gate_blk * TB
    base_down = base_up + n_gate_blk * TB
    base_win = base_down + n_down_blk * TB
    base_wout = base_win + n_win_blk * TB
    PACK = base_wout + n_wout_blk * TB + 8 * TB
    DBGK = dict(kind="ExternalOutput") if c.debug else {}
    d_pack_h = nc.dram_tensor("wpack", [PACK], BF16)
    d_wag_h = nc.dram_tensor("wag", [c.NC * PACK], BF16)
    d_pack = d_pack_h.ap()
    wag = d_wag_h.ap()

    d_scr_xc_h = nc.dram_tensor("scr_xc", [DT, 128, TC], F32, **DBGK)
    d_cc3in_h = nc.dram_tensor("cc3in", [128, DT * 3], F32)
    d_cc3out_h = nc.dram_tensor("cc3out", [c.NC, 128, DT * 3], F32)
    d_scr_u_h = nc.dram_tensor("scr_u", [DT, 128, TC], F32, **DBGK)
    d_scr_dt_h = nc.dram_tensor("scr_dt", [DT, 128, TC], F32, **DBGK)
    d_scr_sg_h = nc.dram_tensor("scr_sg", [DT, 128, TC], F32, **DBGK)
    d_scr_bc_h = nc.dram_tensor("scr_bc", [NH * 2 * NS, TC], BF16, **DBGK)
    d_scr_ln_h = nc.dram_tensor("scr_ln", [2, TW], F32)
    d_x1_h = nc.dram_tensor("scr_x1", [c.H, TW], F32, **DBGK)
    d_macc_h = nc.dram_tensor("scr_macc", [NGR, c.H, TC], F32)
    d_cc2in_h = nc.dram_tensor("cc2in", [2, 128, DT * NS], F32)
    d_cc2out_h = nc.dram_tensor("cc2out", [c.NC, 2, 128, DT * NS], F32)
    d_scr_xc = d_scr_xc_h.ap(); d_cc3in = d_cc3in_h.ap(); d_cc3out = d_cc3out_h.ap()
    d_scr_u = d_scr_u_h.ap(); d_scr_dt = d_scr_dt_h.ap(); d_scr_sg = d_scr_sg_h.ap()
    d_scr_bc = d_scr_bc_h.ap(); d_scr_ln = d_scr_ln_h.ap()
    d_x1 = d_x1_h.ap(); d_macc = d_macc_h.ap()
    d_cc2in = d_cc2in_h.ap(); d_cc2out = d_cc2out_h.ap()
    d_dbg3 = d_dbgcc = None
    if c.debug:
        d_dbg3 = nc.dram_tensor("dbg_xc", [DT, 128, TC], F32, kind="ExternalOutput").ap()
        d_dbgcc = nc.dram_tensor("dbg_cc", [3, 128, DT * NS], F32, kind="ExternalOutput").ap()
        d_dbgxq = nc.dram_tensor("dbg_xq", [128, HT * TW], BF16, kind="ExternalOutput").ap()
        d_dbgiq = nc.dram_tensor("dbg_iq", [128, c.QT * TW], BF16, kind="ExternalOutput").ap()
        d_dbgxn2 = nc.dram_tensor("dbg_xn2", [128, HT * TW], BF16, kind="ExternalOutput").ap()
        d_dbgwin = nc.dram_tensor("dbg_win", [8, 128, 128], BF16, kind="ExternalOutput").ap()

    st = contextlib.ExitStack()

    class Alloc:
        def __init__(self, off=0):
            self.off = off

        def take(self, n):
            o = self.off
            self.off += n
            return o

    al = Alloc()
    O_R0 = al.take(3456)
    O_R1 = al.take(max(HT * TW, NS * TC, 5 * 2048 + 2 * 1024 + 256))
    O_R2 = al.take(max(HT * TW // 2, 4 * TC + (KGRP * TC) // 2 + 64))
    O_R3 = al.take(max(c.QT * TW // 2, 8 * TC + 64))
    O_R4 = al.take(4096)
    O_R5 = al.take(2 * TW + TW // 2 + TW // 4 + 320)
    O_R6 = al.take(max(2 * TW, 2 * TC))
    O_R7 = al.take(NH * TC + TC + (NH * DTH * (RK + 2 * NS)) // 2 + 128)
    ATOT = al.off
    arena = st.enter_context(nc.sbuf_tensor("arena", [P, ATOT], F32))
    arbf = arena[:].bitcast(BF16)
    ar8 = arena[:].bitcast(FP8)
    ari = arena[:].bitcast(I32)

    def fview(off, n):
        return arena[:, off:off + n]

    def bview(off_f, n_bf):
        return arbf[:, 2 * off_f:2 * off_f + n_bf]

    def iview(off_f, n):
        return ari[:, off_f:off_f + n]

    a0 = Alloc(O_R0)
    posA = fview(a0.take(DT * NS), DT * NS)
    negA = fview(a0.take(DT * NS), DT * NS)
    cw_sb = fview(a0.take(DT * 4), DT * 4)
    cb_sb = fview(a0.take(DT), DT)
    dtb_sb = fview(a0.take(DT), DT)
    Dp_sb = fview(a0.take(DT), DT)
    pmask_sb = fview(a0.take(c.NC), c.NC)
    cmask_sb = fview(a0.take(c.NC), c.NC)
    xc3_sb = fview(a0.take(DT * 3), DT * 3)
    b3_sb = fview(a0.take(DT * 3), DT * 3)
    ext6 = fview(a0.take(8), 8)
    identb = bview(a0.take(64), 128)
    identf = fview(a0.take(128), 128)
    onesb = bview(a0.take(1), 1)
    nrow = fview(a0.take(2), 2)
    zcol = fview(a0.take(1), 1)
    Scol = fview(a0.take(DT * NS), DT * NS)
    Decb = fview(a0.take(DT * NS), DT * NS)
    sInb = fview(a0.take(DT * NS), DT * NS)
    dec16 = fview(a0.take(NS), NS)
    assert a0.off <= O_R0 + 3456, a0.off - O_R0

    xres = fview(O_R1, HT * TW)
    brep = bview(O_R1, NS * TC)
    crep = bview(O_R1 + (NS * TC) // 2, NS * TC)


    xqbf = bview(O_R2, HT * TW)
    dAbh = bview(O_R2, 2 * TC)
    gBbh = bview(O_R2 + TC, 2 * TC)
    csbh = bview(O_R2 + 2 * TC, 2 * TC)
    gbh2 = bview(O_R2 + 3 * TC, 2 * TC)
    Fbh, Wbh = dAbh, gBbh
    ygbuf = bview(O_R2 + 4 * TC, KGRP * TC)
    assert 4 * TC + (KGRP * TC) // 2 <= max(HT * TW // 2, 4 * TC + (KGRP * TC) // 2 + 64)

    iqbf = bview(O_R3, c.QT * TW)
    SP_U = [O_R3 + 0 * TC, O_R3 + 1 * TC]
    SP_DT = [O_R3 + 2 * TC, O_R3 + 3 * TC]
    SP_SG = [O_R3 + 4 * TC, O_R3 + 5 * TC]
    stb = fview(O_R3 + 6 * TC, TC)
    cdtb = stb
    gb = fview(O_R3 + 7 * TC, TC)
    ubft = [bview(O_R3 + 6 * TC, TC), bview(O_R3 + 6 * TC + TC // 2, TC)]

    wstream = bview(O_R4, 8192)

    a5 = Alloc(O_R5)
    scrF = fview(a5.take(2 * TW), 2 * TW)
    scrH = bview(a5.take(TW // 2), TW)
    _o8 = a5.take(TW // 4)
    q8t = ar8[:, 4 * _o8:4 * _o8 + TW]
    amt = fview(a5.take(64), 64)
    _oi = a5.take(192)
    set_ = iview(_oi, 64)
    invt = iview(_oi + 64, 64)
    sct = iview(_oi + 128, 64)
    assert TW // 32 <= 64

    lnbc = fview(O_R6, 2 * TW)
    csb = fview(O_R6, 2 * TC)
    agb = csb

    a7 = Alloc(O_R7)
    ssm_dt = fview(O_R6, NH * TC)
    ssm_bc = arena[0:32, O_R5:O_R5 + NH * TC]
    yb = fview(O_R4 + 1024, TC)
    argb = yb
    dumpb = fview(O_R4 + 2048, TC)
    _od = a7.take(NH * TC // 2)
    dtin_bf = arbf[:, 2 * _od:2 * _od + NH * TC]
    ibuf = fview(_od, TC)
    _obc = a7.take(NH * TC // 2)
    bcrow_bf = arbf[0:32, 2 * _obc:2 * _obc + NH * TC]
    evst = fview(a7.take(TC), TC)
    wx_cols = NH * DTH * (RK + 2 * NS)
    _owx = a7.take((wx_cols + 1) // 2)
    wxbf = arbf[:, 2 * _owx:2 * _owx + wx_cols]

    def semh(name):
        return Sem(st.enter_context(nc.semaphore(name)), name)
    dL = semh("dL")
    dW = semh("dW")
    dU = semh("dU")
    dR = semh("dR")
    dO = semh("dO")
    sT = semh("sT")
    sV = semh("sV")
    sS = semh("sS")
    sG = semh("sG")
    sCC = semh("sCC")
    dWG = semh("dWG")

    kb = KB(nc)
    E = kb.emit
    W = kb.wait

    wsmark = {0: 0, 4096: 0}

    def wslab(buf, dst_off, nblk, src_elem_off, wait_mark=True):
        if wait_mark:
            W("sync", sT, wsmark[buf])
        dview = wag[src_elem_off:src_elem_off + nblk * TB].rearrange(
            "(b r q) -> r b q", r=128, q=128)
        sview = wstream[:, buf + dst_off:buf + dst_off + nblk * 128].rearrange(
            "p (b q) -> p b q", q=128)
        E("sync", lambda api, d=dview, s=sview: api.dma_start(out=s, in_=d), inc=dW, inck=16)

    def dma(eng, out, in_, sem):
        E(eng, lambda api, o=out, i=in_: api.dma_start(out=o, in_=i), inc=sem, inck=16)

    def r32(ap, nblk):
        return ap.rearrange("p (b k) -> p b k", k=32)

    def bc32(intap, nblk):
        return intap[:, 0:nblk].rearrange("p (b o) -> p b o", o=1).bitcast(F32).broadcast_to([P, nblk, 32])

    dma("sync", cw_sb.rearrange("p (t k) -> p t k", k=4), d_cw.rearrange("t p k -> p t k"), dL)
    dma("sync", cb_sb, d_cb, dL)
    dma("sync", dtb_sb, d_dtb, dL)
    dma("sync", Dp_sb, d_Dp, dL)
    dma("sync", posA.rearrange("p (t n) -> p t n", n=NS), d_alog.rearrange("t p n -> p t n"), dL)
    dma("sync", identb, d_iden, dL)
    dma("sync", identf, d_idenf, dL)
    dma("sync", pmask_sb, d_pmask, dL)
    dma("sync", cmask_sb, d_cmask, dL)
    base_load_n = dL.n
    W("scalar", dL, base_load_n)
    E("scalar", lambda api: api.activation(out=posA, in_=posA, func=Act.Exp), inc=sS)
    W("vector", sS)
    E("vector", lambda api: api.tensor_scalar(out=negA, in0=posA, scalar1=-1.0,
                                              scalar2=None, op0=Alu.mult), inc=sV)
    E("vector", lambda api: api.memset(zcol, 0.0), inc=sV)
    E("vector", lambda api: api.memset(onesb, 1.0), inc=sV)
    E("vector", lambda api: api.memset(xqbf[:, 0:1024], 0.0), inc=sV)
    E("vector", lambda api: api.memset(amt[:, 0:64], 0.0), inc=sV)
    W("sync", sV)
    dma("sync", d_pack[PACK - 8 * TB:PACK].rearrange("(p q) -> p q", p=128),
        xqbf[:, 0:1024], dO)

    WPW = 2048
    ap1 = Alloc(O_R1)
    wp_in = fview(ap1.take(WPW), WPW)
    wp_z = fview(ap1.take(WPW), WPW)
    wp_sgn = fview(ap1.take(WPW), WPW)
    wp_acc = fview(ap1.take(WPW), WPW)
    wp_tmp = fview(ap1.take(WPW), WPW)
    _oq = ap1.take(WPW // 2)
    wp_q = arbf[:, 2 * _oq:2 * _oq + WPW]
    _oe = ap1.take(WPW // 2)
    wp_ev = arbf[:, 2 * _oe:2 * _oe + WPW]
    wp_am = fview(ap1.take(WPW // 32), WPW // 32)
    _wo = ap1.take(3 * (WPW // 32))
    wp_se = iview(_wo, WPW // 32)
    wp_inv = iview(_wo + WPW // 32, WPW // 32)
    wp_sc = iview(_wo + 2 * (WPW // 32), WPW // 32)
    assert ap1.off <= O_R1 + max(HT * TW, NS * TC, 5 * 2048 + 2 * 1024 + 256)
    psTr_cm = nc.psum_tensor("psTr", [P, 2048], BF16)
    psTr = psTr_cm.__enter__()

    def quantize_weights(ncols):
        nblk = ncols // 32
        W("vector", sT)
        E("vector", lambda api, ncols=ncols, nblk=nblk: api.tensor_reduce(
            out=wp_am[:, 0:nblk], in_=r32(wp_in[:, 0:ncols], nblk),
            axis=AxX, op=Alu.max, apply_absolute_value=True))
        E("vector", lambda api, nblk=nblk: api.tensor_scalar(out=wp_se[:, 0:nblk], in0=wp_am[:, 0:nblk].bitcast(I32),
                                                             scalar1=23, scalar2=None, op0=Alu.logical_shift_right))
        E("vector", lambda api, nblk=nblk: api.tensor_scalar(out=wp_se[:, 0:nblk], in0=wp_se[:, 0:nblk],
                                                             scalar1=-2, scalar2=0, op0=Alu.add, op1=Alu.max))
        E("vector", lambda api, nblk=nblk: api.tensor_scalar(out=wp_se[:, 0:nblk], in0=wp_se[:, 0:nblk],
                                                             scalar1=254, scalar2=None, op0=Alu.min))
        E("vector", lambda api, nblk=nblk: api.tensor_scalar(out=wp_inv[:, 0:nblk], in0=wp_se[:, 0:nblk],
                                                             scalar1=-1, scalar2=254, op0=Alu.mult, op1=Alu.add))
        E("vector", lambda api, nblk=nblk: api.tensor_scalar(out=wp_inv[:, 0:nblk], in0=wp_inv[:, 0:nblk],
                                                             scalar1=23, scalar2=None, op0=Alu.logical_shift_left))
        E("vector", lambda api, nblk=nblk: api.tensor_scalar(out=wp_sc[:, 0:nblk], in0=wp_se[:, 0:nblk],
                                                             scalar1=23, scalar2=None, op0=Alu.logical_shift_left))
        E("vector", lambda api, ncols=ncols, nblk=nblk: api.tensor_tensor(
            out=r32(wp_tmp[:, 0:ncols], nblk), in0=r32(wp_in[:, 0:ncols], nblk),
            in1=bc32(wp_inv, nblk), op=Alu.mult), inc=sV)
        W("scalar", sV)
        E("scalar", lambda api, ncols=ncols: api.activation(out=wp_z[:, 0:ncols], in_=wp_tmp[:, 0:ncols],
                                                            func=Act.Abs), inc=sS)
        E("vector", lambda api, ncols=ncols: api.tensor_scalar(out=wp_sgn[:, 0:ncols], in0=wp_tmp[:, 0:ncols],
                                                               scalar1=0.0, scalar2=2.0, op0=Alu.is_ge, op1=Alu.mult))
        E("vector", lambda api, ncols=ncols: api.tensor_scalar(out=wp_sgn[:, 0:ncols], in0=wp_sgn[:, 0:ncols],
                                                               scalar1=-1.0, scalar2=None, op0=Alu.add))
        W("vector", sS)
        th = [(0.25, 0.5), (0.75, 0.5), (1.25, 0.5), (1.75, 0.5), (2.5, 1.0), (3.5, 1.0), (5.0, 2.0)]
        E("vector", lambda api, ncols=ncols: api.tensor_scalar(out=wp_acc[:, 0:ncols], in0=wp_z[:, 0:ncols],
                                                               scalar1=th[0][0], scalar2=th[0][1],
                                                               op0=Alu.is_ge, op1=Alu.mult))
        for cc_, ww_ in th[1:]:
            E("vector", lambda api, cc_=cc_, ww_=ww_, ncols=ncols: api.tensor_scalar(
                out=wp_tmp[:, 0:ncols], in0=wp_z[:, 0:ncols],
                scalar1=cc_, scalar2=ww_, op0=Alu.is_ge, op1=Alu.mult))
            E("vector", lambda api, ncols=ncols: api.tensor_tensor(out=wp_acc[:, 0:ncols], in0=wp_acc[:, 0:ncols],
                                                                   in1=wp_tmp[:, 0:ncols], op=Alu.add))
        E("vector", lambda api, ncols=ncols: api.tensor_tensor(out=wp_acc[:, 0:ncols], in0=wp_acc[:, 0:ncols],
                                                               in1=wp_sgn[:, 0:ncols], op=Alu.mult))
        E("vector", lambda api, ncols=ncols, nblk=nblk: api.tensor_tensor(
            out=r32(wp_q[:, 0:ncols], nblk), in0=r32(wp_acc[:, 0:ncols], nblk),
            in1=bc32(wp_sc, nblk), op=Alu.mult), inc=sV)

    W("tensor", dL, base_load_n)

    def prep_matrix(src, nrows, ncols, quant, blk_base, blk_order, transpose=True):
        nch = nrows // 128
        ncc = (ncols + WPW - 1) // WPW
        for ch in range(nch):
            for cc in range(ncc):
                c0 = cc * WPW
                cn = min(WPW, ncols - c0)
                nct = cn // 128
                W("sync", sV)
                W("sync", sS)
                dma("sync", wp_in[:, 0:cn], src[ch * 128:(ch + 1) * 128, c0:c0 + cn], dW)
                W("vector", dW)
                if quant:
                    quantize_weights(cn)
                else:
                    W("vector", sT)
                    kb.inc("vector", sV)
                    W("scalar", sV)
                    E("scalar", lambda api, cn=cn: api.copy(out=wp_q[:, 0:cn], in_=wp_in[:, 0:cn]), inc=sS)
                    W("vector", sS)
                    kb.inc("vector", sV)
                if transpose:
                    W("tensor", sV)
                    for j0 in range(0, nct, 16):
                        jn = min(16, nct - j0)
                        W("tensor", sS)
                        for j in range(j0, j0 + jn):
                            E("tensor", lambda api, j=j, j0=j0: api.transpose(
                                psTr[:, (j - j0) * 128:(j - j0 + 1) * 128],
                                wp_q[:, j * 128:(j + 1) * 128], identb), inc=sT)
                        W("scalar", sT)
                        E("scalar", lambda api, jn=jn, j0=j0: api.copy(
                            out=wp_ev[:, j0 * 128:(j0 + jn) * 128],
                            in_=psTr[:, 0:jn * 128]), inc=sS)
                    W("sync", sS)
                    evsrc = wp_ev
                else:
                    W("sync", sS)
                    W("sync", sV)
                    evsrc = wp_q
                idxs = [blk_order(ch, (c0 // 128) + j) for j in range(nct)]
                stride = (idxs[1] - idxs[0]) if nct > 1 else 1
                for j in range(nct - 1):
                    assert idxs[j + 1] - idxs[j] == stride
                base_off = blk_base + idxs[0] * TB
                if stride > 1:
                    dview = d_pack[base_off:base_off + nct * stride * TB]
                    dview = dview.rearrange("(b s r q) -> r b s q", s=stride, r=128, q=128)[:, :, 0, :]
                else:
                    dview = d_pack[base_off:base_off + nct * TB].rearrange("(b r q) -> r b q", r=128, q=128)
                sview = evsrc[:, 0:nct * 128].rearrange("p (b q) -> p b q", q=128)
                dma("sync", dview, sview, dO)

    prep_matrix(d_wg, c.IFF // c.NC, c.H, True, base_gate, lambda ch, j: ch * HT + j)
    prep_matrix(d_wu, c.IFF // c.NC, c.H, True, base_up, lambda ch, j: ch * HT + j)
    prep_matrix(d_wd, c.H // c.NC, c.IFF, True, base_down, lambda ch, j: ch * FT + j)
    prep_matrix(d_wi, c.H // c.NC, 2 * c.IP, False, base_win, lambda ch, j: j * HPC + ch, transpose=False)
    prep_matrix(d_wo, c.IP // c.NC, c.H, False, base_wout, lambda ch, j: j * OPC + ch, transpose=False)

    RW = RK + 2 * NS
    CHT = max(1, 2048 // RW)
    for h_ in range(NH):
        for t0_ in range(0, DTH, CHT):
            tn_ = min(CHT, DTH - t0_)
            W("sync", sS)
            W("sync", sV)
            dma("sync", wp_in[:, 0:tn_ * RW].rearrange("p (t r) -> p t r", r=RW),
                d_wx.rearrange("h (t p) r -> p h t r", p=P)[:, h_, t0_:t0_ + tn_], dL)
            W("scalar", dL)
            woff = (h_ * DTH + t0_) * RW
            E("scalar", lambda api, woff=woff, tn_=tn_: api.copy(
                out=wxbf[:, woff:woff + tn_ * RW], in_=wp_in[:, 0:tn_ * RW]), inc=sS)

    W("gpsimd", dO)
    E("gpsimd", lambda api: api.collective_compute(
        "AllGather", Alu.bypass, replica_groups=[list(range(c.NC))],
        ins=[d_pack_h.ap()], outs=[d_wag_h.ap()]), inc=sCC)

    prep_T, prep_V, prep_S = sT.n, sV.n, sS.n
    for eng in KB.ENGINES:
        W(eng, sT, prep_T)
        W(eng, sV, prep_V)
        W(eng, sS, prep_S)
    psTr_cm.__exit__(None, None, None)

    psA = st.enter_context(nc.psum_tensor("psA", [P, 1536], F32))
    psB = st.enter_context(nc.psum_tensor("psB", [P, 1536], F32))
    psC = st.enter_context(nc.psum_tensor("psC", [P, 1024], F32))
    psmark = {"A": (sV, 0), "B": (sV, 0), "C": (sV, 0)}

    def pswait_t(name):
        sem, val = psmark[name]
        W("tensor", sem, val)

    dma("sync", xres.rearrange("p (t w) -> p t w", w=TW),
        d_x.rearrange("(t p) w -> p t w", p=P), dL)
    xload_mark = dL.n
    W("vector", dL, xload_mark)
    W("scalar", dL, xload_mark)
    W("tensor", dL, xload_mark)

    def ln_and(width, col0, out_consumer):
        inv_n = 1.0 / c.H
        pswait_t("A")
        pswait_t("B")
        W("scalar", sV)
        cast_mark = {}
        xcast = [bview(O_R3, TW), bview(O_R3 + TW // 2, TW)]
        qcast = [bview(O_R3 + TW, TW), bview(O_R3 + 3 * (TW // 2), TW)]
        for t in range(HT):
            xa = xres[:, t * TW + col0: t * TW + col0 + width]
            pb = t % 2
            if t >= 2:
                W("scalar", sT, cast_mark[t - 2])
            E("scalar", lambda api, xa=xa, pb=pb, width=width: api.copy(
                out=xcast[pb][:, 0:width], in_=xa), inc=sS)
            E("scalar", lambda api, xa=xa, pb=pb, width=width: api.activation(
                out=qcast[pb][:, 0:width], in_=xa, func=Act.Square), inc=sS)
            W("tensor", sS)
            for (cc0, cn) in chunks_of(width):
                E("tensor", lambda api, cc0=cc0, cn=cn, t=t, pb=pb: api.matmul(
                    psA[0:1, cc0:cc0 + cn], onesb, xcast[pb][:, cc0:cc0 + cn],
                    start=(t == 0), stop=(t == HT - 1)), inc=sT)
                E("tensor", lambda api, cc0=cc0, cn=cn, t=t, pb=pb: api.matmul(
                    psB[0:1, cc0:cc0 + cn], onesb, qcast[pb][:, cc0:cc0 + cn],
                    start=(t == 0), stop=(t == HT - 1)), inc=sT)
            cast_mark[t] = sT.n
        W("vector", sT)
        m_ = scrF[0:1, 0:width]
        v_ = scrF[0:1, TW:TW + width]
        t_ = lnbc[0:1, 0:width]
        r_ = lnbc[0:1, TW:TW + width]
        E("vector", lambda api, m_=m_, width=width: api.tensor_scalar(out=m_, in0=psA[0:1, 0:width],
                                                                     scalar1=inv_n, scalar2=None, op0=Alu.mult))
        E("vector", lambda api, v_=v_, width=width: api.tensor_scalar(out=v_, in0=psB[0:1, 0:width],
                                                                      scalar1=inv_n, scalar2=None, op0=Alu.mult))
        E("vector", lambda api, t_=t_, m_=m_: api.tensor_tensor(out=t_, in0=m_, in1=m_, op=Alu.mult))
        E("vector", lambda api, v_=v_, t_=t_: api.tensor_tensor(out=v_, in0=v_, in1=t_, op=Alu.subtract))
        E("vector", lambda api, v_=v_: api.tensor_scalar(out=v_, in0=v_,
                                                         scalar1=1e-5, scalar2=None, op0=Alu.add), inc=sV)
        psmark["A"] = (sV, sV.n)
        psmark["B"] = (sV, sV.n)
        W("scalar", sV)
        E("scalar", lambda api, t_=t_, v_=v_: api.activation(out=t_, in_=v_, func=Act.Ln))
        E("scalar", lambda api, r_=r_, t_=t_: api.activation(out=r_, in_=t_, func=Act.Exp, scale=-0.5), inc=sS)
        W("vector", sS)
        E("vector", lambda api, t_=t_, m_=m_, r_=r_: api.tensor_tensor(out=t_, in0=m_, in1=r_, op=Alu.mult))
        E("vector", lambda api, m_=m_, t_=t_: api.tensor_scalar(out=m_, in0=t_,
                                                                scalar1=-1.0, scalar2=None, op0=Alu.mult), inc=sV)
        W("sync", sV)
        dma("sync", d_scr_ln[0:1, 0:width], r_, dU)
        dma("sync", d_scr_ln[1:2, 0:width], m_, dU)
        W("sync", dU)
        dma("sync", lnbc[:, 0:width], d_scr_ln[0:1, 0:width].broadcast_to([P, width]), dR)
        dma("sync", lnbc[:, TW:TW + width], d_scr_ln[1:2, 0:width].broadcast_to([P, width]), dR)
        W("vector", dR)
        for t in range(HT):
            xa = xres[:, t * TW + col0: t * TW + col0 + width]
            E("vector", lambda api, xa=xa, width=width: api.tensor_tensor(
                out=scrF[:, 0:width], in0=xa, in1=lnbc[:, 0:width], op=Alu.mult))
            E("vector", lambda api, width=width: api.tensor_tensor(
                out=scrF[:, TW:TW + width], in0=scrF[:, 0:width], in1=lnbc[:, TW:TW + width],
                op=Alu.add), inc=sV)
            out_consumer(t, scrF[:, TW:TW + width], width)

    def quant_act_tile(xn_ap, width, out_bf_ap):
        assert width == TC
        nblk = width // 32
        NCH = width // 128
        Y = scrF[:, 0:width]
        Y6 = scrF[:, TW:TW + width]
        pswait_t("C")
        W("tensor", sV)
        for j in range(NCH):
            E("tensor", lambda api, j=j, xn_ap=xn_ap: api.transpose(
                psC[:, j * 128:(j + 1) * 128],
                xn_ap[:, j * 128:(j + 1) * 128], identf), inc=sT)
        quant_act_tile.qin_mark = sT.n
        W("scalar", sT)
        E("scalar", lambda api, Y=Y, width=width: api.copy(out=Y, in_=psC[:, 0:width]), inc=sS)
        psmark["C"] = (sS, sS.n)
        W("vector", sS)
        E("vector", lambda api, Y=Y, nblk=nblk: api.tensor_reduce(out=amt[:, 0:nblk], in_=r32(Y, nblk),
                                                                  axis=AxX, op=Alu.max,
                                                                  apply_absolute_value=True), inc=sV)
        W("vector", sV)
        E("vector", lambda api: api.tensor_scalar(out=set_[:, 0:64], in0=amt[:, 0:64].bitcast(I32),
                                                  scalar1=23, scalar2=None, op0=Alu.logical_shift_right))
        E("vector", lambda api: api.tensor_scalar(out=set_[:, 0:64], in0=set_[:, 0:64],
                                                  scalar1=-2, scalar2=0, op0=Alu.add, op1=Alu.max))
        E("vector", lambda api: api.tensor_scalar(out=invt[:, 0:64], in0=set_[:, 0:64],
                                                  scalar1=-1, scalar2=248, op0=Alu.mult, op1=Alu.add))
        E("vector", lambda api: api.tensor_scalar(out=invt[:, 0:64], in0=invt[:, 0:64],
                                                  scalar1=23, scalar2=None, op0=Alu.logical_shift_left))
        E("vector", lambda api: api.tensor_scalar(out=sct[:, 0:64], in0=set_[:, 0:64],
                                                  scalar1=6, scalar2=None, op0=Alu.add))
        E("vector", lambda api: api.tensor_scalar(out=sct[:, 0:64], in0=sct[:, 0:64],
                                                  scalar1=23, scalar2=None, op0=Alu.logical_shift_left), inc=sV)
        W("vector", sV)
        E("vector", lambda api, Y=Y, Y6=Y6, nblk=nblk: api.tensor_tensor(
            out=r32(Y6, nblk), in0=r32(Y, nblk), in1=bc32(invt, nblk), op=Alu.mult))
        M = 7.5 * 2.0 ** -6
        E("vector", lambda api, Y6=Y6: api.tensor_scalar(out=Y6, in0=Y6, scalar1=M, scalar2=-M,
                                                         op0=Alu.min, op1=Alu.max), inc=sV)
        W("scalar", sV)
        E("scalar", lambda api, Y6=Y6, width=width: api.copy(out=q8t[:, 0:width], in_=Y6), inc=sS)
        W("vector", sS)
        E("vector", lambda api, Y=Y, width=width, nblk=nblk: api.tensor_tensor(
            out=r32(Y, nblk), in0=r32(q8t[:, 0:width], nblk),
            in1=bc32(sct, nblk), op=Alu.mult), inc=sV)
        pswait_t("C")
        W("tensor", sV)
        for j in range(NCH):
            E("tensor", lambda api, j=j, Y=Y: api.transpose(
                psC[:, j * 128:(j + 1) * 128],
                Y[:, j * 128:(j + 1) * 128], identf), inc=sT)
        W("scalar", sT)
        E("scalar", lambda api, width=width, out_bf_ap=out_bf_ap: api.copy(
            out=out_bf_ap, in_=psC[:, 0:width]), inc=sS)
        psmark["C"] = (sS, sS.n)
        W("vector", sS)
        kb.inc("vector", sV)

    def ffn(col0, width):
        nch = chunks_of(width)

        def consume(t, xn_ap, w_):
            quant_act_tile(xn_ap, w_, xqbf[:, t * TW:t * TW + w_])
        ln_and(width, col0, consume)
        xq_done = sV.n
        if c.debug and col0 == 0:
            W("sync", sV, xq_done)
            dma("sync", d_dbgxq, xqbf[:, 0:HT * TW], dU)
        for q in range(c.NQ):
            for oj in range(c.QT):
                og = q * c.QT + oj
                rank = og // RPC
                olocal = og % RPC
                gbase = rank * PACK + base_gate + (olocal * HT) * TB
                ubase = rank * PACK + base_up + (olocal * HT) * TB
                buf = (og % 2) * 4096
                wslab(buf, 0, HT, gbase)
                wslab(buf, HT * 128, HT, ubase, wait_mark=False)
                W("tensor", dW)
                W("tensor", sV, xq_done)
                pswait_t("A")
                pswait_t("B")
                for k in range(HT):
                    for (cc0, cn) in nch:
                        E("tensor", lambda api, k=k, cc0=cc0, cn=cn, buf=buf: api.matmul(
                            psA[:, cc0:cc0 + cn], wstream[:, buf + k * 128:buf + (k + 1) * 128],
                            xqbf[:, k * TW + cc0:k * TW + cc0 + cn],
                            start=(k == 0), stop=(k == HT - 1)), inc=sT)
                    for (cc0, cn) in nch:
                        E("tensor", lambda api, k=k, cc0=cc0, cn=cn, buf=buf: api.matmul(
                            psB[:, cc0:cc0 + cn], wstream[:, buf + (HT + k) * 128:buf + (HT + k + 1) * 128],
                            xqbf[:, k * TW + cc0:k * TW + cc0 + cn],
                            start=(k == 0), stop=(k == HT - 1)), inc=sT)
                wsmark[buf] = sT.n
                W("scalar", sT)
                E("scalar", lambda api, width=width: api.activation(
                    out=evst[:, 0:width], in_=psA[:, 0:width], func=Act.Silu), inc=sS)
                W("vector", sS)
                W("vector", sT, getattr(quant_act_tile, "qin_mark", 0))
                E("vector", lambda api, width=width: api.tensor_tensor(
                    out=ibuf[:, 0:width], in0=evst[:, 0:width], in1=psB[:, 0:width],
                    op=Alu.mult), inc=sV)
                psmark["A"] = (sV, sV.n)
                psmark["B"] = (sV, sV.n)
                quant_act_tile(ibuf[:, 0:width], width, iqbf[:, oj * TW:oj * TW + width])
            iq_done = sV.n
            if c.debug and col0 == 0 and q == 0:
                W("sync", sV, iq_done)
                dma("sync", d_dbgiq, iqbf[:, 0:c.QT * TW], dU)
            for m in range(HT):
                rank = m // HPC
                mlocal = m % HPC
                dbase = rank * PACK + base_down + (mlocal * FT + q * c.QT) * TB
                buf = (m % 2) * 4096
                wslab(buf, 0, c.QT, dbase)
                W("tensor", dW)
                W("tensor", sV, iq_done)
                ppname = "A" if (m % 2 == 0) else "B"
                pp = psA if (m % 2 == 0) else psB
                pswait_t(ppname)
                for k in range(c.QT):
                    for (cc0, cn) in nch:
                        E("tensor", lambda api, k=k, cc0=cc0, cn=cn, buf=buf, pp=pp: api.matmul(
                            pp[:, cc0:cc0 + cn], wstream[:, buf + k * 128:buf + (k + 1) * 128],
                            iqbf[:, k * TW + cc0:k * TW + cc0 + cn],
                            start=(k == 0), stop=(k == c.QT - 1)), inc=sT)
                wsmark[buf] = sT.n
                W("vector", sT)
                xa = xres[:, m * TW + col0: m * TW + col0 + width]
                E("vector", lambda api, pp=pp, xa=xa, width=width: api.scalar_tensor_tensor(
                    out=xa, in0=pp[:, 0:width], scalar=0.5, in1=xa, op0=Alu.mult, op1=Alu.add), inc=sV)
                psmark[ppname] = (sV, sV.n)

    W("tensor", sCC)
    W("sync", sCC)
    ffn(0, TW)

    def consume_ln2(t, xn_ap, w_):
        W("scalar", sV)
        E("scalar", lambda api, t=t, xn_ap=xn_ap, w_=w_: api.copy(
            out=xqbf[:, t * TW:t * TW + w_], in_=xn_ap), inc=sS)
        W("vector", sS)
    ln_and(TW, 0, consume_ln2)
    xn2_done = sS.n
    if c.debug:
        W("sync", sS)
        dma("sync", d_dbgxn2, xqbf[:, 0:HT * TW], dU)
        for rr in range(8):
            off = rr * PACK + base_win
            dma("sync", d_dbgwin[rr],
                wag[off:off + TB].rearrange("(r q) -> r q", r=128), dU)
    W("sync", sV)
    W("sync", sS)
    dma("sync", d_x1.rearrange("(t p) w -> p t w", p=P),
        xres.rearrange("p (t w) -> p t w", w=TW), dU)
    x1_spilled = dU.n

    E("vector", lambda api: api.memset(ssm_dt, 0.0))
    E("vector", lambda api: api.memset(ssm_bc, 0.0), inc=sV)

    nchW = chunks_of(TW)
    nchC = chunks_of(TC)
    ip_xp_mark = {}
    ip_spill_mark = {}
    for ot in range(2 * DT):
        buf = (ot % 2) * 4096
        for r in range(c.NC):
            wb_ = r * PACK + base_win + (ot * HPC) * TB
            wslab(buf, r * HPC * 128, HPC, wb_, wait_mark=(r == 0))
        W("tensor", dW)
        W("tensor", sS, xn2_done)
        ppname = "A" if (ot % 2 == 0) else "B"
        pp = psA if (ot % 2 == 0) else psB
        pswait_t(ppname)
        for k in range(HT):
            for (cc0, cn) in nchW:
                E("tensor", lambda api, k=k, cc0=cc0, cn=cn, buf=buf, pp=pp: api.matmul(
                    pp[:, cc0:cc0 + cn], wstream[:, buf + k * 128:buf + (k + 1) * 128],
                    xqbf[:, k * TW + cc0:k * TW + cc0 + cn],
                    start=(k == 0), stop=(k == HT - 1)), inc=sT)
        wsmark[buf] = sT.n
        if ot < DT:
            dtile = ot
            W("scalar", sT)
            if ot >= 2:
                W("scalar", dU, ip_spill_mark.get(ot - 2, 0))
            xcb = SP_U[ot % 2]
            E("scalar", lambda api, pp=pp, xcb=xcb: api.copy(out=fview(xcb, TC), in_=pp[:, 0:TC]), inc=sS)
            psmark[ppname] = (sS, sS.n)
            E("scalar", lambda api, xcb=xcb, dtile=dtile: api.copy(
                out=xc3_sb[:, dtile * 3:dtile * 3 + 3], in_=fview(xcb, TC)[:, TC - 3:TC]), inc=sS)
            W("sync", sS)
            dma("sync", d_scr_xc[dtile], fview(xcb, TC), dU)
            ip_spill_mark[ot] = dU.n
            if c.debug:
                dma("sync", d_dbg3[dtile], fview(xcb, TC), dU)
        else:
            dtile = ot - DT
            W("scalar", sT)
            if ot >= 2:
                W("scalar", dU, ip_spill_mark.get(ot - 2, 0))
            sgb = SP_SG[ot % 2]
            E("scalar", lambda api, pp=pp, sgb=sgb: api.activation(
                out=fview(sgb, TC), in_=pp[:, HALO:HALO + TC], func=Act.Silu), inc=sS)
            psmark[ppname] = (sS, sS.n)
            W("sync", sS)
            dma("sync", d_scr_sg[dtile], fview(sgb, TC), dU)
            ip_spill_mark[ot] = dU.n

    W("sync", sS)
    dma("sync", d_cc3in, xc3_sb, dU)
    W("gpsimd", dU)
    E("gpsimd", lambda api: api.collective_compute(
        "AllGather", Alu.bypass, replica_groups=[list(range(c.NC))],
        ins=[d_cc3in_h.ap().rearrange("p q -> (p q)")],
        outs=[d_cc3out_h.ap().rearrange("n p q -> (n p q)")]), inc=sCC)
    cc3_done = sCC.n
    W("vector", sCC, cc3_done)
    E("vector", lambda api: api.memset(b3_sb, 0.0), inc=sV)
    for j in range(c.NC):
        W("sync", sV)
        dma("sync", gb[:, 0:DT * 3], d_cc3out[j], dR)
        W("vector", dR)
        E("vector", lambda api, j=j: api.tensor_scalar(
            out=gb[:, 0:DT * 3], in0=gb[:, 0:DT * 3],
            scalar1=cmask_sb[:, j:j + 1], scalar2=None, op0=Alu.mult))
        E("vector", lambda api: api.tensor_tensor(out=b3_sb, in0=b3_sb, in1=gb[:, 0:DT * 3],
                                                  op=Alu.add), inc=sV)

    cv_spill = {}
    cv_v = {}
    for dtile in range(DT):
        head = dtile // DTH
        xcb = SP_U[dtile % 2]
        ub = SP_DT[dtile % 2]
        if dtile >= 2:
            W("sync", sV, cv_v.get(dtile - 2, 0))
            W("sync", dU, cv_spill.get(dtile - 2, 0))
        dma("sync", fview(xcb, TC), d_scr_xc[dtile], dR)
        W("vector", dR)
        E("vector", lambda api, xcb=xcb, dtile=dtile: api.tensor_scalar(
            out=gb[:, 3:TC], in0=fview(xcb, TC)[:, 0:TC - 3],
            scalar1=cw_sb[:, dtile * 4:dtile * 4 + 1], scalar2=None, op0=Alu.mult))
        for kk in range(1, 4):
            E("vector", lambda api, xcb=xcb, dtile=dtile, kk=kk: api.scalar_tensor_tensor(
                out=gb[:, 3:TC], in0=fview(xcb, TC)[:, kk:kk + TC - 3],
                scalar=cw_sb[:, dtile * 4 + kk:dtile * 4 + kk + 1], in1=gb[:, 3:TC],
                op0=Alu.mult, op1=Alu.add))
        E("vector", lambda api, dtile=dtile: api.tensor_copy(out=ext6[:, 0:3],
                                                             in_=b3_sb[:, dtile * 3:dtile * 3 + 3]))
        E("vector", lambda api, xcb=xcb: api.tensor_copy(out=ext6[:, 3:6],
                                                         in_=fview(xcb, TC)[:, 0:3]), inc=sV)
        W("vector", sV)
        E("vector", lambda api, dtile=dtile: api.tensor_scalar(
            out=gb[:, 0:3], in0=ext6[:, 0:3],
            scalar1=cw_sb[:, dtile * 4:dtile * 4 + 1], scalar2=None, op0=Alu.mult))
        for kk in range(1, 4):
            E("vector", lambda api, dtile=dtile, kk=kk: api.scalar_tensor_tensor(
                out=gb[:, 0:3], in0=ext6[:, kk:kk + 3],
                scalar=cw_sb[:, dtile * 4 + kk:dtile * 4 + kk + 1], in1=gb[:, 0:3],
                op0=Alu.mult, op1=Alu.add), inc=(sV if kk == 3 else None))
        cv_v[dtile] = sV.n
        W("scalar", sV)
        E("scalar", lambda api, ub=ub, dtile=dtile: api.activation(
            out=fview(ub, TC), in_=gb, func=Act.Silu,
            bias=cb_sb[:, dtile:dtile + 1]), inc=sS)
        E("scalar", lambda api, ub=ub, uu=ubft[dtile % 2]: api.copy(out=uu, in_=fview(ub, TC)), inc=sS)
        W("sync", sS)
        dma("sync", d_scr_u[dtile], fview(ub, TC), dU)
        cv_spill[dtile] = dU.n
        W("tensor", sS)
        if dtile >= 2:
            W("tensor", sS)
        wxoff = (head * DTH + (dtile % DTH)) * (RK + 2 * NS)
        pswait_t("C")
        for (cc0, cn) in nchC:
            E("tensor", lambda api, cc0=cc0, cn=cn, wxoff=wxoff, uu=ubft[dtile % 2]: api.matmul(
                psC[:, cc0:cc0 + cn], wxbf[:, wxoff:wxoff + RK],
                uu[:, cc0:cc0 + cn], start=True, stop=True), inc=sT)
        xp_t1 = sT.n
        W("vector", sT, xp_t1)
        E("vector", lambda api, head=head: api.tensor_tensor(
            out=ssm_dt[:, head * TC:(head + 1) * TC], in0=ssm_dt[:, head * TC:(head + 1) * TC],
            in1=psC[:, 0:TC], op=Alu.add), inc=sV)
        psmark["C"] = (sV, sV.n)
        pswait_t("C")
        for (cc0, cn) in nchC:
            E("tensor", lambda api, cc0=cc0, cn=cn, wxoff=wxoff, uu=ubft[dtile % 2]: api.matmul(
                psC[0:32, cc0:cc0 + cn], wxbf[:, wxoff + RK:wxoff + RK + 2 * NS],
                uu[:, cc0:cc0 + cn], start=True, stop=True), inc=sT)
        W("vector", sT)
        E("vector", lambda api, head=head: api.tensor_tensor(
            out=ssm_bc[:, head * TC:(head + 1) * TC], in0=ssm_bc[:, head * TC:(head + 1) * TC],
            in1=psC[0:32, 0:TC], op=Alu.add), inc=sV)
        psmark["C"] = (sV, sV.n)
        cv_v[dtile] = sV.n

    W("scalar", sV)
    E("scalar", lambda api: api.copy(out=dtin_bf, in_=ssm_dt), inc=sS)
    E("scalar", lambda api: api.copy(out=bcrow_bf, in_=ssm_bc), inc=sS)
    dtin_mark = sS.n
    W("sync", sS)
    dma("sync", d_scr_bc.rearrange("(h r) w -> r h w", h=NH),
        bcrow_bf.rearrange("p (h w) -> p h w", h=NH), dU)
    bc_spill_mark = dU.n

    ph1_v_mark = {}
    ph1_spill_mark = {}
    for head in range(NH):
        W("sync", dU, bc_spill_mark)
        W("sync", sT)
        W("sync", sV)
        if head == 0:
            W("sync", dU, x1_spilled)
        for n in range(NS):
            dma("sync", brep[:, n * TC:(n + 1) * TC],
                d_scr_bc[head * 2 * NS + n:head * 2 * NS + n + 1, :].broadcast_to([P, TC]), dR)
        brep_done = dR.n
        W("gpsimd", sT, wsmark[0])
        dma("gpsimd", wstream[:, 0:DTH * 128].rearrange("p (t q) -> p t q", q=128),
            d_wdt.rearrange("h p (t q) -> p h t q", q=128)[:, head], dWG)
        wdt_here = dWG.n
        W("tensor", dWG, wdt_here)
        for dl in range(DTH):
            dtile = head * DTH + dl
            W("tensor", sS, dtin_mark)
            pswait_t("C")
            for (cc0, cn) in nchC:
                E("tensor", lambda api, cc0=cc0, cn=cn, head=head, dl=dl: api.matmul(
                    psC[:, cc0:cc0 + cn], wstream[:, dl * 128:(dl + 1) * 128],
                    dtin_bf[:, head * TC + cc0:head * TC + cc0 + cn], start=True, stop=True), inc=sT)
            dtp_mark = sT.n
            W("scalar", sT, dtp_mark)
            dtb0 = SP_DT[dtile % 2]
            if dtile >= 2:
                W("scalar", sV, ph1_v_mark[dtile - 2])
                W("vector", dU, ph1_spill_mark.get(dtile - 2, 0))
            E("scalar", lambda api, dtile=dtile: api.activation(
                out=scrF[:, 0:TC], in_=psC[:, 0:TC], func=Act.Identity,
                bias=dtb_sb[:, dtile:dtile + 1]))
            E("scalar", lambda api: api.activation(out=scrF[:, TW:TW + TC], in_=scrF[:, 0:TC], func=Act.Abs))
            E("scalar", lambda api: api.activation(out=scrF[:, TW:TW + TC], in_=scrF[:, TW:TW + TC],
                                                   func=Act.Exp, scale=-1.0))
            E("scalar", lambda api: api.activation(out=scrF[:, TW:TW + TC], in_=scrF[:, TW:TW + TC],
                                                   func=Act.Ln, bias=1.0), inc=sS)
            W("vector", sS)
            E("vector", lambda api: api.tensor_scalar(out=evst, in0=scrF[:, 0:TC],
                                                      scalar1=0.0, scalar2=None, op0=Alu.max))
            E("vector", lambda api, dtb0=dtb0: api.tensor_tensor(
                out=fview(dtb0, TC), in0=scrF[:, TW:TW + TC], in1=evst, op=Alu.add), inc=sV)
            psmark["C"] = (sV, sV.n)
            dt_ready = sV.n
            W("sync", sV, dt_ready)
            dma("sync", d_scr_dt[dtile], fview(dtb0, TC), dU)
            ph1_spill_mark[dtile] = dU.n
            ub = SP_U[dtile % 2]
            if dtile >= 2:
                W("sync", sV, ph1_v_mark[dtile - 2])
            dma("sync", fview(ub, TC), d_scr_u[dtile], dR)
            u_here = dR.n
            E("vector", lambda api, dtb0=dtb0: api.tensor_tensor_scan(
                out=cdtb, data0=fview(dtb0, TC), data1=zcol.broadcast_to([P, TC]),
                initial=0.0, op0=Alu.add, op1=Alu.add), inc=sV)
            W("vector", sV)
            E("vector", lambda api: api.tensor_scalar(out=nrow[:, 0:1], in0=cdtb[:, TC - 1:TC],
                                                      scalar1=-1.0, scalar2=None, op0=Alu.mult), inc=sV)
            W("vector", sV)
            E("vector", lambda api: api.tensor_scalar(out=argb, in0=cdtb,
                                                      scalar1=nrow[:, 0:1], scalar2=None, op0=Alu.add))
            W("vector", dR, u_here)
            E("vector", lambda api, dtb0=dtb0, ub=ub: api.tensor_tensor(
                out=gb, in0=fview(dtb0, TC), in1=fview(ub, TC), op=Alu.mult))
            E("vector", lambda api, dtile=dtile: api.tensor_scalar(
                out=dec16, in0=negA[:, dtile * NS:(dtile + 1) * NS],
                scalar1=cdtb[:, TC - 1:TC], scalar2=None, op0=Alu.mult), inc=sV)
            gv = sV.n
            W("scalar", sV, gv)
            E("scalar", lambda api, dtile=dtile: api.activation(
                out=Decb[:, dtile * NS:(dtile + 1) * NS], in_=dec16, func=Act.Exp), inc=sS)
            ph1_g = {}
            ph1_vn = {}
            for n in range(NS):
                fb0 = (n % 2) * TC
                if n >= 2:
                    W("scalar", sG, ph1_g[n - 2])
                E("scalar", lambda api, dtile=dtile, n=n, fb0=fb0: api.activation(
                    out=Fbh[:, fb0:fb0 + TC], in_=argb, func=Act.Exp,
                    scale=posA[:, dtile * NS + n:dtile * NS + n + 1]), inc=sS)
                W("gpsimd", sS)
                if n >= 2:
                    W("gpsimd", sV, ph1_vn[n - 2])
                E("gpsimd", lambda api, fb0=fb0: api.tensor_tensor(
                    out=Wbh[:, fb0:fb0 + TC], in0=Fbh[:, fb0:fb0 + TC], in1=gb, op=Alu.mult), inc=sG)
                ph1_g[n] = sG.n
                W("vector", sG, ph1_g[n])
                if dtile % DTH == 0 and n == 0:
                    W("vector", dR, brep_done)
                E("vector", lambda api, fb0=fb0, n=n: api.tensor_tensor(
                    out=csbh[:, fb0:fb0 + TC], in0=Wbh[:, fb0:fb0 + TC],
                    in1=brep[:, n * TC:(n + 1) * TC], op=Alu.mult))
                E("vector", lambda api, fb0=fb0, n=n, dtile=dtile: api.tensor_scalar(
                    out=dumpb.bitcast(BF16)[:, 0:TC],
                    in0=csbh[:, fb0:fb0 + TC], scalar1=1.0, scalar2=0.0, op0=Alu.mult, op1=Alu.add,
                    accum_out=Scol[:, dtile * NS + n:dtile * NS + n + 1]), inc=sV)
                ph1_vn[n] = sV.n
            ph1_v_mark[dtile] = sV.n
        wsmark[0] = sT.n

    W("sync", sV)
    W("sync", sS)
    dma("sync", d_cc2in[0], Scol, dU)
    dma("sync", d_cc2in[1], Decb, dU)
    if c.debug:
        dma("sync", d_dbgcc[0], Scol, dU)
        dma("sync", d_dbgcc[1], Decb, dU)
    W("gpsimd", dU)
    E("gpsimd", lambda api: api.collective_compute(
        "AllGather", Alu.bypass, replica_groups=[list(range(c.NC))],
        ins=[d_cc2in_h.ap().rearrange("a p q -> (a p q)")],
        outs=[d_cc2out_h.ap().rearrange("n a p q -> (n a p q)")]), inc=sCC)
    W("sync", sCC)
    E("vector", lambda api: api.memset(sInb, 0.0), inc=sV)
    DN = DT * NS
    for j in range(c.NC - 1):
        W("sync", sV)
        dma("sync", agb[:, 0:DN], d_cc2out[j, 0], dR)
        dma("sync", agb[:, DN:2 * DN], d_cc2out[j, 1], dR)
        W("vector", dR)
        E("vector", lambda api: api.tensor_scalar(
            out=agb[:, DN:2 * DN], in0=agb[:, DN:2 * DN], scalar1=1.0, scalar2=None, op0=Alu.subtract))
        E("vector", lambda api, j=j: api.tensor_scalar(
            out=agb[:, DN:2 * DN], in0=agb[:, DN:2 * DN],
            scalar1=pmask_sb[:, j:j + 1], scalar2=1.0, op0=Alu.mult, op1=Alu.add))
        E("vector", lambda api: api.tensor_tensor(out=sInb, in0=sInb, in1=agb[:, DN:2 * DN], op=Alu.mult))
        E("vector", lambda api, j=j: api.tensor_scalar(
            out=agb[:, 0:DN], in0=agb[:, 0:DN], scalar1=pmask_sb[:, j:j + 1], scalar2=None, op0=Alu.mult))
        E("vector", lambda api: api.tensor_tensor(out=sInb, in0=sInb, in1=agb[:, 0:DN], op=Alu.add), inc=sV)

    if c.debug:
        W("sync", sV)
        dma("sync", d_dbgcc[2], sInb, dU)
    ph2_v_mark = {}
    ph2_ygfree = [0]
    for head in range(NH):
        W("sync", sV)
        W("sync", sT)
        for n in range(NS):
            dma("sync", brep[:, n * TC:(n + 1) * TC],
                d_scr_bc[head * 2 * NS + n:head * 2 * NS + n + 1, :].broadcast_to([P, TC]), dR)
            dma("sync", crep[:, n * TC:(n + 1) * TC],
                d_scr_bc[head * 2 * NS + NS + n:head * 2 * NS + NS + n + 1, :].broadcast_to([P, TC]), dR)
        rep_done = dR.n
        for dl in range(DTH):
            dtile = head * DTH + dl
            kg = dtile % KGRP
            b2 = dtile % 2
            dtb0 = SP_DT[b2]
            ub = SP_U[b2]
            sgb = SP_SG[b2]
            if dtile >= 2:
                W("sync", sV, ph2_v_mark[dtile - 2])
            dma("sync", fview(dtb0, TC), d_scr_dt[dtile], dR)
            dma("sync", fview(ub, TC), d_scr_u[dtile], dR)
            dma("sync", fview(sgb, TC), d_scr_sg[dtile], dR)
            W("vector", dR)
            if dtile == 0:
                W("vector", dR, rep_done)
            E("vector", lambda api, dtb0=dtb0, ub=ub: api.tensor_tensor(
                out=gbh2[:, 0:TC], in0=fview(dtb0, TC), in1=fview(ub, TC), op=Alu.mult), inc=sV)
            W("vector", sV)
            E("vector", lambda api: api.tensor_copy(out=gbh2[:, TC:2 * TC], in_=gbh2[:, 0:TC]))
            E("vector", lambda api: api.memset(yb, 0.0), inc=sV)
            gmark = sV.n
            W("scalar", sV, gmark)
            ph2_vn = {}
            ph2_gn = {}
            for np_ in range(NS // 2):
                n0 = 2 * np_
                if np_ >= 1:
                    W("scalar", sV, ph2_vn[np_ - 1])
                for dn in range(2):
                    E("scalar", lambda api, dtile=dtile, n0=n0, dn=dn, dtb0=dtb0: api.activation(
                        out=dAbh[:, dn * TC:(dn + 1) * TC], in_=fview(dtb0, TC), func=Act.Exp,
                        scale=negA[:, dtile * NS + n0 + dn:dtile * NS + n0 + dn + 1]), inc=sS)
                sx = sS.n
                W("vector", sS, sx)
                E("vector", lambda api, n0=n0: api.tensor_tensor(
                    out=gBbh[:, 0:2 * TC], in0=gbh2[:, 0:2 * TC],
                    in1=brep[:, n0 * TC:(n0 + 2) * TC], op=Alu.mult))
                for dn in range(2):
                    E("vector", lambda api, dn=dn, n0=n0, dtile=dtile: api.tensor_tensor_scan(
                        out=csbh[:, dn * TC:(dn + 1) * TC], data0=dAbh[:, dn * TC:(dn + 1) * TC],
                        data1=gBbh[:, dn * TC:(dn + 1) * TC],
                        initial=sInb[:, dtile * NS + n0 + dn:dtile * NS + n0 + dn + 1],
                        op0=Alu.mult, op1=Alu.add))
                if np_ >= 1:
                    W("vector", sG, ph2_gn[np_ - 1])
                E("vector", lambda api, n0=n0: api.tensor_tensor(
                    out=csbh[:, 0:2 * TC], in0=csbh[:, 0:2 * TC],
                    in1=crep[:, n0 * TC:(n0 + 2) * TC], op=Alu.mult), inc=sV)
                ph2_vn[np_] = sV.n
                W("gpsimd", sV, ph2_vn[np_])
                E("gpsimd", lambda api: api.tensor_tensor(
                    out=yb, in0=yb, in1=csbh[:, 0:TC], op=Alu.add))
                E("gpsimd", lambda api: api.tensor_tensor(
                    out=yb, in0=yb, in1=csbh[:, TC:2 * TC], op=Alu.add), inc=sG)
                ph2_gn[np_] = sG.n
            W("vector", sG, ph2_gn[NS // 2 - 1])
            E("vector", lambda api, ub=ub, dtile=dtile: api.scalar_tensor_tensor(
                out=yb, in0=fview(ub, TC), scalar=Dp_sb[:, dtile:dtile + 1],
                in1=yb, op0=Alu.mult, op1=Alu.add))
            W("vector", sT, ph2_ygfree[0])
            E("vector", lambda api, sgb=sgb, kg=kg: api.tensor_tensor(
                out=ygbuf[:, kg * TC:(kg + 1) * TC], in0=yb, in1=fview(sgb, TC),
                op=Alu.mult), inc=sV)
            ph2_v_mark[dtile] = sV.n
            if kg == KGRP - 1:
                g = dtile // KGRP
                yg_done = sV.n
                ev_mark = [0]
                for m in range(HT):
                    buf = (m % 2) * 4096
                    nkr = max(1, KGRP // OPC)
                    for half in range(nkr):
                        r = g * nkr + half
                        wb_ = r * PACK + base_wout + (m * OPC) * TB
                        wslab(buf, half * OPC * 128, OPC, wb_, wait_mark=(half == 0))
                    W("tensor", dW)
                    W("tensor", sV, yg_done)
                    ppname = "A" if (m % 2 == 0) else "B"
                    pp = psA if (m % 2 == 0) else psB
                    pswait_t(ppname)
                    for k in range(KGRP):
                        for (cc0, cn) in nchC:
                            E("tensor", lambda api, k=k, cc0=cc0, cn=cn, buf=buf, pp=pp: api.matmul(
                                pp[:, cc0:cc0 + cn], wstream[:, buf + k * 128:buf + (k + 1) * 128],
                                ygbuf[:, k * TC + cc0:k * TC + cc0 + cn],
                                start=(k == 0), stop=(k == KGRP - 1)), inc=sT)
                    wsmark[buf] = sT.n
                    W("scalar", sT)
                    W("scalar", dU, ev_mark[0])
                    E("scalar", lambda api, pp=pp: api.copy(out=evst, in_=pp[:, 0:TC]), inc=sS)
                    psmark[ppname] = (sS, sS.n)
                    W("sync", sS)
                    dma("sync", d_macc[g, m * 128:(m + 1) * 128, :], evst, dU)
                    ev_mark[0] = dU.n
                ph2_ygfree[0] = sT.n

    W("sync", sV)
    W("sync", sS)
    for t in range(HT):
        dma("sync", xres[:, t * TW:(t + 1) * TW],
            d_x1.rearrange("(t p) w -> p t w", p=P)[:, t], dR)
    x1_back = dR.n
    mslot = [fview(O_R3, TC), fview(O_R3 + TC, TC)]
    for t in range(HT):
        for g in range(NGR):
            W("sync", sV)
            dma("sync", mslot[g % 2], d_macc[g, t * 128:(t + 1) * 128, :], dR)
            W("vector", dR)
            if t == 0 and g == 0:
                W("vector", dR, x1_back)
            xa = xres[:, t * TW + HALO:(t + 1) * TW]
            E("vector", lambda api, xa=xa, g=g: api.tensor_tensor(
                out=xa, in0=xa, in1=mslot[g % 2], op=Alu.add), inc=sV)

    ffn(HALO, TC)

    W("sync", sV)
    dma("sync", d_out.rearrange("(t p) w -> p t w", p=P),
        xres.rearrange("p (t w) -> p t w", w=TW)[:, :, HALO:TW], dO)
    W("sync", dO)

    with nc.Block() as blk:
        @blk.sync
        def _(api):
            for thk in kb.streams["sync"]:
                thk(api)

        @blk.tensor
        def _(api):
            for thk in kb.streams["tensor"]:
                thk(api)

        @blk.vector
        def _(api):
            for thk in kb.streams["vector"]:
                thk(api)

        @blk.scalar
        def _(api):
            for thk in kb.streams["scalar"]:
                thk(api)

        @blk.gpsimd
        def _(api):
            for thk in kb.streams["gpsimd"]:
                thk(api)
    st.close()
    return nc


def prepare_inputs(inputs, cfg):
    c = cfg

    def get(k):
        return np.asarray(inputs[k], np.float32)
    x = get("L0_attn_hidden_states")[0]
    x_fm = np.ascontiguousarray(x.T)
    w_in = get("L0_attn_in_proj_w")
    w_in_perm = np.concatenate([w_in[0::2], w_in[1::2]], 0)
    W_inT = np.ascontiguousarray(w_in_perm.T)
    WoutT = np.ascontiguousarray(get("L0_attn_out_proj_w").T)
    WxT = np.ascontiguousarray(get("L0_attn_x_proj_weight").transpose(0, 2, 1))
    WdtT = np.ascontiguousarray(get("L0_attn_dt_proj_weight").transpose(0, 2, 1))
    cw = np.ascontiguousarray(get("L0_attn_conv_w").reshape(c.DT, 128, 4))
    cb = np.ascontiguousarray(get("L0_attn_conv_b").reshape(c.DT, 128).T)
    dtb = np.ascontiguousarray(get("L0_attn_dt_proj_bias").reshape(c.DT, 128).T)
    Dp = np.ascontiguousarray(get("L0_attn_D").reshape(c.DT, 128).T)
    alog = np.ascontiguousarray(get("L0_attn_A_log").reshape(c.DT, 128, c.NS))
    wgate = get("L0_ffn_w_gate")
    wup = get("L0_ffn_w_up")
    wdown = get("L0_ffn_w_down")
    import ml_dtypes
    iden = np.eye(128, dtype=ml_dtypes.bfloat16)
    in_maps = []
    rg = c.IFF // c.NC
    rh = c.H // c.NC
    ro = c.IP // c.NC
    for cid in range(c.NC):
        t0 = cid * c.TC
        x_sh = np.ascontiguousarray(x_fm[:, t0:t0 + c.TC])
        pm = np.zeros((128, c.NC), np.float32)
        pm[:, :cid] = 1.0
        cm = np.zeros((128, c.NC), np.float32)
        if cid > 0:
            cm[:, cid - 1] = 1.0
        in_maps.append(dict(
            x_in=x_sh,
            wg_sl=np.ascontiguousarray(wgate[cid * rg:(cid + 1) * rg]),
            wu_sl=np.ascontiguousarray(wup[cid * rg:(cid + 1) * rg]),
            wd_sl=np.ascontiguousarray(wdown[cid * rh:(cid + 1) * rh]),
            wi_sl=np.ascontiguousarray(W_inT[cid * rh:(cid + 1) * rh]),
            wo_sl=np.ascontiguousarray(WoutT[cid * ro:(cid + 1) * ro]),
            wx=WxT, wdt=WdtT, convw=cw, convb=cb, dtb=dtb, Dp=Dp, alog=alog,
            iden=iden, idenf=np.eye(128, dtype=np.float32), pmask=pm, cmask=cm,
        ))
    return in_maps


_CFG = CFG(mini=False)


def kernel(**inputs):
    cfg = _CFG
    in_maps = prepare_inputs(inputs, cfg)
    nc = build_program(cfg)
    res = run_bass_kernel_spmd(nc, in_maps, core_ids=list(range(cfg.NC)))
    outs = [res.results[i]["out_fm"] for i in range(cfg.NC)]
    out_fm = np.concatenate(outs, axis=1)
    return np.ascontiguousarray(out_fm.T)[None]


# version 07
# speedup vs baseline: 1.0774x; 1.0066x over previous
import numpy as np
import contextlib
import concourse.bass as bass
import concourse.mybir as mybir
from concourse.bass_utils import run_bass_kernel_spmd

F32 = mybir.dt.float32
BF16 = mybir.dt.bfloat16
I32 = mybir.dt.int32
FP8 = mybir.dt.float8e4
Alu = mybir.AluOpType
Act = mybir.ActivationFunctionType
AxX = mybir.AxisListType.X


class CFG:
    def __init__(self, mini=False, debug=False):
        self.NC = 8
        self.debug = debug
        if mini:
            self.H = 1024
            self.IFF = 1024
            self.TC = 128
            self.HD = 512
        else:
            self.H = 2048
            self.IFF = 8192
            self.TC = 1024
            self.HD = 2048
        self.NH = 2
        self.IP = self.NH * self.HD
        self.NS = 16
        self.RK = 128
        self.HALO = 0
        self.TW = self.TC
        self.HT = self.H // 128
        self.FT = self.IFF // 128
        self.DT = self.IP // 128
        self.DTH = self.HD // 128
        self.NQ = 4
        self.QT = self.FT // self.NQ
        self.KGRP = min(8, self.DTH)


def chunks_of(total, step=512):
    out = []
    cpos = 0
    while cpos < total:
        out.append((cpos, min(step, total - cpos)))
        cpos += step
    return out


class Sem:
    def __init__(self, h, name):
        self.h = h
        self.n = 0
        self.name = name


class KB:
    ENGINES = ("sync", "tensor", "vector", "scalar", "gpsimd")

    def __init__(self, nc):
        self.nc = nc
        self.streams = {e: [] for e in self.ENGINES}

    def emit(self, eng, fn, inc=None, inck=1):
        if inc is not None:
            inc.n += inck

        def th(api, fn=fn, inc=inc, inck=inck):
            r = fn(api)
            if inc is not None:
                r.then_inc(inc.h, inck)
        self.streams[eng].append(th)

    def wait(self, eng, sem, val=None):
        v = sem.n if val is None else val
        if v is None or v <= 0:
            return
        assert v <= sem.n, f"wait {sem.name}>={v} but only {sem.n} planned"
        self.streams[eng].append(lambda api, s=sem, v=v: api.wait_ge(s.h, v))

    def inc(self, eng, sem, k=1):
        sem.n += k
        self.streams[eng].append(lambda api, s=sem, k=k: api.sem_inc(s.h, k))


def build_program(cfg):
    c = cfg
    nc = bass.Bass("TRN2", target_bir_lowering=False, debug=False,
                   detect_race_conditions=False)
    P = 128
    TW, TC, HALO = c.TW, c.TC, c.HALO
    HT, FT, DT, DTH = c.HT, c.FT, c.DT, c.DTH
    NS, RK, NH = c.NS, c.RK, c.NH
    KGRP = c.KGRP
    NGR = DT // KGRP

    d_x = nc.dram_tensor("x_in", [c.H, TW], F32, kind="ExternalInput").ap()
    d_out = nc.dram_tensor("out_fm", [c.H, TC], F32, kind="ExternalOutput").ap()
    d_wg = nc.dram_tensor("wg_sl", [c.IFF // c.NC, c.H], F32, kind="ExternalInput").ap()
    d_wu = nc.dram_tensor("wu_sl", [c.IFF // c.NC, c.H], F32, kind="ExternalInput").ap()
    d_wd = nc.dram_tensor("wd_sl", [c.H // c.NC, c.IFF], F32, kind="ExternalInput").ap()
    d_wi = nc.dram_tensor("wi_sl", [c.H // c.NC, 2 * c.IP], F32, kind="ExternalInput").ap()
    d_wo = nc.dram_tensor("wo_sl", [c.IP // c.NC, c.H], F32, kind="ExternalInput").ap()
    d_wx = nc.dram_tensor("wx", [NH, c.HD, 2 * NS + RK], F32, kind="ExternalInput").ap()
    d_wdt = nc.dram_tensor("wdt", [NH, RK, c.HD], F32, kind="ExternalInput").ap()
    d_cw = nc.dram_tensor("convw", [DT, 128, 4], F32, kind="ExternalInput").ap()
    d_cb = nc.dram_tensor("convb", [128, DT], F32, kind="ExternalInput").ap()
    d_dtb = nc.dram_tensor("dtb", [128, DT], F32, kind="ExternalInput").ap()
    d_Dp = nc.dram_tensor("Dp", [128, DT], F32, kind="ExternalInput").ap()
    d_alog = nc.dram_tensor("alog", [DT, 128, NS], F32, kind="ExternalInput").ap()
    d_iden = nc.dram_tensor("iden", [128, 128], BF16, kind="ExternalInput").ap()
    d_idenf = nc.dram_tensor("idenf", [128, 128], F32, kind="ExternalInput").ap()
    d_pmask = nc.dram_tensor("pmask", [128, c.NC], F32, kind="ExternalInput").ap()
    d_cmask = nc.dram_tensor("cmask", [128, c.NC], F32, kind="ExternalInput").ap()

    TB = 128 * 128
    RPC = c.IFF // c.NC // 128
    HPC = c.H // c.NC // 128
    OPC = c.IP // c.NC // 128
    n_gate_blk = RPC * HT
    n_down_blk = HPC * FT
    n_win_blk = HPC * (2 * c.IP // 128)
    n_wout_blk = OPC * HT
    base_gate = 0
    base_up = base_gate + n_gate_blk * TB
    base_down = base_up + n_gate_blk * TB
    base_win = base_down + n_down_blk * TB
    base_wout = base_win + n_win_blk * TB
    PACK1 = base_down
    base2_down = 0
    base2_win = n_win_blk * 0 + n_down_blk * TB
    base2_wout = base2_win + n_win_blk * TB
    PACK2 = base2_wout + n_wout_blk * TB + 8 * TB
    DBGK = dict(kind="ExternalOutput") if c.debug else {}
    d_pack1_h = nc.dram_tensor("wpack1", [PACK1], BF16)
    d_pack2_h = nc.dram_tensor("wpack2", [PACK2], BF16)
    d_wag1_h = nc.dram_tensor("wag1", [c.NC * PACK1], BF16)
    d_wag2_h = nc.dram_tensor("wag2", [c.NC * PACK2], BF16)
    d_pack1 = d_pack1_h.ap(); d_pack2 = d_pack2_h.ap()
    wag1 = d_wag1_h.ap(); wag2 = d_wag2_h.ap()

    d_scr_xc_h = nc.dram_tensor("scr_xc", [DT, 128, TC], F32, **DBGK)
    d_cc3in_h = nc.dram_tensor("cc3in", [128, DT * 3], F32)
    d_cc3out_h = nc.dram_tensor("cc3out", [c.NC, 128, DT * 3], F32)
    d_scr_u_h = nc.dram_tensor("scr_u", [DT, 128, TC], F32, **DBGK)
    d_scr_dt_h = nc.dram_tensor("scr_dt", [DT, 128, TC], F32, **DBGK)
    d_scr_sg_h = nc.dram_tensor("scr_sg", [DT, 128, TC], F32, **DBGK)
    d_scr_bc_h = nc.dram_tensor("scr_bc", [NH * 2 * NS, TC], BF16, **DBGK)
    d_scr_ln_h = nc.dram_tensor("scr_ln", [2, TW], F32)
    d_x1_h = nc.dram_tensor("scr_x1", [c.H, TW], F32, **DBGK)
    d_macc_h = nc.dram_tensor("scr_macc", [NGR, c.H, TC], F32)
    d_cc2in_h = nc.dram_tensor("cc2in", [2, 128, DT * NS], F32)
    d_cc2out_h = nc.dram_tensor("cc2out", [c.NC, 2, 128, DT * NS], F32)
    d_scr_xc = d_scr_xc_h.ap(); d_cc3in = d_cc3in_h.ap(); d_cc3out = d_cc3out_h.ap()
    d_scr_u = d_scr_u_h.ap(); d_scr_dt = d_scr_dt_h.ap(); d_scr_sg = d_scr_sg_h.ap()
    d_scr_bc = d_scr_bc_h.ap(); d_scr_ln = d_scr_ln_h.ap()
    d_x1 = d_x1_h.ap(); d_macc = d_macc_h.ap()
    d_cc2in = d_cc2in_h.ap(); d_cc2out = d_cc2out_h.ap()
    d_dbg3 = d_dbgcc = None
    if c.debug:
        d_dbg3 = nc.dram_tensor("dbg_xc", [DT, 128, TC], F32, kind="ExternalOutput").ap()
        d_dbgcc = nc.dram_tensor("dbg_cc", [3, 128, DT * NS], F32, kind="ExternalOutput").ap()
        d_dbgxq = nc.dram_tensor("dbg_xq", [128, HT * TW], BF16, kind="ExternalOutput").ap()
        d_dbgiq = nc.dram_tensor("dbg_iq", [128, c.QT * TW], BF16, kind="ExternalOutput").ap()
        d_dbgxn2 = nc.dram_tensor("dbg_xn2", [128, HT * TW], BF16, kind="ExternalOutput").ap()
        d_dbgwin = nc.dram_tensor("dbg_win", [8, 128, 128], BF16, kind="ExternalOutput").ap()

    st = contextlib.ExitStack()

    class Alloc:
        def __init__(self, off=0):
            self.off = off

        def take(self, n):
            o = self.off
            self.off += n
            return o

    al = Alloc()
    O_R0 = al.take(3456)
    O_R1 = al.take(max(HT * TW, NS * TC, 5 * 2048 + 2 * 1024 + 256))
    O_R2 = al.take(max(HT * TW // 2, 4 * TC + (KGRP * TC) // 2 + 64))
    O_R3 = al.take(max(c.QT * TW // 2, 8 * TC + 64))
    O_R4 = al.take(4096)
    O_R5 = al.take(2 * TW + TW // 2 + TW // 4 + 320)
    O_R6 = al.take(max(2 * TW, 2 * TC))
    O_R7 = al.take(NH * TC + TC + (NH * DTH * (RK + 2 * NS)) // 2 + 128)
    ATOT = al.off
    arena = st.enter_context(nc.sbuf_tensor("arena", [P, ATOT], F32))
    arbf = arena[:].bitcast(BF16)
    ar8 = arena[:].bitcast(FP8)
    ari = arena[:].bitcast(I32)

    def fview(off, n):
        return arena[:, off:off + n]

    def bview(off_f, n_bf):
        return arbf[:, 2 * off_f:2 * off_f + n_bf]

    def iview(off_f, n):
        return ari[:, off_f:off_f + n]

    a0 = Alloc(O_R0)
    posA = fview(a0.take(DT * NS), DT * NS)
    negA = fview(a0.take(DT * NS), DT * NS)
    cw_sb = fview(a0.take(DT * 4), DT * 4)
    cb_sb = fview(a0.take(DT), DT)
    dtb_sb = fview(a0.take(DT), DT)
    Dp_sb = fview(a0.take(DT), DT)
    pmask_sb = fview(a0.take(c.NC), c.NC)
    cmask_sb = fview(a0.take(c.NC), c.NC)
    xc3_sb = fview(a0.take(DT * 3), DT * 3)
    b3_sb = fview(a0.take(DT * 3), DT * 3)
    ext6 = fview(a0.take(8), 8)
    identb = bview(a0.take(64), 128)
    identf = fview(a0.take(128), 128)
    onesb = bview(a0.take(1), 1)
    nrow = fview(a0.take(2), 2)
    zcol = fview(a0.take(1), 1)
    Scol = fview(a0.take(DT * NS), DT * NS)
    Decb = fview(a0.take(DT * NS), DT * NS)
    sInb = fview(a0.take(DT * NS), DT * NS)
    dec16 = fview(a0.take(NS), NS)
    assert a0.off <= O_R0 + 3456, a0.off - O_R0

    xres = fview(O_R1, HT * TW)
    brep = bview(O_R1, NS * TC)
    crep = bview(O_R1 + (NS * TC) // 2, NS * TC)


    xqbf = bview(O_R2, HT * TW)
    dAbh = bview(O_R2, 2 * TC)
    gBbh = bview(O_R2 + TC, 2 * TC)
    csbh = bview(O_R2 + 2 * TC, 2 * TC)
    gbh2 = bview(O_R2 + 3 * TC, 2 * TC)
    Fbh, Wbh = dAbh, gBbh
    ygbuf = bview(O_R2 + 4 * TC, KGRP * TC)
    assert 4 * TC + (KGRP * TC) // 2 <= max(HT * TW // 2, 4 * TC + (KGRP * TC) // 2 + 64)

    iqbf = bview(O_R3, c.QT * TW)
    SP_U = [O_R3 + 0 * TC, O_R3 + 1 * TC]
    SP_DT = [O_R3 + 2 * TC, O_R3 + 3 * TC]
    SP_SG = [O_R3 + 4 * TC, O_R3 + 5 * TC]
    stb = fview(O_R3 + 6 * TC, TC)
    cdtb = stb
    gb = fview(O_R3 + 7 * TC, TC)
    ubft = [bview(O_R3 + 6 * TC, TC), bview(O_R3 + 6 * TC + TC // 2, TC)]

    wstream = bview(O_R4, 8192)

    a5 = Alloc(O_R5)
    scrF = fview(a5.take(2 * TW), 2 * TW)
    scrH = bview(a5.take(TW // 2), TW)
    _o8 = a5.take(TW // 4)
    q8t = ar8[:, 4 * _o8:4 * _o8 + TW]
    amt = fview(a5.take(64), 64)
    _oi = a5.take(192)
    set_ = iview(_oi, 64)
    invt = iview(_oi + 64, 64)
    sct = iview(_oi + 128, 64)
    assert TW // 32 <= 64

    lnbc = fview(O_R6, 2 * TW)
    csb = fview(O_R6, 2 * TC)
    agb = csb

    a7 = Alloc(O_R7)
    ssm_dt = fview(O_R6, NH * TC)
    ssm_bc = arena[0:32, O_R5:O_R5 + NH * TC]
    yb = fview(O_R4 + 1024, TC)
    argb = yb
    dumpb = fview(O_R4 + 2048, TC)
    _od = a7.take(NH * TC // 2)
    dtin_bf = arbf[:, 2 * _od:2 * _od + NH * TC]
    ibuf = fview(_od, TC)
    _obc = a7.take(NH * TC // 2)
    bcrow_bf = arbf[0:32, 2 * _obc:2 * _obc + NH * TC]
    evst = fview(a7.take(TC), TC)
    wx_cols = NH * DTH * (RK + 2 * NS)
    _owx = a7.take((wx_cols + 1) // 2)
    wxbf = arbf[:, 2 * _owx:2 * _owx + wx_cols]

    def semh(name):
        return Sem(st.enter_context(nc.semaphore(name)), name)
    dL = semh("dL")
    dW = semh("dW")
    dU = semh("dU")
    dR = semh("dR")
    dO = semh("dO")
    sT = semh("sT")
    sV = semh("sV")
    sS = semh("sS")
    sG = semh("sG")
    sCC = semh("sCC")
    dWG = semh("dWG")

    kb = KB(nc)
    E = kb.emit
    W = kb.wait

    wsmark = {0: 0, 4096: 0}

    def wslab(buf, dst_off, nblk, src_elem_off, wait_mark=True, wagt=None):
        if wait_mark:
            W("sync", sT, wsmark[buf])
        wsrc = wagt if wagt is not None else wag1
        dview = wsrc[src_elem_off:src_elem_off + nblk * TB].rearrange(
            "(b r q) -> r b q", r=128, q=128)
        sview = wstream[:, buf + dst_off:buf + dst_off + nblk * 128].rearrange(
            "p (b q) -> p b q", q=128)
        E("sync", lambda api, d=dview, s=sview: api.dma_start(out=s, in_=d), inc=dW, inck=16)

    def dma(eng, out, in_, sem):
        E(eng, lambda api, o=out, i=in_: api.dma_start(out=o, in_=i), inc=sem, inck=16)

    def r32(ap, nblk):
        return ap.rearrange("p (b k) -> p b k", k=32)

    def bc32(intap, nblk):
        return intap[:, 0:nblk].rearrange("p (b o) -> p b o", o=1).bitcast(F32).broadcast_to([P, nblk, 32])

    dma("sync", cw_sb.rearrange("p (t k) -> p t k", k=4), d_cw.rearrange("t p k -> p t k"), dL)
    dma("sync", cb_sb, d_cb, dL)
    dma("sync", dtb_sb, d_dtb, dL)
    dma("sync", Dp_sb, d_Dp, dL)
    dma("sync", posA.rearrange("p (t n) -> p t n", n=NS), d_alog.rearrange("t p n -> p t n"), dL)
    dma("sync", identb, d_iden, dL)
    dma("sync", identf, d_idenf, dL)
    dma("sync", pmask_sb, d_pmask, dL)
    dma("sync", cmask_sb, d_cmask, dL)
    base_load_n = dL.n
    W("scalar", dL, base_load_n)
    E("scalar", lambda api: api.activation(out=posA, in_=posA, func=Act.Exp), inc=sS)
    W("vector", sS)
    E("vector", lambda api: api.tensor_scalar(out=negA, in0=posA, scalar1=-1.0,
                                              scalar2=None, op0=Alu.mult), inc=sV)
    E("vector", lambda api: api.memset(zcol, 0.0), inc=sV)
    E("vector", lambda api: api.memset(onesb, 1.0), inc=sV)
    E("vector", lambda api: api.memset(xqbf[:, 0:1024], 0.0), inc=sV)
    E("vector", lambda api: api.memset(amt[:, 0:64], 0.0), inc=sV)
    W("sync", sV)
    dma("sync", d_pack2[PACK2 - 8 * TB:PACK2].rearrange("(p q) -> p q", p=128),
        xqbf[:, 0:1024], dO)

    WPW = 2048
    ap1 = Alloc(O_R1)
    wp_in = fview(ap1.take(WPW), WPW)
    wp_z = fview(ap1.take(WPW), WPW)
    wp_sgn = fview(ap1.take(WPW), WPW)
    wp_acc = fview(ap1.take(WPW), WPW)
    wp_tmp = fview(ap1.take(WPW), WPW)
    _oq = ap1.take(WPW // 2)
    wp_q = arbf[:, 2 * _oq:2 * _oq + WPW]
    _oe = ap1.take(WPW // 2)
    wp_ev = arbf[:, 2 * _oe:2 * _oe + WPW]
    wp_am = fview(ap1.take(WPW // 32), WPW // 32)
    _wo = ap1.take(3 * (WPW // 32))
    wp_se = iview(_wo, WPW // 32)
    wp_inv = iview(_wo + WPW // 32, WPW // 32)
    wp_sc = iview(_wo + 2 * (WPW // 32), WPW // 32)
    assert ap1.off <= O_R1 + max(HT * TW, NS * TC, 5 * 2048 + 2 * 1024 + 256)
    psTr_cm = nc.psum_tensor("psTr", [P, 2048], BF16)
    psTr = psTr_cm.__enter__()

    def quantize_weights(ncols):
        nblk = ncols // 32
        W("vector", sT)
        E("vector", lambda api, ncols=ncols, nblk=nblk: api.tensor_reduce(
            out=wp_am[:, 0:nblk], in_=r32(wp_in[:, 0:ncols], nblk),
            axis=AxX, op=Alu.max, apply_absolute_value=True))
        E("vector", lambda api, nblk=nblk: api.tensor_scalar(out=wp_se[:, 0:nblk], in0=wp_am[:, 0:nblk].bitcast(I32),
                                                             scalar1=23, scalar2=None, op0=Alu.logical_shift_right))
        E("vector", lambda api, nblk=nblk: api.tensor_scalar(out=wp_se[:, 0:nblk], in0=wp_se[:, 0:nblk],
                                                             scalar1=-2, scalar2=0, op0=Alu.add, op1=Alu.max))
        E("vector", lambda api, nblk=nblk: api.tensor_scalar(out=wp_se[:, 0:nblk], in0=wp_se[:, 0:nblk],
                                                             scalar1=254, scalar2=None, op0=Alu.min))
        E("vector", lambda api, nblk=nblk: api.tensor_scalar(out=wp_inv[:, 0:nblk], in0=wp_se[:, 0:nblk],
                                                             scalar1=-1, scalar2=254, op0=Alu.mult, op1=Alu.add))
        E("vector", lambda api, nblk=nblk: api.tensor_scalar(out=wp_inv[:, 0:nblk], in0=wp_inv[:, 0:nblk],
                                                             scalar1=23, scalar2=None, op0=Alu.logical_shift_left))
        E("vector", lambda api, nblk=nblk: api.tensor_scalar(out=wp_sc[:, 0:nblk], in0=wp_se[:, 0:nblk],
                                                             scalar1=23, scalar2=None, op0=Alu.logical_shift_left))
        E("vector", lambda api, ncols=ncols, nblk=nblk: api.tensor_tensor(
            out=r32(wp_tmp[:, 0:ncols], nblk), in0=r32(wp_in[:, 0:ncols], nblk),
            in1=bc32(wp_inv, nblk), op=Alu.mult), inc=sV)
        W("scalar", sV)
        E("scalar", lambda api, ncols=ncols: api.activation(out=wp_z[:, 0:ncols], in_=wp_tmp[:, 0:ncols],
                                                            func=Act.Abs), inc=sS)
        E("vector", lambda api, ncols=ncols: api.tensor_scalar(out=wp_sgn[:, 0:ncols], in0=wp_tmp[:, 0:ncols],
                                                               scalar1=0.0, scalar2=2.0, op0=Alu.is_ge, op1=Alu.mult))
        E("vector", lambda api, ncols=ncols: api.tensor_scalar(out=wp_sgn[:, 0:ncols], in0=wp_sgn[:, 0:ncols],
                                                               scalar1=-1.0, scalar2=None, op0=Alu.add))
        W("vector", sS)
        th = [(0.25, 0.5), (0.75, 0.5), (1.25, 0.5), (1.75, 0.5), (2.5, 1.0), (3.5, 1.0), (5.0, 2.0)]
        E("vector", lambda api, ncols=ncols: api.tensor_scalar(out=wp_acc[:, 0:ncols], in0=wp_z[:, 0:ncols],
                                                               scalar1=th[0][0], scalar2=th[0][1],
                                                               op0=Alu.is_ge, op1=Alu.mult))
        for cc_, ww_ in th[1:]:
            E("vector", lambda api, cc_=cc_, ww_=ww_, ncols=ncols: api.tensor_scalar(
                out=wp_tmp[:, 0:ncols], in0=wp_z[:, 0:ncols],
                scalar1=cc_, scalar2=ww_, op0=Alu.is_ge, op1=Alu.mult))
            E("vector", lambda api, ncols=ncols: api.tensor_tensor(out=wp_acc[:, 0:ncols], in0=wp_acc[:, 0:ncols],
                                                                   in1=wp_tmp[:, 0:ncols], op=Alu.add))
        E("vector", lambda api, ncols=ncols: api.tensor_tensor(out=wp_acc[:, 0:ncols], in0=wp_acc[:, 0:ncols],
                                                               in1=wp_sgn[:, 0:ncols], op=Alu.mult))
        E("vector", lambda api, ncols=ncols, nblk=nblk: api.tensor_tensor(
            out=r32(wp_q[:, 0:ncols], nblk), in0=r32(wp_acc[:, 0:ncols], nblk),
            in1=bc32(wp_sc, nblk), op=Alu.mult), inc=sV)

    W("tensor", dL, base_load_n)

    def prep_matrix(src, nrows, ncols, quant, blk_base, blk_order, transpose=True, packt=None):
        nch = nrows // 128
        ncc = (ncols + WPW - 1) // WPW
        for ch in range(nch):
            for cc in range(ncc):
                c0 = cc * WPW
                cn = min(WPW, ncols - c0)
                nct = cn // 128
                W("sync", sV)
                W("sync", sS)
                dma("sync", wp_in[:, 0:cn], src[ch * 128:(ch + 1) * 128, c0:c0 + cn], dW)
                W("vector", dW)
                if quant:
                    quantize_weights(cn)
                else:
                    W("vector", sT)
                    kb.inc("vector", sV)
                    W("scalar", sV)
                    E("scalar", lambda api, cn=cn: api.copy(out=wp_q[:, 0:cn], in_=wp_in[:, 0:cn]), inc=sS)
                    W("vector", sS)
                    kb.inc("vector", sV)
                if transpose:
                    W("tensor", sV)
                    for j0 in range(0, nct, 16):
                        jn = min(16, nct - j0)
                        W("tensor", sS)
                        for j in range(j0, j0 + jn):
                            E("tensor", lambda api, j=j, j0=j0: api.transpose(
                                psTr[:, (j - j0) * 128:(j - j0 + 1) * 128],
                                wp_q[:, j * 128:(j + 1) * 128], identb), inc=sT)
                        W("scalar", sT)
                        E("scalar", lambda api, jn=jn, j0=j0: api.copy(
                            out=wp_ev[:, j0 * 128:(j0 + jn) * 128],
                            in_=psTr[:, 0:jn * 128]), inc=sS)
                    W("sync", sS)
                    evsrc = wp_ev
                else:
                    W("sync", sS)
                    W("sync", sV)
                    evsrc = wp_q
                idxs = [blk_order(ch, (c0 // 128) + j) for j in range(nct)]
                stride = (idxs[1] - idxs[0]) if nct > 1 else 1
                for j in range(nct - 1):
                    assert idxs[j + 1] - idxs[j] == stride
                base_off = blk_base + idxs[0] * TB
                pk = packt if packt is not None else d_pack1
                if stride > 1:
                    dview = pk[base_off:base_off + nct * stride * TB]
                    dview = dview.rearrange("(b s r q) -> r b s q", s=stride, r=128, q=128)[:, :, 0, :]
                else:
                    dview = pk[base_off:base_off + nct * TB].rearrange("(b r q) -> r b q", r=128, q=128)
                sview = evsrc[:, 0:nct * 128].rearrange("p (b q) -> p b q", q=128)
                dma("sync", dview, sview, dO)

    prep_matrix(d_wg, c.IFF // c.NC, c.H, True, base_gate, lambda ch, j: ch * HT + j)
    prep_matrix(d_wu, c.IFF // c.NC, c.H, True, base_up, lambda ch, j: ch * HT + j)
    W("gpsimd", dO)
    E("gpsimd", lambda api: api.collective_compute(
        "AllGather", Alu.bypass, replica_groups=[list(range(c.NC))],
        ins=[d_pack1_h.ap()], outs=[d_wag1_h.ap()]), inc=sCC)
    gu_ag = sCC.n
    prep_matrix(d_wd, c.H // c.NC, c.IFF, True, base2_down, lambda ch, j: ch * FT + j, packt=d_pack2)
    prep_matrix(d_wi, c.H // c.NC, 2 * c.IP, False, base2_win, lambda ch, j: j * HPC + ch, transpose=False, packt=d_pack2)
    prep_matrix(d_wo, c.IP // c.NC, c.H, False, base2_wout, lambda ch, j: j * OPC + ch, transpose=False, packt=d_pack2)

    RW = RK + 2 * NS
    CHT = max(1, 2048 // RW)
    for h_ in range(NH):
        for t0_ in range(0, DTH, CHT):
            tn_ = min(CHT, DTH - t0_)
            W("sync", sS)
            W("sync", sV)
            dma("sync", wp_in[:, 0:tn_ * RW].rearrange("p (t r) -> p t r", r=RW),
                d_wx.rearrange("h (t p) r -> p h t r", p=P)[:, h_, t0_:t0_ + tn_], dL)
            W("scalar", dL)
            woff = (h_ * DTH + t0_) * RW
            E("scalar", lambda api, woff=woff, tn_=tn_: api.copy(
                out=wxbf[:, woff:woff + tn_ * RW], in_=wp_in[:, 0:tn_ * RW]), inc=sS)

    W("gpsimd", dO)
    E("gpsimd", lambda api: api.collective_compute(
        "AllGather", Alu.bypass, replica_groups=[list(range(c.NC))],
        ins=[d_pack2_h.ap()], outs=[d_wag2_h.ap()]), inc=sCC)

    prep_T, prep_V, prep_S = sT.n, sV.n, sS.n
    for eng in KB.ENGINES:
        W(eng, sT, prep_T)
        W(eng, sV, prep_V)
        W(eng, sS, prep_S)
    psTr_cm.__exit__(None, None, None)

    psA = st.enter_context(nc.psum_tensor("psA", [P, 1536], F32))
    psB = st.enter_context(nc.psum_tensor("psB", [P, 1536], F32))
    psC = st.enter_context(nc.psum_tensor("psC", [P, 1024], F32))
    psmark = {"A": (sV, 0), "B": (sV, 0), "C": (sV, 0)}

    def pswait_t(name):
        sem, val = psmark[name]
        W("tensor", sem, val)

    dma("sync", xres.rearrange("p (t w) -> p t w", w=TW),
        d_x.rearrange("(t p) w -> p t w", p=P), dL)
    xload_mark = dL.n
    W("vector", dL, xload_mark)
    W("scalar", dL, xload_mark)
    W("tensor", dL, xload_mark)

    def ln_and(width, col0, out_consumer):
        inv_n = 1.0 / c.H
        pswait_t("A")
        pswait_t("B")
        W("scalar", sV)
        cast_mark = {}
        xcast = [bview(O_R3, TW), bview(O_R3 + TW // 2, TW)]
        qcast = [bview(O_R3 + TW, TW), bview(O_R3 + 3 * (TW // 2), TW)]
        for t in range(HT):
            xa = xres[:, t * TW + col0: t * TW + col0 + width]
            pb = t % 2
            if t >= 2:
                W("scalar", sT, cast_mark[t - 2])
            E("scalar", lambda api, xa=xa, pb=pb, width=width: api.copy(
                out=xcast[pb][:, 0:width], in_=xa), inc=sS)
            E("scalar", lambda api, xa=xa, pb=pb, width=width: api.activation(
                out=qcast[pb][:, 0:width], in_=xa, func=Act.Square), inc=sS)
            W("tensor", sS)
            for (cc0, cn) in chunks_of(width):
                E("tensor", lambda api, cc0=cc0, cn=cn, t=t, pb=pb: api.matmul(
                    psA[0:1, cc0:cc0 + cn], onesb, xcast[pb][:, cc0:cc0 + cn],
                    start=(t == 0), stop=(t == HT - 1)), inc=sT)
                E("tensor", lambda api, cc0=cc0, cn=cn, t=t, pb=pb: api.matmul(
                    psB[0:1, cc0:cc0 + cn], onesb, qcast[pb][:, cc0:cc0 + cn],
                    start=(t == 0), stop=(t == HT - 1)), inc=sT)
            cast_mark[t] = sT.n
        W("vector", sT)
        m_ = scrF[0:1, 0:width]
        v_ = scrF[0:1, TW:TW + width]
        t_ = lnbc[0:1, 0:width]
        r_ = lnbc[0:1, TW:TW + width]
        E("vector", lambda api, m_=m_, width=width: api.tensor_scalar(out=m_, in0=psA[0:1, 0:width],
                                                                     scalar1=inv_n, scalar2=None, op0=Alu.mult))
        E("vector", lambda api, v_=v_, width=width: api.tensor_scalar(out=v_, in0=psB[0:1, 0:width],
                                                                      scalar1=inv_n, scalar2=None, op0=Alu.mult))
        E("vector", lambda api, t_=t_, m_=m_: api.tensor_tensor(out=t_, in0=m_, in1=m_, op=Alu.mult))
        E("vector", lambda api, v_=v_, t_=t_: api.tensor_tensor(out=v_, in0=v_, in1=t_, op=Alu.subtract))
        E("vector", lambda api, v_=v_: api.tensor_scalar(out=v_, in0=v_,
                                                         scalar1=1e-5, scalar2=None, op0=Alu.add), inc=sV)
        psmark["A"] = (sV, sV.n)
        psmark["B"] = (sV, sV.n)
        W("scalar", sV)
        E("scalar", lambda api, t_=t_, v_=v_: api.activation(out=t_, in_=v_, func=Act.Ln))
        E("scalar", lambda api, r_=r_, t_=t_: api.activation(out=r_, in_=t_, func=Act.Exp, scale=-0.5), inc=sS)
        W("vector", sS)
        E("vector", lambda api, t_=t_, m_=m_, r_=r_: api.tensor_tensor(out=t_, in0=m_, in1=r_, op=Alu.mult))
        E("vector", lambda api, m_=m_, t_=t_: api.tensor_scalar(out=m_, in0=t_,
                                                                scalar1=-1.0, scalar2=None, op0=Alu.mult), inc=sV)
        W("sync", sV)
        dma("sync", d_scr_ln[0:1, 0:width], r_, dU)
        dma("sync", d_scr_ln[1:2, 0:width], m_, dU)
        W("sync", dU)
        dma("sync", lnbc[:, 0:width], d_scr_ln[0:1, 0:width].broadcast_to([P, width]), dR)
        dma("sync", lnbc[:, TW:TW + width], d_scr_ln[1:2, 0:width].broadcast_to([P, width]), dR)
        W("vector", dR)
        for t in range(HT):
            xa = xres[:, t * TW + col0: t * TW + col0 + width]
            E("vector", lambda api, xa=xa, width=width: api.tensor_tensor(
                out=scrF[:, 0:width], in0=xa, in1=lnbc[:, 0:width], op=Alu.mult))
            E("vector", lambda api, width=width: api.tensor_tensor(
                out=scrF[:, TW:TW + width], in0=scrF[:, 0:width], in1=lnbc[:, TW:TW + width],
                op=Alu.add), inc=sV)
            out_consumer(t, scrF[:, TW:TW + width], width)

    def quant_act_tile(xn_ap, width, out_bf_ap):
        assert width == TC
        nblk = width // 32
        NCH = width // 128
        Y = scrF[:, 0:width]
        Y6 = scrF[:, TW:TW + width]
        pswait_t("C")
        W("tensor", sV)
        for j in range(NCH):
            E("tensor", lambda api, j=j, xn_ap=xn_ap: api.transpose(
                psC[:, j * 128:(j + 1) * 128],
                xn_ap[:, j * 128:(j + 1) * 128], identf), inc=sT)
        quant_act_tile.qin_mark = sT.n
        W("scalar", sT)
        E("scalar", lambda api, Y=Y, width=width: api.copy(out=Y, in_=psC[:, 0:width]), inc=sS)
        psmark["C"] = (sS, sS.n)
        W("vector", sS)
        E("vector", lambda api, Y=Y, nblk=nblk: api.tensor_reduce(out=amt[:, 0:nblk], in_=r32(Y, nblk),
                                                                  axis=AxX, op=Alu.max,
                                                                  apply_absolute_value=True), inc=sV)
        W("vector", sV)
        E("vector", lambda api: api.tensor_scalar(out=set_[:, 0:64], in0=amt[:, 0:64].bitcast(I32),
                                                  scalar1=23, scalar2=None, op0=Alu.logical_shift_right))
        E("vector", lambda api: api.tensor_scalar(out=set_[:, 0:64], in0=set_[:, 0:64],
                                                  scalar1=-2, scalar2=0, op0=Alu.add, op1=Alu.max))
        E("vector", lambda api: api.tensor_scalar(out=invt[:, 0:64], in0=set_[:, 0:64],
                                                  scalar1=-1, scalar2=248, op0=Alu.mult, op1=Alu.add))
        E("vector", lambda api: api.tensor_scalar(out=invt[:, 0:64], in0=invt[:, 0:64],
                                                  scalar1=23, scalar2=None, op0=Alu.logical_shift_left))
        E("vector", lambda api: api.tensor_scalar(out=sct[:, 0:64], in0=set_[:, 0:64],
                                                  scalar1=6, scalar2=None, op0=Alu.add))
        E("vector", lambda api: api.tensor_scalar(out=sct[:, 0:64], in0=sct[:, 0:64],
                                                  scalar1=23, scalar2=None, op0=Alu.logical_shift_left), inc=sV)
        W("vector", sV)
        E("vector", lambda api, Y=Y, Y6=Y6, nblk=nblk: api.tensor_tensor(
            out=r32(Y6, nblk), in0=r32(Y, nblk), in1=bc32(invt, nblk), op=Alu.mult))
        M = 7.5 * 2.0 ** -6
        E("vector", lambda api, Y6=Y6: api.tensor_scalar(out=Y6, in0=Y6, scalar1=M, scalar2=-M,
                                                         op0=Alu.min, op1=Alu.max), inc=sV)
        W("scalar", sV)
        E("scalar", lambda api, Y6=Y6, width=width: api.copy(out=q8t[:, 0:width], in_=Y6), inc=sS)
        W("vector", sS)
        E("vector", lambda api, Y=Y, width=width, nblk=nblk: api.tensor_tensor(
            out=r32(Y, nblk), in0=r32(q8t[:, 0:width], nblk),
            in1=bc32(sct, nblk), op=Alu.mult), inc=sV)
        pswait_t("C")
        W("tensor", sV)
        for j in range(NCH):
            E("tensor", lambda api, j=j, Y=Y: api.transpose(
                psC[:, j * 128:(j + 1) * 128],
                Y[:, j * 128:(j + 1) * 128], identf), inc=sT)
        W("scalar", sT)
        E("scalar", lambda api, width=width, out_bf_ap=out_bf_ap: api.copy(
            out=out_bf_ap, in_=psC[:, 0:width]), inc=sS)
        psmark["C"] = (sS, sS.n)
        W("vector", sS)
        kb.inc("vector", sV)

    def ffn(col0, width):
        nch = chunks_of(width)

        def consume(t, xn_ap, w_):
            quant_act_tile(xn_ap, w_, xqbf[:, t * TW:t * TW + w_])
        ln_and(width, col0, consume)
        xq_done = sV.n
        if c.debug and col0 == 0:
            W("sync", sV, xq_done)
            dma("sync", d_dbgxq, xqbf[:, 0:HT * TW], dU)
        for q in range(c.NQ):
            for oj in range(c.QT):
                og = q * c.QT + oj
                rank = og // RPC
                olocal = og % RPC
                gbase = rank * PACK1 + base_gate + (olocal * HT) * TB
                ubase = rank * PACK1 + base_up + (olocal * HT) * TB
                buf = (og % 2) * 4096
                wslab(buf, 0, HT, gbase)
                wslab(buf, HT * 128, HT, ubase, wait_mark=False)
                W("tensor", dW)
                W("tensor", sV, xq_done)
                pswait_t("A")
                pswait_t("B")
                for k in range(HT):
                    for (cc0, cn) in nch:
                        E("tensor", lambda api, k=k, cc0=cc0, cn=cn, buf=buf: api.matmul(
                            psA[:, cc0:cc0 + cn], wstream[:, buf + k * 128:buf + (k + 1) * 128],
                            xqbf[:, k * TW + cc0:k * TW + cc0 + cn],
                            start=(k == 0), stop=(k == HT - 1)), inc=sT)
                    for (cc0, cn) in nch:
                        E("tensor", lambda api, k=k, cc0=cc0, cn=cn, buf=buf: api.matmul(
                            psB[:, cc0:cc0 + cn], wstream[:, buf + (HT + k) * 128:buf + (HT + k + 1) * 128],
                            xqbf[:, k * TW + cc0:k * TW + cc0 + cn],
                            start=(k == 0), stop=(k == HT - 1)), inc=sT)
                wsmark[buf] = sT.n
                W("scalar", sT)
                E("scalar", lambda api, width=width: api.activation(
                    out=evst[:, 0:width], in_=psA[:, 0:width], func=Act.Silu), inc=sS)
                W("vector", sS)
                W("vector", sT, getattr(quant_act_tile, "qin_mark", 0))
                E("vector", lambda api, width=width: api.tensor_tensor(
                    out=ibuf[:, 0:width], in0=evst[:, 0:width], in1=psB[:, 0:width],
                    op=Alu.mult), inc=sV)
                psmark["A"] = (sV, sV.n)
                psmark["B"] = (sV, sV.n)
                quant_act_tile(ibuf[:, 0:width], width, iqbf[:, oj * TW:oj * TW + width])
            iq_done = sV.n
            W("sync", sCC)
            if c.debug and col0 == 0 and q == 0:
                W("sync", sV, iq_done)
                dma("sync", d_dbgiq, iqbf[:, 0:c.QT * TW], dU)
            for m in range(HT):
                rank = m // HPC
                mlocal = m % HPC
                dbase = rank * PACK2 + base2_down + (mlocal * FT + q * c.QT) * TB
                buf = (m % 2) * 4096
                wslab(buf, 0, c.QT, dbase, wagt=wag2)
                W("tensor", dW)
                W("tensor", sV, iq_done)
                ppname = "A" if (m % 2 == 0) else "B"
                pp = psA if (m % 2 == 0) else psB
                pswait_t(ppname)
                for k in range(c.QT):
                    for (cc0, cn) in nch:
                        E("tensor", lambda api, k=k, cc0=cc0, cn=cn, buf=buf, pp=pp: api.matmul(
                            pp[:, cc0:cc0 + cn], wstream[:, buf + k * 128:buf + (k + 1) * 128],
                            iqbf[:, k * TW + cc0:k * TW + cc0 + cn],
                            start=(k == 0), stop=(k == c.QT - 1)), inc=sT)
                wsmark[buf] = sT.n
                W("vector", sT)
                xa = xres[:, m * TW + col0: m * TW + col0 + width]
                E("vector", lambda api, pp=pp, xa=xa, width=width: api.scalar_tensor_tensor(
                    out=xa, in0=pp[:, 0:width], scalar=0.5, in1=xa, op0=Alu.mult, op1=Alu.add), inc=sV)
                psmark[ppname] = (sV, sV.n)

    kb.gu_ag = gu_ag
    W("sync", sCC, gu_ag)
    W("tensor", sCC, gu_ag)
    ffn(0, TW)

    def consume_ln2(t, xn_ap, w_):
        W("scalar", sV)
        E("scalar", lambda api, t=t, xn_ap=xn_ap, w_=w_: api.copy(
            out=xqbf[:, t * TW:t * TW + w_], in_=xn_ap), inc=sS)
        W("vector", sS)
    ln_and(TW, 0, consume_ln2)
    xn2_done = sS.n
    if c.debug:
        W("sync", sS)
        dma("sync", d_dbgxn2, xqbf[:, 0:HT * TW], dU)
        for rr in range(8):
            off = rr * PACK2 + base2_win
            dma("sync", d_dbgwin[rr],
                wag2[off:off + TB].rearrange("(r q) -> r q", r=128), dU)
    W("sync", sV)
    W("sync", sS)
    dma("sync", d_x1.rearrange("(t p) w -> p t w", p=P),
        xres.rearrange("p (t w) -> p t w", w=TW), dU)
    x1_spilled = dU.n

    E("vector", lambda api: api.memset(ssm_dt, 0.0))
    E("vector", lambda api: api.memset(ssm_bc, 0.0), inc=sV)

    nchW = chunks_of(TW)
    nchC = chunks_of(TC)
    ip_xp_mark = {}
    ip_spill_mark = {}
    for ot in range(2 * DT):
        buf = (ot % 2) * 4096
        for r in range(c.NC):
            wb_ = r * PACK2 + base2_win + (ot * HPC) * TB
            wslab(buf, r * HPC * 128, HPC, wb_, wait_mark=(r == 0), wagt=wag2)
        W("tensor", dW)
        W("tensor", sS, xn2_done)
        ppname = "A" if (ot % 2 == 0) else "B"
        pp = psA if (ot % 2 == 0) else psB
        pswait_t(ppname)
        for k in range(HT):
            for (cc0, cn) in nchW:
                E("tensor", lambda api, k=k, cc0=cc0, cn=cn, buf=buf, pp=pp: api.matmul(
                    pp[:, cc0:cc0 + cn], wstream[:, buf + k * 128:buf + (k + 1) * 128],
                    xqbf[:, k * TW + cc0:k * TW + cc0 + cn],
                    start=(k == 0), stop=(k == HT - 1)), inc=sT)
        wsmark[buf] = sT.n
        if ot < DT:
            dtile = ot
            W("scalar", sT)
            if ot >= 2:
                W("scalar", dU, ip_spill_mark.get(ot - 2, 0))
            xcb = SP_U[ot % 2]
            E("scalar", lambda api, pp=pp, xcb=xcb: api.copy(out=fview(xcb, TC), in_=pp[:, 0:TC]), inc=sS)
            psmark[ppname] = (sS, sS.n)
            E("scalar", lambda api, xcb=xcb, dtile=dtile: api.copy(
                out=xc3_sb[:, dtile * 3:dtile * 3 + 3], in_=fview(xcb, TC)[:, TC - 3:TC]), inc=sS)
            W("sync", sS)
            dma("sync", d_scr_xc[dtile], fview(xcb, TC), dU)
            ip_spill_mark[ot] = dU.n
            if c.debug:
                dma("sync", d_dbg3[dtile], fview(xcb, TC), dU)
        else:
            dtile = ot - DT
            W("scalar", sT)
            if ot >= 2:
                W("scalar", dU, ip_spill_mark.get(ot - 2, 0))
            sgb = SP_SG[ot % 2]
            E("scalar", lambda api, pp=pp, sgb=sgb: api.activation(
                out=fview(sgb, TC), in_=pp[:, HALO:HALO + TC], func=Act.Silu), inc=sS)
            psmark[ppname] = (sS, sS.n)
            W("sync", sS)
            dma("sync", d_scr_sg[dtile], fview(sgb, TC), dU)
            ip_spill_mark[ot] = dU.n

    W("sync", sS)
    dma("sync", d_cc3in, xc3_sb, dU)
    W("gpsimd", dU)
    E("gpsimd", lambda api: api.collective_compute(
        "AllGather", Alu.bypass, replica_groups=[list(range(c.NC))],
        ins=[d_cc3in_h.ap().rearrange("p q -> (p q)")],
        outs=[d_cc3out_h.ap().rearrange("n p q -> (n p q)")]), inc=sCC)
    cc3_done = sCC.n
    W("vector", sCC, cc3_done)
    E("vector", lambda api: api.memset(b3_sb, 0.0), inc=sV)
    for j in range(c.NC):
        W("sync", sV)
        dma("sync", gb[:, 0:DT * 3], d_cc3out[j], dR)
        W("vector", dR)
        E("vector", lambda api, j=j: api.tensor_scalar(
            out=gb[:, 0:DT * 3], in0=gb[:, 0:DT * 3],
            scalar1=cmask_sb[:, j:j + 1], scalar2=None, op0=Alu.mult))
        E("vector", lambda api: api.tensor_tensor(out=b3_sb, in0=b3_sb, in1=gb[:, 0:DT * 3],
                                                  op=Alu.add), inc=sV)

    cv_spill = {}
    cv_v = {}
    for dtile in range(DT):
        head = dtile // DTH
        xcb = SP_U[dtile % 2]
        ub = SP_DT[dtile % 2]
        if dtile >= 2:
            W("sync", sV, cv_v.get(dtile - 2, 0))
            W("sync", dU, cv_spill.get(dtile - 2, 0))
        dma("sync", fview(xcb, TC), d_scr_xc[dtile], dR)
        W("vector", dR)
        E("vector", lambda api, xcb=xcb, dtile=dtile: api.tensor_scalar(
            out=gb[:, 3:TC], in0=fview(xcb, TC)[:, 0:TC - 3],
            scalar1=cw_sb[:, dtile * 4:dtile * 4 + 1], scalar2=None, op0=Alu.mult))
        for kk in range(1, 4):
            E("vector", lambda api, xcb=xcb, dtile=dtile, kk=kk: api.scalar_tensor_tensor(
                out=gb[:, 3:TC], in0=fview(xcb, TC)[:, kk:kk + TC - 3],
                scalar=cw_sb[:, dtile * 4 + kk:dtile * 4 + kk + 1], in1=gb[:, 3:TC],
                op0=Alu.mult, op1=Alu.add))
        E("vector", lambda api, dtile=dtile: api.tensor_copy(out=ext6[:, 0:3],
                                                             in_=b3_sb[:, dtile * 3:dtile * 3 + 3]))
        E("vector", lambda api, xcb=xcb: api.tensor_copy(out=ext6[:, 3:6],
                                                         in_=fview(xcb, TC)[:, 0:3]), inc=sV)
        W("vector", sV)
        E("vector", lambda api, dtile=dtile: api.tensor_scalar(
            out=gb[:, 0:3], in0=ext6[:, 0:3],
            scalar1=cw_sb[:, dtile * 4:dtile * 4 + 1], scalar2=None, op0=Alu.mult))
        for kk in range(1, 4):
            E("vector", lambda api, dtile=dtile, kk=kk: api.scalar_tensor_tensor(
                out=gb[:, 0:3], in0=ext6[:, kk:kk + 3],
                scalar=cw_sb[:, dtile * 4 + kk:dtile * 4 + kk + 1], in1=gb[:, 0:3],
                op0=Alu.mult, op1=Alu.add), inc=(sV if kk == 3 else None))
        cv_v[dtile] = sV.n
        W("scalar", sV)
        E("scalar", lambda api, ub=ub, dtile=dtile: api.activation(
            out=fview(ub, TC), in_=gb, func=Act.Silu,
            bias=cb_sb[:, dtile:dtile + 1]), inc=sS)
        E("scalar", lambda api, ub=ub, uu=ubft[dtile % 2]: api.copy(out=uu, in_=fview(ub, TC)), inc=sS)
        W("sync", sS)
        dma("sync", d_scr_u[dtile], fview(ub, TC), dU)
        cv_spill[dtile] = dU.n
        W("tensor", sS)
        if dtile >= 2:
            W("tensor", sS)
        wxoff = (head * DTH + (dtile % DTH)) * (RK + 2 * NS)
        pswait_t("C")
        for (cc0, cn) in nchC:
            E("tensor", lambda api, cc0=cc0, cn=cn, wxoff=wxoff, uu=ubft[dtile % 2]: api.matmul(
                psC[:, cc0:cc0 + cn], wxbf[:, wxoff:wxoff + RK],
                uu[:, cc0:cc0 + cn], start=True, stop=True), inc=sT)
        xp_t1 = sT.n
        W("vector", sT, xp_t1)
        E("vector", lambda api, head=head: api.tensor_tensor(
            out=ssm_dt[:, head * TC:(head + 1) * TC], in0=ssm_dt[:, head * TC:(head + 1) * TC],
            in1=psC[:, 0:TC], op=Alu.add), inc=sV)
        psmark["C"] = (sV, sV.n)
        pswait_t("C")
        for (cc0, cn) in nchC:
            E("tensor", lambda api, cc0=cc0, cn=cn, wxoff=wxoff, uu=ubft[dtile % 2]: api.matmul(
                psC[0:32, cc0:cc0 + cn], wxbf[:, wxoff + RK:wxoff + RK + 2 * NS],
                uu[:, cc0:cc0 + cn], start=True, stop=True), inc=sT)
        W("vector", sT)
        E("vector", lambda api, head=head: api.tensor_tensor(
            out=ssm_bc[:, head * TC:(head + 1) * TC], in0=ssm_bc[:, head * TC:(head + 1) * TC],
            in1=psC[0:32, 0:TC], op=Alu.add), inc=sV)
        psmark["C"] = (sV, sV.n)
        cv_v[dtile] = sV.n

    W("scalar", sV)
    E("scalar", lambda api: api.copy(out=dtin_bf, in_=ssm_dt), inc=sS)
    E("scalar", lambda api: api.copy(out=bcrow_bf, in_=ssm_bc), inc=sS)
    dtin_mark = sS.n
    W("sync", sS)
    dma("sync", d_scr_bc.rearrange("(h r) w -> r h w", h=NH),
        bcrow_bf.rearrange("p (h w) -> p h w", h=NH), dU)
    bc_spill_mark = dU.n

    ph1_v_mark = {}
    ph1_spill_mark = {}
    for head in range(NH):
        W("sync", dU, bc_spill_mark)
        W("sync", sT)
        W("sync", sV)
        if head == 0:
            W("sync", dU, x1_spilled)
        for n in range(NS):
            dma("sync", brep[:, n * TC:(n + 1) * TC],
                d_scr_bc[head * 2 * NS + n:head * 2 * NS + n + 1, :].broadcast_to([P, TC]), dR)
        brep_done = dR.n
        W("gpsimd", sT, wsmark[0])
        dma("gpsimd", wstream[:, 0:DTH * 128].rearrange("p (t q) -> p t q", q=128),
            d_wdt.rearrange("h p (t q) -> p h t q", q=128)[:, head], dWG)
        wdt_here = dWG.n
        W("tensor", dWG, wdt_here)
        for dl in range(DTH):
            dtile = head * DTH + dl
            W("tensor", sS, dtin_mark)
            pswait_t("C")
            for (cc0, cn) in nchC:
                E("tensor", lambda api, cc0=cc0, cn=cn, head=head, dl=dl: api.matmul(
                    psC[:, cc0:cc0 + cn], wstream[:, dl * 128:(dl + 1) * 128],
                    dtin_bf[:, head * TC + cc0:head * TC + cc0 + cn], start=True, stop=True), inc=sT)
            dtp_mark = sT.n
            W("scalar", sT, dtp_mark)
            dtb0 = SP_DT[dtile % 2]
            if dtile >= 2:
                W("scalar", sV, ph1_v_mark[dtile - 2])
                W("vector", dU, ph1_spill_mark.get(dtile - 2, 0))
            E("scalar", lambda api, dtile=dtile: api.activation(
                out=scrF[:, 0:TC], in_=psC[:, 0:TC], func=Act.Identity,
                bias=dtb_sb[:, dtile:dtile + 1]))
            E("scalar", lambda api: api.activation(out=scrF[:, TW:TW + TC], in_=scrF[:, 0:TC], func=Act.Abs))
            E("scalar", lambda api: api.activation(out=scrF[:, TW:TW + TC], in_=scrF[:, TW:TW + TC],
                                                   func=Act.Exp, scale=-1.0))
            E("scalar", lambda api: api.activation(out=scrF[:, TW:TW + TC], in_=scrF[:, TW:TW + TC],
                                                   func=Act.Ln, bias=1.0), inc=sS)
            W("vector", sS)
            E("vector", lambda api: api.tensor_scalar(out=evst, in0=scrF[:, 0:TC],
                                                      scalar1=0.0, scalar2=None, op0=Alu.max))
            E("vector", lambda api, dtb0=dtb0: api.tensor_tensor(
                out=fview(dtb0, TC), in0=scrF[:, TW:TW + TC], in1=evst, op=Alu.add), inc=sV)
            psmark["C"] = (sV, sV.n)
            dt_ready = sV.n
            W("sync", sV, dt_ready)
            dma("sync", d_scr_dt[dtile], fview(dtb0, TC), dU)
            ph1_spill_mark[dtile] = dU.n
            ub = SP_U[dtile % 2]
            if dtile >= 2:
                W("sync", sV, ph1_v_mark[dtile - 2])
            dma("sync", fview(ub, TC), d_scr_u[dtile], dR)
            u_here = dR.n
            E("vector", lambda api, dtb0=dtb0: api.tensor_tensor_scan(
                out=cdtb, data0=fview(dtb0, TC), data1=zcol.broadcast_to([P, TC]),
                initial=0.0, op0=Alu.add, op1=Alu.add), inc=sV)
            W("vector", sV)
            E("vector", lambda api: api.tensor_scalar(out=nrow[:, 0:1], in0=cdtb[:, TC - 1:TC],
                                                      scalar1=-1.0, scalar2=None, op0=Alu.mult), inc=sV)
            W("vector", sV)
            E("vector", lambda api: api.tensor_scalar(out=argb, in0=cdtb,
                                                      scalar1=nrow[:, 0:1], scalar2=None, op0=Alu.add))
            W("vector", dR, u_here)
            E("vector", lambda api, dtb0=dtb0, ub=ub: api.tensor_tensor(
                out=gb, in0=fview(dtb0, TC), in1=fview(ub, TC), op=Alu.mult))
            E("vector", lambda api, dtile=dtile: api.tensor_scalar(
                out=dec16, in0=negA[:, dtile * NS:(dtile + 1) * NS],
                scalar1=cdtb[:, TC - 1:TC], scalar2=None, op0=Alu.mult), inc=sV)
            gv = sV.n
            W("scalar", sV, gv)
            E("scalar", lambda api, dtile=dtile: api.activation(
                out=Decb[:, dtile * NS:(dtile + 1) * NS], in_=dec16, func=Act.Exp), inc=sS)
            ph1_g = {}
            ph1_vn = {}
            for n in range(NS):
                fb0 = (n % 2) * TC
                if n >= 2:
                    W("scalar", sG, ph1_g[n - 2])
                E("scalar", lambda api, dtile=dtile, n=n, fb0=fb0: api.activation(
                    out=Fbh[:, fb0:fb0 + TC], in_=argb, func=Act.Exp,
                    scale=posA[:, dtile * NS + n:dtile * NS + n + 1]), inc=sS)
                W("gpsimd", sS)
                if n >= 2:
                    W("gpsimd", sV, ph1_vn[n - 2])
                E("gpsimd", lambda api, fb0=fb0: api.tensor_tensor(
                    out=Wbh[:, fb0:fb0 + TC], in0=Fbh[:, fb0:fb0 + TC], in1=gb, op=Alu.mult), inc=sG)
                ph1_g[n] = sG.n
                W("vector", sG, ph1_g[n])
                if dtile % DTH == 0 and n == 0:
                    W("vector", dR, brep_done)
                E("vector", lambda api, fb0=fb0, n=n: api.tensor_tensor(
                    out=csbh[:, fb0:fb0 + TC], in0=Wbh[:, fb0:fb0 + TC],
                    in1=brep[:, n * TC:(n + 1) * TC], op=Alu.mult))
                E("vector", lambda api, fb0=fb0, n=n, dtile=dtile: api.tensor_scalar(
                    out=dumpb.bitcast(BF16)[:, 0:TC],
                    in0=csbh[:, fb0:fb0 + TC], scalar1=1.0, scalar2=0.0, op0=Alu.mult, op1=Alu.add,
                    accum_out=Scol[:, dtile * NS + n:dtile * NS + n + 1]), inc=sV)
                ph1_vn[n] = sV.n
            ph1_v_mark[dtile] = sV.n
        wsmark[0] = sT.n

    W("sync", sV)
    W("sync", sS)
    dma("sync", d_cc2in[0], Scol, dU)
    dma("sync", d_cc2in[1], Decb, dU)
    if c.debug:
        dma("sync", d_dbgcc[0], Scol, dU)
        dma("sync", d_dbgcc[1], Decb, dU)
    W("gpsimd", dU)
    E("gpsimd", lambda api: api.collective_compute(
        "AllGather", Alu.bypass, replica_groups=[list(range(c.NC))],
        ins=[d_cc2in_h.ap().rearrange("a p q -> (a p q)")],
        outs=[d_cc2out_h.ap().rearrange("n a p q -> (n a p q)")]), inc=sCC)
    W("sync", sCC)
    E("vector", lambda api: api.memset(sInb, 0.0), inc=sV)
    DN = DT * NS
    for j in range(c.NC - 1):
        W("sync", sV)
        dma("sync", agb[:, 0:DN], d_cc2out[j, 0], dR)
        dma("sync", agb[:, DN:2 * DN], d_cc2out[j, 1], dR)
        W("vector", dR)
        E("vector", lambda api: api.tensor_scalar(
            out=agb[:, DN:2 * DN], in0=agb[:, DN:2 * DN], scalar1=1.0, scalar2=None, op0=Alu.subtract))
        E("vector", lambda api, j=j: api.tensor_scalar(
            out=agb[:, DN:2 * DN], in0=agb[:, DN:2 * DN],
            scalar1=pmask_sb[:, j:j + 1], scalar2=1.0, op0=Alu.mult, op1=Alu.add))
        E("vector", lambda api: api.tensor_tensor(out=sInb, in0=sInb, in1=agb[:, DN:2 * DN], op=Alu.mult))
        E("vector", lambda api, j=j: api.tensor_scalar(
            out=agb[:, 0:DN], in0=agb[:, 0:DN], scalar1=pmask_sb[:, j:j + 1], scalar2=None, op0=Alu.mult))
        E("vector", lambda api: api.tensor_tensor(out=sInb, in0=sInb, in1=agb[:, 0:DN], op=Alu.add), inc=sV)

    if c.debug:
        W("sync", sV)
        dma("sync", d_dbgcc[2], sInb, dU)
    ph2_v_mark = {}
    ph2_ygfree = [0]
    for head in range(NH):
        W("sync", sV)
        W("sync", sT)
        for n in range(NS):
            dma("sync", brep[:, n * TC:(n + 1) * TC],
                d_scr_bc[head * 2 * NS + n:head * 2 * NS + n + 1, :].broadcast_to([P, TC]), dR)
            dma("sync", crep[:, n * TC:(n + 1) * TC],
                d_scr_bc[head * 2 * NS + NS + n:head * 2 * NS + NS + n + 1, :].broadcast_to([P, TC]), dR)
        rep_done = dR.n
        for dl in range(DTH):
            dtile = head * DTH + dl
            kg = dtile % KGRP
            b2 = dtile % 2
            dtb0 = SP_DT[b2]
            ub = SP_U[b2]
            sgb = SP_SG[b2]
            if dtile >= 2:
                W("sync", sV, ph2_v_mark[dtile - 2])
            dma("sync", fview(dtb0, TC), d_scr_dt[dtile], dR)
            dma("sync", fview(ub, TC), d_scr_u[dtile], dR)
            dma("sync", fview(sgb, TC), d_scr_sg[dtile], dR)
            W("vector", dR)
            if dtile == 0:
                W("vector", dR, rep_done)
            E("vector", lambda api, dtb0=dtb0, ub=ub: api.tensor_tensor(
                out=gbh2[:, 0:TC], in0=fview(dtb0, TC), in1=fview(ub, TC), op=Alu.mult), inc=sV)
            W("vector", sV)
            E("vector", lambda api: api.tensor_copy(out=gbh2[:, TC:2 * TC], in_=gbh2[:, 0:TC]))
            E("vector", lambda api: api.memset(yb, 0.0), inc=sV)
            gmark = sV.n
            W("scalar", sV, gmark)
            ph2_vn = {}
            ph2_gn = {}
            for np_ in range(NS // 2):
                n0 = 2 * np_
                if np_ >= 1:
                    W("scalar", sV, ph2_vn[np_ - 1])
                for dn in range(2):
                    E("scalar", lambda api, dtile=dtile, n0=n0, dn=dn, dtb0=dtb0: api.activation(
                        out=dAbh[:, dn * TC:(dn + 1) * TC], in_=fview(dtb0, TC), func=Act.Exp,
                        scale=negA[:, dtile * NS + n0 + dn:dtile * NS + n0 + dn + 1]), inc=sS)
                sx = sS.n
                W("vector", sS, sx)
                E("vector", lambda api, n0=n0: api.tensor_tensor(
                    out=gBbh[:, 0:2 * TC], in0=gbh2[:, 0:2 * TC],
                    in1=brep[:, n0 * TC:(n0 + 2) * TC], op=Alu.mult))
                for dn in range(2):
                    E("vector", lambda api, dn=dn, n0=n0, dtile=dtile: api.tensor_tensor_scan(
                        out=csbh[:, dn * TC:(dn + 1) * TC], data0=dAbh[:, dn * TC:(dn + 1) * TC],
                        data1=gBbh[:, dn * TC:(dn + 1) * TC],
                        initial=sInb[:, dtile * NS + n0 + dn:dtile * NS + n0 + dn + 1],
                        op0=Alu.mult, op1=Alu.add))
                if np_ >= 1:
                    W("vector", sG, ph2_gn[np_ - 1])
                E("vector", lambda api, n0=n0: api.tensor_tensor(
                    out=csbh[:, 0:2 * TC], in0=csbh[:, 0:2 * TC],
                    in1=crep[:, n0 * TC:(n0 + 2) * TC], op=Alu.mult), inc=sV)
                ph2_vn[np_] = sV.n
                W("gpsimd", sV, ph2_vn[np_])
                E("gpsimd", lambda api: api.tensor_tensor(
                    out=yb, in0=yb, in1=csbh[:, 0:TC], op=Alu.add))
                E("gpsimd", lambda api: api.tensor_tensor(
                    out=yb, in0=yb, in1=csbh[:, TC:2 * TC], op=Alu.add), inc=sG)
                ph2_gn[np_] = sG.n
            W("vector", sG, ph2_gn[NS // 2 - 1])
            E("vector", lambda api, ub=ub, dtile=dtile: api.scalar_tensor_tensor(
                out=yb, in0=fview(ub, TC), scalar=Dp_sb[:, dtile:dtile + 1],
                in1=yb, op0=Alu.mult, op1=Alu.add))
            W("vector", sT, ph2_ygfree[0])
            E("vector", lambda api, sgb=sgb, kg=kg: api.tensor_tensor(
                out=ygbuf[:, kg * TC:(kg + 1) * TC], in0=yb, in1=fview(sgb, TC),
                op=Alu.mult), inc=sV)
            ph2_v_mark[dtile] = sV.n
            if kg == KGRP - 1:
                g = dtile // KGRP
                yg_done = sV.n
                ev_mark = [0]
                for m in range(HT):
                    buf = (m % 2) * 4096
                    nkr = max(1, KGRP // OPC)
                    for half in range(nkr):
                        r = g * nkr + half
                        wb_ = r * PACK2 + base2_wout + (m * OPC) * TB
                        wslab(buf, half * OPC * 128, OPC, wb_, wait_mark=(half == 0), wagt=wag2)
                    W("tensor", dW)
                    W("tensor", sV, yg_done)
                    ppname = "A" if (m % 2 == 0) else "B"
                    pp = psA if (m % 2 == 0) else psB
                    pswait_t(ppname)
                    for k in range(KGRP):
                        for (cc0, cn) in nchC:
                            E("tensor", lambda api, k=k, cc0=cc0, cn=cn, buf=buf, pp=pp: api.matmul(
                                pp[:, cc0:cc0 + cn], wstream[:, buf + k * 128:buf + (k + 1) * 128],
                                ygbuf[:, k * TC + cc0:k * TC + cc0 + cn],
                                start=(k == 0), stop=(k == KGRP - 1)), inc=sT)
                    wsmark[buf] = sT.n
                    W("scalar", sT)
                    W("scalar", dU, ev_mark[0])
                    E("scalar", lambda api, pp=pp: api.copy(out=evst, in_=pp[:, 0:TC]), inc=sS)
                    psmark[ppname] = (sS, sS.n)
                    W("sync", sS)
                    dma("sync", d_macc[g, m * 128:(m + 1) * 128, :], evst, dU)
                    ev_mark[0] = dU.n
                ph2_ygfree[0] = sT.n

    W("sync", sV)
    W("sync", sS)
    for t in range(HT):
        dma("sync", xres[:, t * TW:(t + 1) * TW],
            d_x1.rearrange("(t p) w -> p t w", p=P)[:, t], dR)
    x1_back = dR.n
    mslot = [fview(O_R3, TC), fview(O_R3 + TC, TC)]
    for t in range(HT):
        for g in range(NGR):
            W("sync", sV)
            dma("sync", mslot[g % 2], d_macc[g, t * 128:(t + 1) * 128, :], dR)
            W("vector", dR)
            if t == 0 and g == 0:
                W("vector", dR, x1_back)
            xa = xres[:, t * TW + HALO:(t + 1) * TW]
            E("vector", lambda api, xa=xa, g=g: api.tensor_tensor(
                out=xa, in0=xa, in1=mslot[g % 2], op=Alu.add), inc=sV)

    ffn(HALO, TC)

    W("sync", sV)
    dma("sync", d_out.rearrange("(t p) w -> p t w", p=P),
        xres.rearrange("p (t w) -> p t w", w=TW)[:, :, HALO:TW], dO)
    W("sync", dO)

    with nc.Block() as blk:
        @blk.sync
        def _(api):
            for thk in kb.streams["sync"]:
                thk(api)

        @blk.tensor
        def _(api):
            for thk in kb.streams["tensor"]:
                thk(api)

        @blk.vector
        def _(api):
            for thk in kb.streams["vector"]:
                thk(api)

        @blk.scalar
        def _(api):
            for thk in kb.streams["scalar"]:
                thk(api)

        @blk.gpsimd
        def _(api):
            for thk in kb.streams["gpsimd"]:
                thk(api)
    st.close()
    return nc


def prepare_inputs(inputs, cfg):
    c = cfg

    def get(k):
        return np.asarray(inputs[k], np.float32)
    x = get("L0_attn_hidden_states")[0]
    x_fm = np.ascontiguousarray(x.T)
    w_in = get("L0_attn_in_proj_w")
    w_in_perm = np.concatenate([w_in[0::2], w_in[1::2]], 0)
    W_inT = np.ascontiguousarray(w_in_perm.T)
    WoutT = np.ascontiguousarray(get("L0_attn_out_proj_w").T)
    WxT = np.ascontiguousarray(get("L0_attn_x_proj_weight").transpose(0, 2, 1))
    WdtT = np.ascontiguousarray(get("L0_attn_dt_proj_weight").transpose(0, 2, 1))
    cw = np.ascontiguousarray(get("L0_attn_conv_w").reshape(c.DT, 128, 4))
    cb = np.ascontiguousarray(get("L0_attn_conv_b").reshape(c.DT, 128).T)
    dtb = np.ascontiguousarray(get("L0_attn_dt_proj_bias").reshape(c.DT, 128).T)
    Dp = np.ascontiguousarray(get("L0_attn_D").reshape(c.DT, 128).T)
    alog = np.ascontiguousarray(get("L0_attn_A_log").reshape(c.DT, 128, c.NS))
    wgate = get("L0_ffn_w_gate")
    wup = get("L0_ffn_w_up")
    wdown = get("L0_ffn_w_down")
    import ml_dtypes
    iden = np.eye(128, dtype=ml_dtypes.bfloat16)
    in_maps = []
    rg = c.IFF // c.NC
    rh = c.H // c.NC
    ro = c.IP // c.NC
    for cid in range(c.NC):
        t0 = cid * c.TC
        x_sh = np.ascontiguousarray(x_fm[:, t0:t0 + c.TC])
        pm = np.zeros((128, c.NC), np.float32)
        pm[:, :cid] = 1.0
        cm = np.zeros((128, c.NC), np.float32)
        if cid > 0:
            cm[:, cid - 1] = 1.0
        in_maps.append(dict(
            x_in=x_sh,
            wg_sl=np.ascontiguousarray(wgate[cid * rg:(cid + 1) * rg]),
            wu_sl=np.ascontiguousarray(wup[cid * rg:(cid + 1) * rg]),
            wd_sl=np.ascontiguousarray(wdown[cid * rh:(cid + 1) * rh]),
            wi_sl=np.ascontiguousarray(W_inT[cid * rh:(cid + 1) * rh]),
            wo_sl=np.ascontiguousarray(WoutT[cid * ro:(cid + 1) * ro]),
            wx=WxT, wdt=WdtT, convw=cw, convb=cb, dtb=dtb, Dp=Dp, alog=alog,
            iden=iden, idenf=np.eye(128, dtype=np.float32), pmask=pm, cmask=cm,
        ))
    return in_maps


_CFG = CFG(mini=False)


def kernel(**inputs):
    cfg = _CFG
    in_maps = prepare_inputs(inputs, cfg)
    nc = build_program(cfg)
    res = run_bass_kernel_spmd(nc, in_maps, core_ids=list(range(cfg.NC)))
    outs = [res.results[i]["out_fm"] for i in range(cfg.NC)]
    out_fm = np.concatenate(outs, axis=1)
    return np.ascontiguousarray(out_fm.T)[None]
